# version 01
# baseline (speedup 1.0000x reference)
import numpy as np
from contextlib import ExitStack

import concourse.bass as bass
import concourse.mybir as mybir
from concourse.bass_utils import run_bass_kernel_spmd

F32 = mybir.dt.float32
BF16 = mybir.dt.bfloat16
I32 = mybir.dt.int32
AF = mybir.ActivationFunctionType
ALU = mybir.AluOpType
AX = mybir.AxisListType

NCORES = 8
D = 1024
L = 8192
BATCH = 4
G = D // 128
TL = L // 2
DI = 2048
DIH = DI // 2
GI = DIH // 128
NST = 16
TCH = 512
NCH_F = L // TCH
NCH_H = TL // TCH
LN_EPS = 1e-5
RMS_EPS = 1e-6


def _a(x):
    return x if isinstance(x, bass.AP) else x.ap()


def dram_fm(x_hbm, col0, ncols, ngroups, g0=0):
    ap = _a(x_hbm)
    return ap[g0 * 128:(g0 + ngroups) * 128, col0:col0 + ncols].rearrange(
        "(g p) t -> p g t", p=128)


def dram_vec(v_hbm, ngroups, g0=0):
    ap = _a(v_hbm)
    return ap[g0 * 128:(g0 + ngroups) * 128].rearrange("(g p) -> p g", p=128)


def sb_fm(tile, ncols, ngroups, col0=0):
    return _a(tile)[:, col0:col0 + ngroups * ncols].rearrange(
        "p (g t) -> p g t", g=ngroups)


class SemPair:
    def __init__(self, P, name):
        self.s = [P.sem(name + "a"), P.sem(name + "b")]

    def inc(self, inst, j):
        self.s[j % 2].inc(inst, 16)

    def wait(self, eng, j):
        eng.wait_ge(self.s[j % 2].h, (j // 2 + 1) * 16)


class Sem:
    def __init__(self, handle):
        self.h = handle
        self.n = 0

    def inc(self, inst, k):
        inst.then_inc(self.h, k)
        self.n += k
        return self.n


class Prog:
    def __init__(self, name="mk"):
        self.nc = bass.Bass("TRN2", target_bir_lowering=False, detect_race_conditions=False)
        self.gctx = ExitStack()
        self.pctx = None
        self.blk = None
        self._n = 0
        self._consts = {}

    def start_block(self):
        self.blk = self.gctx.enter_context(self.nc.Block())
        return self.blk

    def sem(self, name):
        self._n += 1
        return Sem(self.gctx.enter_context(self.nc.semaphore(f"{name}{self._n}")))

    def gsb(self, shape, dtype=F32, name="g"):
        self._n += 1
        return self.gctx.enter_context(self.nc.sbuf_tensor(f"{name}{self._n}", shape, dtype))

    def sb(self, shape, dtype=F32, name="t"):
        self._n += 1
        return self.pctx.enter_context(self.nc.sbuf_tensor(f"{name}{self._n}", shape, dtype))

    def ps(self, shape, name="p"):
        self._n += 1
        cols = max(shape[1], 512)
        t = self.pctx.enter_context(self.nc.psum_tensor(f"{name}{self._n}",
                                                        [shape[0], cols], F32))
        return t.ap()[:, 0:shape[1]] if cols != shape[1] else t

    def psb(self, shape, name="pb"):
        self._n += 1
        cols = max(shape[1], 1024)
        t = self.pctx.enter_context(self.nc.psum_tensor(f"{name}{self._n}",
                                                        [shape[0], cols], BF16))
        return t.ap()[:, 0:shape[1]] if cols != shape[1] else t

    def const(self, val):
        if val in (0.0, 1.0):
            return val
        if val not in self._consts:
            tile = self.gsb([128, 1], F32, "c")
            self._consts[val] = tile
            self.nc.const_aps.register(F32, float(val), tile[:, 0:1])
        return float(val)

    def ensure_consts(self, eng, sem):
        if not hasattr(self, "_consts_done"):
            self._consts_done = set()
        for val, tile in self._consts.items():
            if val not in self._consts_done:
                sem.inc(eng.memset(tile[:, 0:1], float(val)), 1)
                self._consts_done.add(val)

    def barrier(self, name="bar"):
        s = self.sem(name)
        engines = ["sync", "vector", "scalar", "gpsimd", "tensor"]
        for e in engines:
            getattr(self.blk, e)(lambda eng, _s=s: eng.sem_inc(_s.h, 1))
        s.n = len(engines)
        for e in engines:
            getattr(self.blk, e)(lambda eng, _s=s: eng.wait_ge(_s.h, _s.n))
        return s



def emit_ln(P, x_hbm, xn_bf_hbm, rows_hbm, xn_f32_hbm=None, ncols=L, in_bf16=False,
            gamma_hbm=None, beta_hbm=None, rms=False, eps=LN_EPS, name="ln"):
    nc, blk = P.nc, P.blk
    nch = ncols // TCH
    s_ld = SemPair(P, name)
    s_w = P.sem(name)
    s_ini = P.sem(name)
    s_sq = P.sem(name)
    s_mm = P.sem(name)
    s_row = P.sem(name)
    s_rw = P.sem(name)
    s_bc = P.sem(name)
    s_ap = P.sem(name)
    s_st = P.sem(name)
    nst_per = 32 if xn_f32_hbm is not None else 16

    xt = [P.sb([128, G * TCH], BF16 if in_bf16 else F32, name + "x") for _ in range(2)]
    xsq = P.sb([128, G * TCH], F32, name + "sq")
    ones_col = P.sb([128, 1], F32, name + "oc")
    rowbuf = P.sb([1, 4 * TCH], F32, name + "rows")
    mean = rowbuf[:, 0:TCH]
    rstd = rowbuf[:, TCH:2 * TCH]
    msq = rowbuf[:, 2 * TCH:3 * TCH]
    var = rowbuf[:, 3 * TCH:4 * TCH]
    bc_t = P.sb([128, 2 * TCH], F32, name + "bc")
    t0 = P.sb([128, TCH], F32, name + "t0")
    xnb = P.sb([128, G * TCH], BF16, name + "nb")
    xnf = P.sb([128, G * TCH], F32, name + "nf") if xn_f32_hbm is not None else None
    gam = P.sb([128, G], F32, name + "gam") if gamma_hbm is not None else None
    bet = P.sb([128, G], F32, name + "bet") if beta_hbm is not None else None
    ps_sum = P.ps([1, TCH], name + "psum")
    ps_ssq = P.ps([1, TCH], name + "pssq")

    @blk.vector
    def _(v):
        s_ini.inc(v.memset(ones_col[:], 1.0), 1)

    @blk.gpsimd
    def _(g):
        if gam is not None:
            s_w.inc(g.dma_start(out=gam[:], in_=_a(gamma_hbm)), 16)
        if bet is not None:
            s_w.inc(g.dma_start(out=bet[:], in_=_a(beta_hbm)), 16)

    def emit_store(sync, j):
        sync.wait_ge(s_ap.h, j + 1)
        s_st.inc(sync.dma_start(out=dram_fm(xn_bf_hbm, j * TCH, TCH, G),
                                in_=sb_fm(xnb, TCH, G)), 16)
        if xn_f32_hbm is not None:
            s_st.inc(sync.dma_start(out=dram_fm(xn_f32_hbm, j * TCH, TCH, G),
                                    in_=sb_fm(xnf, TCH, G)), 16)

    @blk.sync
    def _(sync):
        for j in range(nch):
            if j >= 2:
                sync.wait_ge(s_ap.h, j - 1)
            s_ld.inc(sync.dma_start(out=xt[j % 2][:],
                                    in_=dram_fm(x_hbm, j * TCH, TCH, G)), j)
            if j >= 1:
                emit_store(sync, j - 1)
        emit_store(sync, nch - 1)
        sync.wait_ge(s_st.h, s_st.n)

    @blk.scalar
    def _(s):
        for j in range(nch):
            s_ld.wait(s, j)
            if j >= 1:
                s.wait_ge(s_mm.h, j)
            s_sq.inc(s.activation(xsq[:], xt[j % 2][:], AF.Square), 1)
            s.wait_ge(s_row.h, 2 * j + 1)
            s_row.inc(s.activation(rstd, var, AF.Sqrt), 1)

    @blk.tensor
    def _(t):
        t.wait_ge(s_ini.h, 1)
        for j in range(nch):
            t.wait_ge(s_sq.h, j + 1)
            if j >= 1:
                t.wait_ge(s_row.h, 2 * j)
            if not rms:
                for g_ in range(G):
                    t.matmul(ps_sum[:], ones_col[:], xt[j % 2][:, g_ * TCH:(g_ + 1) * TCH],
                             start=(g_ == 0), stop=(g_ == G - 1))
            for g_ in range(G):
                i = t.matmul(ps_ssq[:], ones_col[:], xsq[:, g_ * TCH:(g_ + 1) * TCH],
                             start=(g_ == 0), stop=(g_ == G - 1))
            s_mm.inc(i, 1)

    @blk.vector
    def _(v):
        c_inv = 1.0 / D
        for j in range(nch):
            v.wait_ge(s_mm.h, j + 1)
            if j >= 1:
                v.wait_ge(s_rw.h, 32 * j)
            if not rms:
                v.tensor_scalar_mul(mean, ps_sum[:], c_inv)
                v.tensor_scalar(var, ps_ssq[:], c_inv, eps, ALU.mult, ALU.add)
                v.tensor_mul(msq, mean, mean)
                v.tensor_sub(var, var, msq)
            else:
                v.tensor_scalar(var, ps_ssq[:], c_inv, eps, ALU.mult, ALU.add)
                v.memset(msq, 0.0)
            i = v.reciprocal(var, var)
            s_row.inc(i, 1)
            v.wait_ge(s_bc.h, (j + 1) * 16)
            if gam is not None:
                v.wait_ge(s_w.h, s_w.n)
            if j >= 1:
                v.wait_ge(s_st.h, j * nst_per)
            m_bc = bc_t[:, 0:TCH]
            r_bc = bc_t[:, TCH:2 * TCH]
            for g_ in range(G):
                sl = slice(g_ * TCH, (g_ + 1) * TCH)
                if not rms:
                    v.tensor_sub(t0[:], xt[j % 2][:, sl], m_bc)
                    src = t0[:]
                else:
                    src = xt[j % 2][:, sl]
                if gam is None:
                    i = v.tensor_mul(xnb[:, sl], src, r_bc)
                    if xnf is not None:
                        i = v.tensor_mul(xnf[:, sl], src, r_bc)
                else:
                    v.tensor_mul(t0[:], src, r_bc)
                    gcol = gam[:, g_:g_ + 1]
                    if bet is None:
                        i = v.tensor_scalar_mul(xnb[:, sl], t0[:], gcol)
                        if xnf is not None:
                            i = v.tensor_scalar_mul(xnf[:, sl], t0[:], gcol)
                    else:
                        i = v.tensor_scalar(xnb[:, sl], t0[:], gcol, bet[:, g_:g_ + 1],
                                            ALU.mult, ALU.add)
                        if xnf is not None:
                            i = v.tensor_scalar(xnf[:, sl], t0[:], gcol, bet[:, g_:g_ + 1],
                                                ALU.mult, ALU.add)
            s_ap.inc(i, 1)

    @blk.gpsimd
    def _(g):
        rows_ap = _a(rows_hbm)
        for j in range(nch):
            g.wait_ge(s_row.h, 2 * (j + 1))
            s_rw.inc(g.dma_start(out=rows_ap[0:1, j * TCH:(j + 1) * TCH], in_=mean), 16)
            s_rw.inc(g.dma_start(out=rows_ap[1:2, j * TCH:(j + 1) * TCH], in_=rstd), 16)
            g.wait_ge(s_rw.h, 32 * (j + 1))
            if j >= 1:
                g.wait_ge(s_ap.h, j)
            rep_src = bass.AP(rows_hbm.tensor if isinstance(rows_hbm, bass.AP)
                              else rows_hbm, j * TCH,
                              [[0, 128], [ncols, 2], [1, TCH]])
            s_bc.inc(g.dma_start(
                out=_a(bc_t).rearrange("p (r t) -> p r t", r=2), in_=rep_src), 16)

    return s_st


def load_w_bf16(P, s_w, w_hbm, rows, cols, name):
    kg = rows // 128
    wsb = P.sb([128, kg * cols], BF16, name)

    @P.blk.gpsimd
    def _(g):
        s_w.inc(g.dma_start(out=sb_fm(wsb, cols, kg), in_=dram_fm(w_hbm, 0, cols, kg)), 16)
    return wsb


def emit_mamba_front(P, xn_hbm, w_in_x_hbm, w_in_z_hbm, conv_w_hbm, conv_b_hbm,
                     w_x_hbm, u_hbm, z_hbm, sp_hbm, Rsp, name="mf"):
    nc, blk = P.nc, P.blk
    s_w = P.sem(name)
    s_ld = SemPair(P, name)
    s_mmx = P.sem(name)
    s_evx = P.sem(name)
    s_silu = P.sem(name)
    s_ub = P.sem(name)
    s_mmz = P.sem(name)
    s_evz = P.sem(name)
    s_mmsp = P.sem(name)
    s_evsp = P.sem(name)
    s_odu = P.sem(name)
    s_odz = P.sem(name)
    s_odsp = P.sem(name)

    wx_sb = load_w_bf16(P, s_w, w_in_x_hbm, 1024, 1024, name + "wx")
    wz_sb = load_w_bf16(P, s_w, w_in_z_hbm, 1024, 1024, name + "wz")
    wsp_sb = load_w_bf16(P, s_w, w_x_hbm, 1024, Rsp, name + "wsp")
    cw_sb = P.sb([128, 4 * GI], F32, name + "cw")
    cb_sb = P.sb([128, GI], F32, name + "cb")

    @blk.gpsimd
    def _(g):
        s_w.inc(g.dma_start(out=cw_sb[:], in_=_a(conv_w_hbm)), 16)
        s_w.inc(g.dma_start(out=cb_sb[:], in_=_a(conv_b_hbm)), 16)

    xt = [P.sb([128, G * TCH], BF16, name + "xt") for _ in range(2)]
    ut = P.sb([128, GI * TCH], F32, name + "ut")
    ub = P.sb([128, GI * TCH], BF16, name + "ub")
    zt = P.sb([128, GI * TCH], BF16, name + "zt")
    xp_ext = P.sb([128, TCH + 3], F32, name + "xpe")
    hal = P.sb([128, 3 * GI], F32, name + "hal")
    spt = P.sb([Rsp, TCH], F32, name + "spt")
    ps_x = [P.ps([128, TCH], name + "psx") for _ in range(2)]
    ps_z = [P.ps([128, TCH], name + "psz") for _ in range(2)]
    ps_sp = P.ps([Rsp, TCH], name + "pssp")

    @blk.vector
    def _(v):
        v.memset(hal[:], 0.0)

    def emit_store(sync, j):
        sync.wait_ge(s_silu.h, (j + 1) * GI)
        sync.wait_ge(s_ub.h, j + 1)
        s_odu.inc(sync.dma_start(out=dram_fm(u_hbm, j * TCH, TCH, GI),
                                 in_=sb_fm(ut, TCH, GI)), 16)
        sync.wait_ge(s_evz.h, (j + 1) * GI)
        s_odz.inc(sync.dma_start(out=dram_fm(z_hbm, j * TCH, TCH, GI),
                                 in_=sb_fm(zt, TCH, GI)), 16)
        sync.wait_ge(s_evsp.h, j + 1)
        s_odsp.inc(sync.dma_start(out=_a(sp_hbm)[:, j * TCH:(j + 1) * TCH],
                                  in_=spt[:]), 16)

    @blk.sync
    def _(sync):
        sync.wait_ge(s_w.h, s_w.n)
        for j in range(NCH_F):
            if j >= 2:
                sync.wait_ge(s_mmz.h, (j - 1) * GI)
            s_ld.inc(sync.dma_start(out=xt[j % 2][:],
                                    in_=dram_fm(xn_hbm, j * TCH, TCH, G)), j)
            if j >= 1:
                emit_store(sync, j - 1)
        emit_store(sync, NCH_F - 1)
        sync.wait_ge(s_odu.h, s_odu.n)
        sync.wait_ge(s_odz.h, s_odz.n)
        sync.wait_ge(s_odsp.h, s_odsp.n)

    @blk.tensor
    def _(t):
        t.wait_ge(s_w.h, s_w.n)
        for j in range(NCH_F):
            s_ld.wait(t, j)
            for g in range(GI):
                gi = j * GI + g
                if gi >= 1:
                    t.wait_ge(s_evx.h, gi - 1)
                for kg in range(G):
                    i = t.matmul(ps_x[gi % 2][:],
                                 wx_sb[:, (kg * GI + g) * 128:(kg * GI + g + 1) * 128],
                                 xt[j % 2][:, kg * TCH:(kg + 1) * TCH],
                                 start=(kg == 0), stop=(kg == G - 1))
                s_mmx.inc(i, 1)
            for g in range(GI):
                gi = j * GI + g
                if gi >= 1:
                    t.wait_ge(s_evz.h, gi - 1)
                for kg in range(G):
                    i = t.matmul(ps_z[gi % 2][:],
                                 wz_sb[:, (kg * GI + g) * 128:(kg * GI + g + 1) * 128],
                                 xt[j % 2][:, kg * TCH:(kg + 1) * TCH],
                                 start=(kg == 0), stop=(kg == G - 1))
                s_mmz.inc(i, 1)
            t.wait_ge(s_ub.h, j + 1)
            if j >= 1:
                t.wait_ge(s_evsp.h, j)
            for kg in range(GI):
                i = t.matmul(ps_sp[:], wsp_sb[:, kg * Rsp:(kg + 1) * Rsp],
                             ub[:, kg * TCH:(kg + 1) * TCH],
                             start=(kg == 0), stop=(kg == GI - 1))
            s_mmsp.inc(i, 1)

    @blk.vector
    def _(v):
        for j in range(NCH_F):
            if j >= 1:
                v.wait_ge(s_odu.h, j * 16)
            for g in range(GI):
                gi = j * GI + g
                v.wait_ge(s_mmx.h, gi + 1)
                hs = slice(3 * g, 3 * g + 3)
                v.tensor_copy(xp_ext[:, 0:3], hal[:, hs])
                v.tensor_copy(xp_ext[:, 3:3 + TCH], ps_x[gi % 2][:])
                v.tensor_copy(hal[:, hs], ps_x[gi % 2][:, TCH - 3:TCH])
                sl = slice(g * TCH, (g + 1) * TCH)
                v.tensor_scalar_mul(ut[:, sl], xp_ext[:, 0:TCH], cw_sb[:, 4 * g:4 * g + 1])
                for k in range(1, 4):
                    i = v.scalar_tensor_tensor(ut[:, sl], xp_ext[:, k:k + TCH],
                                               cw_sb[:, 4 * g + k:4 * g + k + 1],
                                               ut[:, sl], ALU.mult, ALU.add)
                s_evx.inc(i, 1)
            v.wait_ge(s_silu.h, (j + 1) * GI)
            i = v.tensor_copy(ub[:], ut[:])
            s_ub.inc(i, 1)
            v.wait_ge(s_mmsp.h, j + 1)
            if j >= 1:
                v.wait_ge(s_odsp.h, j * 16)
            i = v.tensor_copy(spt[:], ps_sp[:])
            s_evsp.inc(i, 1)

    @blk.scalar
    def _(s):
        s.wait_ge(s_w.h, s_w.n)
        for j in range(NCH_F):
            for g in range(GI):
                gi = j * GI + g
                sl = slice(g * TCH, (g + 1) * TCH)
                s.wait_ge(s_evx.h, gi + 1)
                i = s.activation(ut[:, sl], ut[:, sl], AF.Silu, bias=cb_sb[:, g:g + 1])
                s_silu.inc(i, 1)
                s.wait_ge(s_mmz.h, gi + 1)
                if j >= 1 and g == 0:
                    s.wait_ge(s_odz.h, j * 16)
                i = s.copy(zt[:, sl], ps_z[gi % 2][:])
                s_evz.inc(i, 1)

    return s_odu


TS = 1024
NCHS = L // TS


def emit_scan(P, u_hbm, z_hbm, sp_hbm, g_hbm, wdt_hbm, alog_hbm, bdt_hbm,
              dvec_hbm, dtw_hbm, layer, name="sc"):
    nc, blk = P.nc, P.blk
    P.const(-1.0)
    nchs = L // TS
    NSUB = TS // 512
    Boff, Coff = (64, 80) if layer == 0 else (0, 16)

    s_w = P.sem(name)
    s_ci = P.sem(name)
    s_ini = P.sem(name)
    s_aex = P.sem(name)
    s_uld = [P.sem(name), P.sem(name)]
    s_zld = [P.sem(name), P.sem(name)]
    s_dl = P.sem(name)
    s_mmdt = P.sem(name)
    s_pre = P.sem(name)
    s_dtev = P.sem(name)
    s_dt = P.sem(name)
    s_du = P.sem(name)
    s_bc = [P.sem(name), P.sem(name)]
    s_a = P.sem(name)
    s_b = P.sem(name)
    s_sc = P.sem(name)
    s_nv = P.sem(name)
    s_yp = P.sem(name)
    s_si = P.sem(name)
    s_g0 = P.sem(name)
    s_go = P.sem(name)

    acol = P.sb([128, GI * NST], F32, name + "acol")
    bdt_c = P.sb([128, GI], F32, name + "bdt")
    d_c = P.sb([128, GI], F32, name + "dc")
    state = P.sb([128, GI * NST], F32, name + "st")
    if layer == 0:
        wdt_sb = P.sb([64, DIH], BF16, name + "wdt")
        dtlow = [P.sb([64, L // 2], BF16, name + "dtl") for _ in range(2)]
        ps_dt = [P.ps([128, 512], name + "pdt") for _ in range(2)]
        dtw_c = dtraw_bc = None
    else:
        dtw_c = P.sb([128, GI], F32, name + "dtw")
        dtraw_bc = [P.sb([128, TS], F32, name + "dtr") for _ in range(2)]
        wdt_sb = dtlow = ps_dt = None

    dtt = P.sb([128, GI * TS], F32, name + "dtt")
    du = P.sb([128, GI * TS], F32, name + "du")
    yb = P.sb([128, GI * TS], F32, name + "yb")
    sptmp = P.sb([128, 2 * 512], F32, name + "sptmp")
    spabs = P.sb([128, 512], F32, name + "spabs")
    jnk = P.sb([128, 8], F32, name + "jnk")
    ut = [P.sb([128, TS], F32, name + "ut") for _ in range(2)]
    at = [P.sb([128, TS], F32, name + "at") for _ in range(2)]
    bt = [P.sb([128, TS], F32, name + "bt") for _ in range(2)]
    ht = [P.sb([128, TS], F32, name + "ht") for _ in range(2)]
    tmp = P.sb([128, TS], F32, name + "tmp")
    bcb = [P.sb([128, 2 * TS], F32, name + "bcb") for _ in range(2)]
    zt = [P.sb([128, TS], BF16, name + "zt") for _ in range(2)]
    sz = [P.sb([128, TS], F32, name + "sz") for _ in range(2)]
    g0b = P.sb([128, GI * TS], BF16, name + "g0")

    sp_t = _a(sp_hbm).tensor

    @blk.gpsimd
    def _(g):
        s_w.inc(g.dma_start(out=acol[:], in_=_a(alog_hbm)), 16)
        s_w.inc(g.dma_start(out=bdt_c[:], in_=_a(bdt_hbm)), 16)
        s_w.inc(g.dma_start(out=d_c[:], in_=_a(dvec_hbm)), 16)
        if layer == 0:
            s_w.inc(g.dma_start(out=wdt_sb[:], in_=_a(wdt_hbm)), 16)
            s_dl.inc(g.dma_start(out=dtlow[0][:], in_=_a(sp_hbm)[0:64, 0:L // 2]), 16)
        else:
            s_w.inc(g.dma_start(out=dtw_c[:], in_=_a(dtw_hbm)), 16)
        P.ensure_consts(g, s_ci)

    @blk.scalar
    def _(s):
        s.wait_ge(s_w.h, s_w.n)
        s_aex.inc(s.activation(acol[:], acol[:], AF.Exp), 1)

    @blk.vector
    def _(v):
        v.wait_ge(s_aex.h, 1)
        v.tensor_scalar_mul(acol[:], acol[:], -1.0)
        s_ini.inc(v.memset(state[:], 0.0), 1)

    @blk.sync
    def _(sync):
        sync.wait_ge(s_w.h, s_w.n)
        for c in range(nchs):
            col0 = c * TS
            if layer == 1:
                if c >= 2:
                    sync.wait_ge(s_pre.h, (c - 1) * GI)
                rep = bass.AP(sp_t, 32 * L + col0, [[0, 128], [1, TS]])
                s_dl.inc(sync.dma_start(out=dtraw_bc[c % 2][:], in_=rep), 16)
            for g in range(GI):
                ic = c * GI + g
                if ic >= 2:
                    sync.wait_ge(s_du.h, ic - 1)
                s_uld[ic % 2].inc(sync.dma_start(
                    out=ut[ic % 2][:], in_=dram_fm(u_hbm, col0, TS, 1, g0=g)), 16)
            for n in range(NST):
                icn = c * NST + n
                if icn >= 2:
                    sync.wait_ge(s_nv.h, icn - 1)
                rep = bass.AP(sp_t, (Boff + n) * L + col0, [[0, 128], [1, TS]])
                s_bc[icn % 2].inc(sync.dma_start(out=bcb[icn % 2][:, 0:TS], in_=rep), 16)
                rep = bass.AP(sp_t, (Coff + n) * L + col0, [[0, 128], [1, TS]])
                s_bc[icn % 2].inc(sync.dma_start(out=bcb[icn % 2][:, TS:2 * TS], in_=rep), 16)
            for g in range(GI):
                ic = c * GI + g
                if ic >= 2:
                    sync.wait_ge(s_si.h, ic - 1)
                s_zld[ic % 2].inc(sync.dma_start(
                    out=zt[ic % 2][:], in_=dram_fm(z_hbm, col0, TS, 1, g0=g)), 16)
            sync.wait_ge(s_g0.h, (c + 1) * GI)
            s_go.inc(sync.dma_start(out=dram_fm(g_hbm, col0, TS, GI),
                                    in_=sb_fm(g0b, TS, GI)), 16)
        sync.wait_ge(s_go.h, s_go.n)

    if layer == 0:
        @blk.tensor
        def _(t):
            t.wait_ge(s_w.h, s_w.n)
            for c in range(nchs):
                for g in range(GI):
                    for sub in range(NSUB):
                        isub = (c * GI + g) * NSUB + sub
                        if isub >= 2:
                            t.wait_ge(s_dtev.h, isub - 1)
                        half = c // (nchs // 2)
                        t.wait_ge(s_dl.h, (half + 1) * 16)
                        c0 = (c % (nchs // 2)) * TS + sub * 512
                        i = t.matmul(ps_dt[isub % 2][:],
                                     wdt_sb[:, g * 128:(g + 1) * 128],
                                     dtlow[half][:, c0:c0 + 512],
                                     start=True, stop=True)
                        s_mmdt.inc(i, 1)

    @blk.scalar
    def _(s):
        s.wait_ge(s_ci.h, s_ci.n)
        for c in range(nchs):
            for g in range(GI):
                ic = c * GI + g
                for sub in range(NSUB):
                    isub = ic * NSUB + sub
                    sl = slice(g * TS + sub * 512, g * TS + (sub + 1) * 512)
                    tsl = slice(sub * 512, (sub + 1) * 512)
                    if layer == 0:
                        s.wait_ge(s_mmdt.h, isub + 1)
                        src = ps_dt[isub % 2][:]
                        bias = bdt_c[:, g:g + 1]
                    else:
                        if sub == 0:
                            s.wait_ge(s_pre.h, ic + 1)
                        src = dtt[:, sl]
                        bias = 0.0
                    if ic >= 1 and sub == 0:
                        s.wait_ge(s_dt.h, ic - 1 + 1)
                    s.activation(spabs[:], src, AF.Abs, bias=bias)
                    s.activation(spabs[:], spabs[:], AF.Exp, scale=-1.0)
                    s.activation(sptmp[:, tsl], spabs[:], AF.Ln, bias=1.0)
                    i = s.activation(dtt[:, sl], src, AF.Relu, bias=bias)
                    s_dtev.inc(i, 1)
            for n in range(NST):
                for g in range(GI):
                    icng = (c * NST + n) * GI + g
                    if icng == 0:
                        s.wait_ge(s_ini.h, 1)
                    s.wait_ge(s_dt.h, c * GI + g + 1)
                    if icng >= 2:
                        s.wait_ge(s_sc.h, icng - 1)
                    i = s.activation(at[icng % 2][:], dtt[:, g * TS:(g + 1) * TS],
                                     AF.Exp, scale=acol[:, g * NST + n:g * NST + n + 1])
                    s_a.inc(i, 1)
            for g in range(GI):
                ic = c * GI + g
                s.wait_ge(s_zld[ic % 2].h, ((ic // 2) + 1) * 16)
                if ic >= 2:
                    s.wait_ge(s_g0.h, ic - 1)
                i = s.activation(sz[ic % 2][:], zt[ic % 2][:], AF.Silu)
                s_si.inc(i, 1)

    @blk.gpsimd
    def _(g):
        for c in range(nchs):
            if layer == 0 and c == nchs // 2:
                g.wait_ge(s_mmdt.h, (nchs // 2) * GI * NSUB)
                s_dl.inc(g.dma_start(out=dtlow[1][:],
                                     in_=_a(sp_hbm)[0:64, L // 2:L]), 16)
            for n in range(NST):
                icn = c * NST + n
                for gg in range(GI):
                    icng = icn * GI + gg
                    g.wait_ge(s_du.h, c * GI + gg + 1)
                    g.wait_ge(s_bc[icn % 2].h, ((icn // 2) + 1) * 32)
                    if icng >= 2:
                        g.wait_ge(s_sc.h, icng - 1)
                    i = g.tensor_mul(bt[icng % 2][:], du[:, gg * TS:(gg + 1) * TS],
                                     bcb[icn % 2][:, 0:TS])
                    s_b.inc(i, 1)

    @blk.vector
    def _(v):
        def yops(gg, icng):
            slg = slice(gg * TS, (gg + 1) * TS)
            cbuf = bcb[(icng // GI) % 2][:, TS:2 * TS]
            v.tensor_mul(tmp[:], ht[gg % 2][:], cbuf)
            v.tensor_add(yb[:, slg], yb[:, slg], tmp[:])
            n_ = (icng // GI) % NST
            col = gg * NST + n_
            return v.tensor_copy(state[:, col:col + 1], ht[gg % 2][:, TS - 1:TS])

        for c in range(nchs):
            for g in range(GI):
                ic = c * GI + g
                sl = slice(g * TS, (g + 1) * TS)
                if layer == 1:
                    v.wait_ge(s_dl.h, (c + 1) * 16)
                    if c >= 1:
                        v.wait_ge(s_a.h, c * NST * GI)
                    i = v.tensor_scalar(dtt[:, sl], dtraw_bc[c % 2][:],
                                        dtw_c[:, g:g + 1], bdt_c[:, g:g + 1],
                                        ALU.mult, ALU.add)
                    s_pre.inc(i, 1)
                v.wait_ge(s_dtev.h, ic * NSUB + NSUB)
                if layer == 0 and c >= 1 and g == 0:
                    v.wait_ge(s_a.h, c * NST * GI)
                for sub in range(NSUB):
                    ssl = slice(g * TS + sub * 512, g * TS + (sub + 1) * 512)
                    i = v.tensor_add(dtt[:, ssl], dtt[:, ssl],
                                     sptmp[:, sub * 512:(sub + 1) * 512])
                s_dt.inc(i, 1)
                v.wait_ge(s_uld[ic % 2].h, ((ic // 2) + 1) * 16)
                if c >= 1 and g == 0:
                    v.wait_ge(s_b.h, c * NST * GI)
                v.tensor_mul(du[:, sl], dtt[:, sl], ut[ic % 2][:])
                i = v.tensor_scalar_mul(yb[:, sl], ut[ic % 2][:], d_c[:, g:g + 1])
                s_du.inc(i, 1)
            for n in range(NST):
                icn = c * NST + n
                prev = None
                for g in range(GI):
                    icng = icn * GI + g
                    col = g * NST + n
                    v.wait_ge(s_a.h, icng + 1)
                    v.wait_ge(s_b.h, icng + 1)
                    if icng == 0:
                        v.wait_ge(s_ini.h, 1)
                    i = v.tensor_tensor_scan(ht[g % 2][:], at[icng % 2][:],
                                             bt[icng % 2][:], state[:, col:col + 1],
                                             ALU.mult, ALU.add)
                    s_sc.inc(i, 1)
                    if prev is not None:
                        yops(prev[0], prev[1])
                    prev = (g, icng)
                v.tensor_copy(jnk[:], at[prev[1] % 2][:, 0:8])
                i = yops(prev[0], prev[1])
                s_nv.inc(i, 1)
            for g in range(GI):
                ic = c * GI + g
                sl = slice(g * TS, (g + 1) * TS)
                v.wait_ge(s_si.h, ic + 1)
                if c >= 1 and g == 0:
                    v.wait_ge(s_go.h, c * 16)
                i = v.tensor_mul(g0b[:, sl], yb[:, sl], sz[ic % 2][:])
                s_g0.inc(i, 1)
    return s_go



def emit_proj(P, rhs_hbm, w_hbm, out_hbm, K, M, T, res_hbm=None, res2_hbm=None,
              out_f32=True, name="pj"):
    nc, blk = P.nc, P.blk
    KG, MG = K // 128, M // 128
    nch = T // TCH
    s_w = P.sem(name)
    s_ld = SemPair(P, name)
    s_res = SemPair(P, name)
    s_res2 = SemPair(P, name)
    s_mm = P.sem(name)
    s_ev = P.sem(name)
    s_st = P.sem(name)

    wsb = load_w_bf16(P, s_w, w_hbm, K, M, name + "w")
    rt = [P.sb([128, KG * TCH], BF16, name + "r") for _ in range(2)]
    rest = [P.sb([128, MG * TCH], F32, name + "res") for _ in range(2)]         if res_hbm is not None else None
    res2t = [P.sb([128, MG * TCH], F32, name + "res2") for _ in range(2)] \
        if res2_hbm is not None else None
    ot = P.sb([128, MG * TCH], F32 if out_f32 else BF16, name + "o")
    ps = [P.ps([128, TCH], name + "ps") for _ in range(2)]

    @blk.sync
    def _(sync):
        for j in range(nch):
            if j >= 2:
                sync.wait_ge(s_mm.h, (j - 1) * MG)
            s_ld.inc(sync.dma_start(out=rt[j % 2][:],
                                    in_=dram_fm(rhs_hbm, j * TCH, TCH, KG)), j)
            if res_hbm is not None:
                if j >= 2:
                    sync.wait_ge(s_ev.h, (j - 1) * MG)
                s_res.inc(sync.dma_start(out=rest[j % 2][:],
                                         in_=dram_fm(res_hbm, j * TCH, TCH, MG)), j)
            if res2_hbm is not None:
                s_res2.inc(sync.dma_start(out=res2t[j % 2][:],
                                          in_=dram_fm(res2_hbm, j * TCH, TCH, MG)), j)
            if j >= 1:
                sync.wait_ge(s_ev.h, j * MG)
                s_st.inc(sync.dma_start(out=dram_fm(out_hbm, (j - 1) * TCH, TCH, MG),
                                        in_=sb_fm(ot, TCH, MG)), 16)
        sync.wait_ge(s_ev.h, nch * MG)
        s_st.inc(sync.dma_start(out=dram_fm(out_hbm, (nch - 1) * TCH, TCH, MG),
                                in_=sb_fm(ot, TCH, MG)), 16)
        sync.wait_ge(s_st.h, s_st.n)

    @blk.tensor
    def _(t):
        t.wait_ge(s_w.h, s_w.n)
        for j in range(nch):
            s_ld.wait(t, j)
            for mg in range(MG):
                imm = j * MG + mg
                if imm >= 2:
                    t.wait_ge(s_ev.h, imm - 1)
                for kg in range(KG):
                    i = t.matmul(ps[imm % 2][:],
                                 wsb[:, (kg * MG + mg) * 128:(kg * MG + mg + 1) * 128],
                                 rt[j % 2][:, kg * TCH:(kg + 1) * TCH],
                                 start=(kg == 0), stop=(kg == KG - 1))
                s_mm.inc(i, 1)

    @blk.vector
    def _(v):
        for j in range(nch):
            if res_hbm is not None:
                s_res.wait(v, j)
            if res2_hbm is not None:
                s_res2.wait(v, j)
            if j >= 1:
                v.wait_ge(s_st.h, j * 16)
            for mg in range(MG):
                imm = j * MG + mg
                v.wait_ge(s_mm.h, imm + 1)
                sl = slice(mg * TCH, (mg + 1) * TCH)
                if res2_hbm is not None:
                    v.tensor_add(ot[:, sl], rest[j % 2][:, sl], ps[imm % 2][:])
                    i = v.tensor_add(ot[:, sl], ot[:, sl], res2t[j % 2][:, sl])
                elif res_hbm is not None:
                    i = v.tensor_add(ot[:, sl], rest[j % 2][:, sl], ps[imm % 2][:])
                else:
                    i = v.tensor_copy(ot[:, sl], ps[imm % 2][:])
                s_ev.inc(i, 1)
    return s_st



def emit_ident(P, blk, s_done, dtype=F32, name="id"):
    idt = P.sb([128, 128], dtype, name + "t")
    i0 = P.sb([128, 128], F32, name + "i0")
    i1 = P.sb([128, 1], F32, name + "i1")

    @blk.gpsimd
    def _(g):
        g.iota(i0[:], [[1, 128]], channel_multiplier=0, allow_small_or_imprecise_dtypes=True)
        s_done.inc(g.iota(i1[:], [[1, 1]], channel_multiplier=1,
                          allow_small_or_imprecise_dtypes=True), 1)

    @blk.vector
    def _(v):
        v.wait_ge(s_done.h, 1)
        s_done.inc(v.tensor_scalar(idt[:], i0[:], i1[:, 0:1], None, ALU.is_equal), 1)
    return idt


def emit_router(P, xn_hbm, wr_hbm, w2_hbm, br_hbm, cw_hbm, mode, T=TL, name="rt",
                debug=None):
    nc, blk = P.nc, P.blk
    P.const(-1.0)
    if mode == "l0":
        P.const(10.0)
    NE = 8
    nblk = T // 128
    GSZ = min(16, nblk)
    ngrp = nblk // GSZ
    s_w = P.sem(name)
    s_ci = P.sem(name)
    s_id = P.sem(name)
    s_ld = SemPair(P, name)
    s_mm = P.sem(name)
    s_sm = P.sem(name)
    s_ex = P.sem(name)
    s_pp = P.sem(name)
    s_p2 = P.sem(name)
    s_cw = P.sem(name)
    s_tr = P.sem(name)
    s_ote = P.sem(name)
    s_st = P.sem(name)

    ident = emit_ident(P, blk, s_id, BF16, name + "id") if debug not in ("notr", "mm", "p1", "p2") else None
    nw = 2 if mode == "l0" else 1
    wr_sb = P.sb([128, G * NE * nw], BF16, name + "wr")
    br_rep = P.sb([128, NE], F32, name + "br") if mode == "l1" else None
    xt = [P.sb([128, G * 512], BF16, name + "x") for _ in range(2)]
    lg = P.sb([128, 2 * NE], F32, name + "lg")
    mx = P.sb([128, 2], F32, name + "mx")
    ex = P.sb([128, 2 * NE], F32, name + "ex")
    sm = P.sb([128, 2], F32, name + "sm")
    rcp = P.sb([128, 2], F32, name + "rcp")
    cwt = P.sb([128, NE], F32, name + "cw")
    cws = P.sb([128, 4], F32, name + "cws")
    spc = P.sb([128, 2], F32, name + "spc")
    spc2 = P.sb([128, 8], F32, name + "spc2")
    cwt16 = P.sb([128, GSZ * NE], F32 if debug in ("notr", "mm", "p1", "p2") else BF16, name + "cw16")
    cw128 = P.sb([128, GSZ * NE], F32, name + "cw128")
    ps_r = [P.ps([128, NE * nw], name + "pr") for _ in range(2)]
    pst = P.psb([128, GSZ * NE], name + "pst")

    @blk.gpsimd
    def _(g):
        wv = _a(wr_sb)
        s_w.inc(g.dma_start(
            out=bass.AP(wv.tensor, wv.offset, [[NE * nw * G, 128], [NE * nw, G], [1, NE]]),
            in_=dram_fm(wr_hbm, 0, NE, G)), 16)
        if mode == "l0":
            s_w.inc(g.dma_start(
                out=bass.AP(wv.tensor, wv.offset + NE,
                            [[NE * nw * G, 128], [NE * nw, G], [1, NE]]),
                in_=dram_fm(w2_hbm, 0, NE, G)), 16)
        else:
            rep = bass.AP(_a(br_hbm).tensor, 0, [[0, 128], [1, NE]])
            s_w.inc(g.dma_start(out=br_rep[:], in_=rep), 16)
        P.ensure_consts(g, s_ci)
        if debug in ("notr", "mm", "p1", "p2"):
            for blki in range(nblk):
                g.wait_ge(s_cw.h, blki + 1)
                dst = bass.AP(_a(cw_hbm).tensor, blki * 128 * NE, [[NE, 128], [1, NE]])
                s_st.inc(g.dma_start(out=dst, in_=cwt16[:, 0:NE]), 16)
            g.wait_ge(s_st.h, s_st.n)

    @blk.sync
    def _(sync):
        nch = T // 512
        for j in range(nch):
            if j >= 2:
                sync.wait_ge(s_mm.h, (j - 1) * 4)
            s_ld.inc(sync.dma_start(out=xt[j % 2][:],
                                    in_=dram_fm(xn_hbm, j * 512, 512, G)), j)
        if debug in ("notr", "mm", "p1", "p2"):
            return
        for gr in range(ngrp):
            sync.wait_ge(s_ote.h, gr + 1)
            for e in range(NE):
                dst = bass.AP(_a(cw_hbm).tensor, e * T + gr * GSZ * 128,
                              [[128, GSZ], [1, 128]])
                s_st.inc(sync.dma_start(
                    out=dst, in_=cw128[e * GSZ:(e + 1) * GSZ, :]), 16)
        sync.wait_ge(s_st.h, s_st.n)

    @blk.tensor
    def _(t):
        t.wait_ge(s_w.h, s_w.n)
        if debug not in ("notr", "mm", "p1", "p2"):
            t.wait_ge(s_id.h, 2)
        for blki in range(nblk):
            j = blki // 4
            sub = blki % 4
            s_ld.wait(t, j)
            if blki >= 2:
                t.wait_ge(s_sm.h, blki - 1)
            xsl = lambda kg: xt[j % 2][:, kg * 512 + sub * 128:kg * 512 + (sub + 1) * 128]
            for kg in range(G):
                i = t.matmul(ps_r[blki % 2][:], xsl(kg),
                             wr_sb[:, kg * NE * nw:(kg + 1) * NE * nw],
                             start=(kg == 0), stop=(kg == G - 1))
            s_mm.inc(i, 1)
            if debug not in ("notr", "mm", "p1", "p2") and blki % GSZ == GSZ - 1:
                gr = blki // GSZ
                t.wait_ge(s_cw.h, blki + 1)
                if gr >= 1:
                    t.wait_ge(s_ote.h, gr)
                i = t.transpose(pst[:], cwt16[:], ident[:])
                s_tr.inc(i, 1)

    @blk.scalar
    def _(s):
        s.wait_ge(s_ci.h, s_ci.n)
        if debug == "mm" or debug == "p1":
            return
        import os
        _v = os.environ.get("RT_ACT_VARIANT", "full")
        for blki in range(nblk):
            s.wait_ge(s_sm.h, blki + 1)
            if mode == "l0":
                if _v == "noaccum":
                    s.activation(ex[:, 0:NE], lg[:, 0:NE], AF.Exp, bias=mx[:, 0:1])
                    i = s.activation(ex[:, NE:2 * NE], lg[:, NE:2 * NE], AF.Exp,
                                     bias=mx[:, 1:2])
                elif _v == "nobias":
                    s.activation(ex[:, 0:NE], lg[:, 0:NE], AF.Exp,
                                 accum_out=sm[:, 0:1])
                    i = s.activation(ex[:, NE:2 * NE], lg[:, NE:2 * NE], AF.Exp,
                                     accum_out=sm[:, 1:2])
                elif _v == "one":
                    i = s.activation(ex[:, 0:NE], lg[:, 0:NE], AF.Exp, bias=mx[:, 0:1],
                                     accum_out=sm[:, 0:1])
                else:
                    s.activation(ex[:, 0:NE], lg[:, 0:NE], AF.Exp, bias=mx[:, 0:1],
                                 accum_out=sm[:, 0:1])
                    i = s.activation(ex[:, NE:2 * NE], lg[:, NE:2 * NE], AF.Exp,
                                     bias=mx[:, 1:2], accum_out=sm[:, 1:2])
            else:
                i = s.activation(ex[:, 0:NE], lg[:, 0:NE], AF.Exp, bias=mx[:, 0:1],
                                 accum_out=sm[:, 0:1])
            s_ex.inc(i, 1)
            if mode == "l0" and debug is None:
                s.wait_ge(s_pp.h, blki + 1)
                i = s.activation(spc2[:, 0:NE], cwt[:], AF.Copy, accum_out=cws[:, 2:3])
                s_p2.inc(i, 1)

    @blk.vector
    def _(v):
        if debug == "p2":
            for blki in range(nblk):
                v.wait_ge(s_mm.h, blki + 1)
                if blki >= 1:
                    v.wait_ge(s_st.h, blki * 16)
                v.tensor_copy(lg[:, 0:NE], ps_r[blki % 2][:, 0:NE])
                v.tensor_scalar_mul(lg[:, NE:2 * NE], ps_r[blki % 2][:, NE:2 * NE], 10.0)
                v.reduce_max(mx[:, 0:1], lg[:, 0:NE], axis=AX.X, negate=True)
                i = v.reduce_max(mx[:, 1:2], lg[:, NE:2 * NE], axis=AX.X, negate=True)
                s_sm.inc(i, 1)
                import os as _os
                _w = _os.environ.get("RT_P2V", "wait1")
                if _w == "wait1":
                    v.wait_ge(s_ex.h, blki + 1)
                    i = v.tensor_copy(cwt16[:, 0:NE], lg[:, 0:NE])
                elif _w == "copy2":
                    v.tensor_copy(cwt[:], lg[:, 0:NE])
                    i = v.tensor_copy(cwt16[:, 0:NE], cwt[:])
                else:
                    i = v.tensor_copy(cwt16[:, 0:NE], lg[:, 0:NE])
                s_cw.inc(i, 1)
            return
        if debug == "p1":
            for blki in range(nblk):
                v.wait_ge(s_mm.h, blki + 1)
                if blki >= 1:
                    v.wait_ge(s_st.h, blki * 16)
                v.tensor_copy(lg[:, 0:NE], ps_r[blki % 2][:, 0:NE])
                v.tensor_scalar_mul(lg[:, NE:2 * NE], ps_r[blki % 2][:, NE:2 * NE], 10.0)
                v.reduce_max(mx[:, 0:1], lg[:, 0:NE], axis=AX.X, negate=True)
                i = v.reduce_max(mx[:, 1:2], lg[:, NE:2 * NE], axis=AX.X, negate=True)
                s_sm.inc(i, 1)
                i = v.tensor_copy(cwt16[:, 0:NE], lg[:, 0:NE])
                s_cw.inc(i, 1)
            return
        if debug == "mm":
            for blki in range(nblk):
                v.wait_ge(s_mm.h, blki + 1)
                if blki >= 1:
                    v.wait_ge(s_st.h, blki * 16)
                i = v.tensor_copy(cwt16[:, 0:NE], ps_r[blki % 2][:, 0:NE])
                s_sm.inc(i, 1)
                i = v.tensor_copy(spc[:], ps_r[blki % 2][:, 0:2])
                s_cw.inc(i, 1)
            return
        for blki in range(nblk):
            v.wait_ge(s_mm.h, blki + 1)
            if mode == "l0":
                v.tensor_copy(lg[:, 0:NE], ps_r[blki % 2][:, 0:NE])
                v.tensor_scalar_mul(lg[:, NE:2 * NE], ps_r[blki % 2][:, NE:2 * NE], 10.0)
                v.reduce_max(mx[:, 0:1], lg[:, 0:NE], axis=AX.X, negate=True)
                i = v.reduce_max(mx[:, 1:2], lg[:, NE:2 * NE], axis=AX.X, negate=True)
            else:
                v.tensor_add(lg[:, 0:NE], ps_r[blki % 2][:, 0:NE], br_rep[:])
                v.tensor_copy(spc[:], br_rep[:, 0:2])
                i = v.reduce_max(mx[:, 0:1], lg[:, 0:NE], axis=AX.X, negate=True)
            s_sm.inc(i, 1)
            v.wait_ge(s_ex.h, blki + 1)
            gr = blki // GSZ
            if blki % GSZ == 0 and gr >= 1:
                v.wait_ge(s_tr.h, gr)
            cw16_view = bass.AP(_a(cwt16).tensor, _a(cwt16).offset + (blki % GSZ),
                                [[GSZ * NE, 128], [GSZ, NE]])
            if debug == "logits":
                v.tensor_copy(cwt[:], lg[:, 0:NE])
            elif debug == "exp":
                v.tensor_copy(cwt[:], ex[:, 0:NE])
            elif mode == "l0":
                v.tensor_mul(cws[:, 0:1], sm[:, 0:1], sm[:, 1:2])
                i = v.tensor_mul(cwt[:], ex[:, 0:NE], ex[:, NE:2 * NE])
                s_pp.inc(i, 1)
                v.wait_ge(s_p2.h, blki + 1)
                v.scalar_tensor_tensor(cws[:, 1:2], cws[:, 0:1], 1e-9,
                                       cws[:, 2:3], ALU.mult, ALU.add)
                v.tensor_copy(spc[:], sm[:])
                v.reciprocal(cws[:, 3:4], cws[:, 1:2])
                v.tensor_copy(spc[:], sm[:])
                v.tensor_scalar_mul(cwt[:], cwt[:], cws[:, 3:4])
                v.tensor_copy(spc[:], sm[:])
            else:
                v.reciprocal(rcp[:, 0:1], sm[:, 0:1])
                v.tensor_scalar_mul(mx[:, 1:2], mx[:, 0:1], -1.0)
                v.tensor_copy(spc[:], lg[:, 0:2])
                v.tensor_scalar(cwt[:], lg[:, 0:NE], mx[:, 1:2], None, ALU.is_ge)
                v.tensor_copy(spc[:], lg[:, 0:2])
                v.tensor_scalar_mul(cwt[:], cwt[:], rcp[:, 0:1])
                v.tensor_copy(spc[:], sm[:])
            i = v.tensor_copy(cw16_view, cwt[:])
            s_cw.inc(i, 1)
            if debug != "notr" and blki % GSZ == GSZ - 1:
                v.wait_ge(s_tr.h, gr + 1)
                if gr >= 1:
                    v.wait_ge(s_st.h, gr * 16)
                i = v.tensor_copy(cw128[:], pst[:])
                s_ote.inc(i, 1)
    return s_st

def emit_moe(P, xn_hbm, wg_hbm, wu_hbm, wd_hbm, cw_hbm, out_hbm, two_mats=True,
             b1_hbm=None, b2_hbm=None, T=TL, name="moe"):
    nc, blk = P.nc, P.blk
    NE, F = 8, 4096
    TB = min(1024, T)
    NTC = TB // 512
    ntb = T // TB
    FG = F // 128
    MB = 4
    MH = FG // MB
    s_xn = [P.sem(name), P.sem(name)]
    s_wgu = [P.sem(name), P.sem(name)]
    s_wd = P.sem(name)
    s_gu = P.sem(name)
    s_sil = P.sem(name)
    s_hev = P.sem(name)
    s_dmm = P.sem(name)
    s_dev = P.sem(name)
    s_acc = P.sem(name)
    s_cwb = [P.sem(name), P.sem(name)]

    xnt = [P.sb([128, G * TB], BF16, name + "xn") for _ in range(2)]
    wbuf = P.sb([128, 2 * 16384], BF16, name + "wb")
    hsb = P.sb([128, FG * TB], BF16, name + "h")
    cwbc = [P.sb([128, TB], F32, name + "cwb") for _ in range(2)]
    sil = P.sb([128, 512], F32, name + "sil")
    mjnk = P.sb([128, 4], F32, name + "jnk")
    dtmp = [P.sb([128, 512], F32, name + "dt") for _ in range(2)]
    b1c = P.sb([128, NE * FG], F32, name + "b1") if b1_hbm is not None else None
    b2c = P.sb([128, NE * G], F32, name + "b2") if b2_hbm is not None else None
    ps_g = [P.ps([128, 512], name + "pg") for _ in range(2)]
    ps_u = [P.ps([128, 512], name + "pu") for _ in range(2)] if two_mats else None
    ps_d = [P.ps([128, 512], name + "pd") for _ in range(2)]

    def wgu_ap(par, which):
        base = par * 16384 + which * 8192
        return _a(wbuf)[:, base:base + 8192]

    def wd_ap(fh):
        return _a(wbuf)[:, fh * 1024:(fh + 1) * 1024]

    s_w0 = P.sem(name)

    @blk.gpsimd
    def _(g):
        if b1_hbm is not None:
            s_w0.inc(g.dma_start(out=b1c[:], in_=_a(b1_hbm)), 16)
        if b2_hbm is not None:
            s_w0.inc(g.dma_start(out=b2c[:], in_=_a(b2_hbm)), 16)
        for e in range(NE):
            for tb in range(ntb):
                it = e * ntb + tb
                if it >= 2:
                    g.wait_ge(s_gu.h, (it - 1) * MB * MH * NTC)
                s_xn[it % 2].inc(g.dma_start(
                    out=xnt[it % 2][:], in_=dram_fm(xn_hbm, tb * TB, TB, G)), 16)
                for mb in range(MB):
                    imb = it * MB + mb
                    if imb >= 1:
                        g.wait_ge(s_dmm.h, it * G * NTC)
                    if imb >= 2:
                        g.wait_ge(s_gu.h, (imb - 1) * MH * NTC)
                    par = imb % 2
                    wge = _a(wg_hbm)[e]
                    s_wgu[par].inc(g.dma_start(
                        out=wgu_ap(par, 0).rearrange("p (k m) -> p k m", k=G),
                        in_=dram_fm(wge, mb * TB, TB, G)), 16)
                    if two_mats:
                        wue = _a(wu_hbm)[e]
                        s_wgu[par].inc(g.dma_start(
                            out=wgu_ap(par, 1).rearrange("p (k m) -> p k m", k=G),
                            in_=dram_fm(wue, mb * TB, TB, G)), 16)
                g.wait_ge(s_gu.h, (it + 1) * MB * MH * NTC)
                wde = _a(wd_hbm)[e]
                for fh in range(FG):
                    s_wd.inc(g.dma_start(
                        out=wd_ap(fh).rearrange("p (o m) -> p o m", o=1),
                        in_=dram_fm(wde, 0, D, 1, g0=fh)), 16)
                for dmh in range(G):
                    for tc in range(NTC):
                        idt = (it * G + dmh) * NTC + tc
                        g.wait_ge(s_dev.h, idt + 1)
                        import os as _os
                        if _os.environ.get("MOE_SERIAL_ACC"):
                            g.wait_ge(s_acc.h, s_acc.n)
                        elif e >= 1:
                            g.wait_ge(s_acc.h, e * ntb * G * NTC * 16)
                        s_acc.inc(g.dma_start(
                            out=dram_fm(out_hbm, tb * TB + tc * 512, 512, 1, g0=dmh),
                            in_=dtmp[idt % 2][:], accum_op=ALU.add), 16)
        g.wait_ge(s_acc.h, s_acc.n)

    @blk.sync
    def _(sync):
        for e in range(NE):
            for tb in range(ntb):
                it = e * ntb + tb
                if it >= 2:
                    sync.wait_ge(s_dev.h, (it - 1) * G * NTC)
                rep = bass.AP(_a(cw_hbm).tensor, e * T + tb * TB, [[0, 128], [1, TB]])
                s_cwb[it % 2].inc(sync.dma_start(out=cwbc[it % 2][:], in_=rep), 16)

    @blk.tensor
    def _(t):
        t.wait_ge(s_w0.h, s_w0.n)
        for e in range(NE):
            for tb in range(ntb):
                it = e * ntb + tb
                t.wait_ge(s_xn[it % 2].h, ((it // 2) + 1) * 16)
                for mb in range(MB):
                    imb = it * MB + mb
                    par = imb % 2
                    nw = 32 if two_mats else 16
                    t.wait_ge(s_wgu[par].h, ((imb // 2) + 1) * nw)
                    for mh in range(MH):
                        for tc in range(NTC):
                            ih = ((it * MB + mb) * MH + mh) * NTC + tc
                            if ih >= 2:
                                t.wait_ge(s_hev.h, ih - 1)
                            for kg in range(G):
                                i = t.matmul(
                                    ps_g[ih % 2][:],
                                    wgu_ap(par, 0)[:, kg * TB + mh * 128:kg * TB + (mh + 1) * 128],
                                    xnt[it % 2][:, kg * TB + tc * 512:kg * TB + (tc + 1) * 512],
                                    start=(kg == 0), stop=(kg == G - 1))
                            if two_mats:
                                for kg in range(G):
                                    i = t.matmul(
                                        ps_u[ih % 2][:],
                                        wgu_ap(par, 1)[:, kg * TB + mh * 128:kg * TB + (mh + 1) * 128],
                                        xnt[it % 2][:, kg * TB + tc * 512:kg * TB + (tc + 1) * 512],
                                        start=(kg == 0), stop=(kg == G - 1))
                            s_gu.inc(i, 1)
                t.wait_ge(s_hev.h, (it + 1) * MB * MH * NTC)
                t.wait_ge(s_wd.h, (it + 1) * FG * 16)
                for dmh in range(G):
                    for tc in range(NTC):
                        idt = (it * G + dmh) * NTC + tc
                        if idt >= 2:
                            t.wait_ge(s_dev.h, idt - 1)
                        for fh in range(FG):
                            i = t.matmul(
                                ps_d[idt % 2][:],
                                wd_ap(fh)[:, dmh * 128:(dmh + 1) * 128],
                                hsb[:, fh * TB + tc * 512:fh * TB + (tc + 1) * 512],
                                start=(fh == 0), stop=(fh == FG - 1))
                        s_dmm.inc(i, 1)

    @blk.scalar
    def _(s):
        for e in range(NE):
            for tb in range(ntb):
                it = e * ntb + tb
                if it >= 1:
                    s.wait_ge(s_dmm.h, it * G * NTC)
                for mb in range(MB):
                    for mh in range(MH):
                        for tc in range(NTC):
                            ih = ((it * MB + mb) * MH + mh) * NTC + tc
                            s.wait_ge(s_gu.h, ih + 1)
                            fh = mb * MH + mh
                            hsl = slice(fh * TB + tc * 512, fh * TB + (tc + 1) * 512)
                            if two_mats:
                                if ih >= 1:
                                    s.wait_ge(s_hev.h, ih)
                                i = s.activation(sil[:], ps_g[ih % 2][:], AF.Silu)
                            else:
                                b = b1c[:, e * FG + fh:e * FG + fh + 1]
                                s.activation(hsb[:, hsl], ps_g[ih % 2][:],
                                             AF.Relu, bias=b)
                                i = s.copy(sil[:, 0:4], ps_g[ih % 2][:, 0:4])
                            s_sil.inc(i, 1)

    @blk.vector
    def _(v):
        for e in range(NE):
            for tb in range(ntb):
                it = e * ntb + tb
                if it >= 1 and two_mats:
                    v.wait_ge(s_dmm.h, it * G * NTC)
                for mb in range(MB):
                    for mh in range(MH):
                        for tc in range(NTC):
                            ih = ((it * MB + mb) * MH + mh) * NTC + tc
                            fh = mb * MH + mh
                            hsl = slice(fh * TB + tc * 512, fh * TB + (tc + 1) * 512)
                            v.wait_ge(s_sil.h, ih + 1)
                            if two_mats:
                                v.tensor_mul(hsb[:, hsl], sil[:], ps_u[ih % 2][:])
                                i = v.tensor_copy(mjnk[:], sil[:, 0:4])
                            else:
                                i = v.tensor_copy(mjnk[:, 0:1], hsb[:, hsl][:, 0:1])
                            s_hev.inc(i, 1)
                for dmh in range(G):
                    for tc in range(NTC):
                        idt = (it * G + dmh) * NTC + tc
                        v.wait_ge(s_dmm.h, idt + 1)
                        v.wait_ge(s_cwb[it % 2].h, ((it // 2) + 1) * 16)
                        if idt >= 2:
                            v.wait_ge(s_acc.h, (idt - 1) * 16)
                        csl = cwbc[it % 2][:, tc * 512:(tc + 1) * 512]
                        if b2_hbm is not None:
                            v.scalar_tensor_tensor(
                                dtmp[idt % 2][:], ps_d[idt % 2][:],
                                b2c[:, e * G + dmh:e * G + dmh + 1], csl,
                                ALU.add, ALU.mult)
                        else:
                            v.tensor_mul(dtmp[idt % 2][:], ps_d[idt % 2][:], csl)
                        i = v.tensor_copy(mjnk[:], csl[:, 0:4])
                        s_dev.inc(i, 1)
    return s_acc


def emit_swiglu(P, xn_hbm, wg_hbm, wu_hbm, h_hbm, MF, T=TL, name="sw"):
    nc, blk = P.nc, P.blk
    MG = MF // 128
    nch = T // TCH
    s_w = P.sem(name)
    s_ld = SemPair(P, name)
    s_mm = P.sem(name)
    s_sil = P.sem(name)
    s_h = P.sem(name)
    s_st = P.sem(name)

    wgs = load_w_bf16(P, s_w, wg_hbm, 1024, MF, name + "wg")
    wus = load_w_bf16(P, s_w, wu_hbm, 1024, MF, name + "wu")
    xt = [P.sb([128, G * TCH], BF16, name + "x") for _ in range(2)]
    sil = P.sb([128, TCH], F32, name + "sil")
    jnk = P.sb([128, 4], F32, name + "jnk")
    ht = P.sb([128, MG * TCH], BF16, name + "h")
    ps_g = [P.ps([128, TCH], name + "pg") for _ in range(2)]
    ps_u = [P.ps([128, TCH], name + "pu") for _ in range(2)]

    @blk.sync
    def _(sync):
        for j in range(nch):
            if j >= 2:
                sync.wait_ge(s_mm.h, (j - 1) * MG)
            s_ld.inc(sync.dma_start(out=xt[j % 2][:],
                                    in_=dram_fm(xn_hbm, j * TCH, TCH, G)), j)
            if j >= 1:
                sync.wait_ge(s_h.h, j * MG)
                s_st.inc(sync.dma_start(out=dram_fm(h_hbm, (j - 1) * TCH, TCH, MG),
                                        in_=sb_fm(ht, TCH, MG)), 16)
        sync.wait_ge(s_h.h, nch * MG)
        s_st.inc(sync.dma_start(out=dram_fm(h_hbm, (nch - 1) * TCH, TCH, MG),
                                in_=sb_fm(ht, TCH, MG)), 16)
        sync.wait_ge(s_st.h, s_st.n)

    @blk.tensor
    def _(t):
        t.wait_ge(s_w.h, s_w.n)
        for j in range(nch):
            s_ld.wait(t, j)
            for mg in range(MG):
                imm = j * MG + mg
                if imm >= 2:
                    t.wait_ge(s_h.h, imm - 1)
                for kg in range(G):
                    t.matmul(ps_g[imm % 2][:],
                             wgs[:, (kg * MG + mg) * 128:(kg * MG + mg + 1) * 128],
                             xt[j % 2][:, kg * TCH:(kg + 1) * TCH],
                             start=(kg == 0), stop=(kg == G - 1))
                for kg in range(G):
                    i = t.matmul(ps_u[imm % 2][:],
                                 wus[:, (kg * MG + mg) * 128:(kg * MG + mg + 1) * 128],
                                 xt[j % 2][:, kg * TCH:(kg + 1) * TCH],
                                 start=(kg == 0), stop=(kg == G - 1))
                s_mm.inc(i, 1)

    @blk.scalar
    def _(s):
        for j in range(nch):
            for mg in range(MG):
                imm = j * MG + mg
                s.wait_ge(s_mm.h, imm + 1)
                if imm >= 1:
                    s.wait_ge(s_h.h, imm)
                i = s.activation(sil[:], ps_g[imm % 2][:], AF.Silu)
                s_sil.inc(i, 1)

    @blk.vector
    def _(v):
        for j in range(nch):
            if j >= 1:
                v.wait_ge(s_st.h, j * 16)
            for mg in range(MG):
                imm = j * MG + mg
                v.wait_ge(s_sil.h, imm + 1)
                sl = slice(mg * TCH, (mg + 1) * TCH)
                v.tensor_mul(ht[:, sl], sil[:], ps_u[imm % 2][:])
                i = v.tensor_copy(jnk[:], sil[:, 0:4])
                s_h.inc(i, 1)
    return s_st


def emit_attn(P, q_hbm, k_hbm, v_hbm, o_hbm, name="at"):
    nc, blk = P.nc, P.blk
    HD = 256
    HG = HD // 128
    NQB = L // 128
    sc = float(HD) ** -0.5
    P.const(sc)
    P.const(-1.0)
    s_ci = P.sem(name)
    s_id = P.sem(name)
    s_w = P.sem(name)
    s_vt = P.sem(name)
    s_vte = P.sem(name)
    s_S = P.sem(name)
    s_nm = P.sem(name)
    s_P = P.sem(name)
    s_fac = P.sem(name)
    s_pt = P.sem(name)
    s_pte = P.sem(name)
    s_pv = P.sem(name)
    s_ost = P.sem(name)
    s_fin = P.sem(name)
    s_otr = P.sem(name)
    s_oev = P.sem(name)
    s_go = P.sem(name)

    ident = emit_ident(P, blk, s_id, BF16, name + "id")
    qs = P.sb([128, 4 * L], BF16, name + "qs")
    ks = P.sb([128, HG * L], BF16, name + "ks")
    vs = P.sb([128, HG * L], BF16, name + "vs")
    vt = P.sb([128, 2 * L], BF16, name + "vt")
    masks = P.sb([128, 4 * 512], F32, name + "mask")
    zeros = P.sb([128, 512], F32, name + "z")
    Sm = P.sb([128, 512], F32, name + "smm")
    Pt = P.sb([128, 512], BF16, name + "ptm")
    ptt = P.sb([128, 512], BF16, name + "ptt")
    nmb = P.sb([128, 4], F32, name + "nmb")
    fac = P.sb([128, 1], F32, name + "fac")
    lrow = P.sb([128, 1], F32, name + "lr")
    lblk = P.sb([128, 1], F32, name + "lb")
    osb = P.sb([128, HD], F32, name + "osb")
    ofin = P.sb([128, HD], BF16, name + "ofin")
    ojnk = P.sb([128, 4], F32, name + "ojnk")
    ajnk = P.sb([128, 4], F32, name + "ajnk")
    oev = P.sb([128, HG * 128], BF16, name + "oevt")
    ps_s = [P.ps([128, 512], name + "pss") for _ in range(2)]
    ps_pt = P.psb([128, 1024], name + "pspt")
    ps_o = [P.ps([128, HD], name + "pso") for _ in range(2)]
    ps_ot = [P.psb([128, 1024], name + "psot") for _ in range(2)]

    def nkb(qb):
        return qb // 4 + 1

    tot_ikb = 2 * sum(nkb(qb) for qb in range(NQB))

    @blk.gpsimd
    def _(g):
        s_w.inc(g.dma_start(out=sb_fm(qs, L, 4), in_=dram_fm(q_hbm, 0, L, 4)), 16)
        s_w.inc(g.dma_start(out=sb_fm(ks, L, HG), in_=dram_fm(k_hbm, 0, L, HG)), 16)
        s_w.inc(g.dma_start(out=sb_fm(vs, L, HG), in_=dram_fm(v_hbm, 0, L, HG)), 16)
        g.memset(zeros[:], 0.0)
        for r in range(4):
            i = g.affine_select(masks[:, r * 512:(r + 1) * 512], zeros[:],
                                [[-1, 512]], ALU.is_ge, -1e30,
                                base=r * 128, channel_multiplier=1)
            s_ci.inc(i, 1)
        P.ensure_consts(g, s_ci)

    @blk.tensor
    def _(t):
        t.wait_ge(s_w.h, s_w.n)
        t.wait_ge(s_id.h, 2)
        for kb1 in range(L // 128):
            for vg in range(HG):
                ivt = kb1 * HG + vg
                if ivt >= 1:
                    t.wait_ge(s_vte.h, ivt)
                i = t.transpose(ps_pt[:, 0:128],
                                vs[:, vg * L + kb1 * 128:vg * L + (kb1 + 1) * 128],
                                ident[:])
                s_vt.inc(i, 1)
        ikb = 0
        for hh in range(2):
            for qb in range(NQB):
                iq = hh * NQB + qb
                for kb in range(nkb(qb)):
                    if ikb >= 2:
                        t.wait_ge(s_P.h, ikb - 1)
                    for hg in range(HG):
                        i = t.matmul(
                            ps_s[ikb % 2][:],
                            qs[:, (hh * HG + hg) * L + qb * 128:(hh * HG + hg) * L + (qb + 1) * 128],
                            ks[:, hg * L + kb * 512:hg * L + (kb + 1) * 512],
                            start=(hg == 0), stop=(hg == HG - 1))
                    s_S.inc(i, 1)
                    t.wait_ge(s_P.h, ikb + 1)
                    if ikb >= 1:
                        t.wait_ge(s_pte.h, ikb)
                    for sb_ in range(4):
                        i = t.transpose(ps_pt[:, sb_ * 128:(sb_ + 1) * 128],
                                        Pt[:, sb_ * 128:(sb_ + 1) * 128], ident[:])
                    s_pt.inc(i, 1)
                    t.wait_ge(s_pte.h, ikb + 1)
                    if ikb >= 2:
                        t.wait_ge(s_ost.h, ikb - 1)
                    for sb_ in range(4):
                        kbsub = kb * 4 + sb_
                        i = t.matmul(ps_o[ikb % 2][:],
                                     ptt[:, sb_ * 128:(sb_ + 1) * 128],
                                     vt[:, kbsub * HD:(kbsub + 1) * HD],
                                     start=(sb_ == 0), stop=(sb_ == 3))
                    s_pv.inc(i, 1)
                    ikb += 1
                t.wait_ge(s_fin.h, iq + 1)
                if iq >= 1:
                    t.wait_ge(s_oev.h, iq)
                for g_ in range(HG):
                    i = t.transpose(ps_ot[g_][:, 0:128],
                                    ofin[:, g_ * 128:(g_ + 1) * 128], ident[:])
                    s_otr.inc(i, 1)

    @blk.vector
    def _(v):
        v.wait_ge(s_ci.h, s_ci.n)
        for kb1 in range(L // 128):
            for vg in range(HG):
                ivt = kb1 * HG + vg
                v.wait_ge(s_vt.h, ivt + 1)
                i = v.tensor_copy(vt[:, (kb1 * HG + vg) * 128:(kb1 * HG + vg + 1) * 128],
                                  ps_pt[:, 0:128])
                s_vte.inc(i, 1)
        ikb = 0
        for hh in range(2):
            for qb in range(NQB):
                iq = hh * NQB + qb
                diag_kb = qb // 4
                for kb in range(nkb(qb)):
                    v.wait_ge(s_S.h, ikb + 1)
                    if kb == 0:
                        v.memset(nmb[:, 3:4], 1e30)
                        v.memset(osb[:], 0.0)
                        v.memset(lrow[:], 0.0)
                    else:
                        v.tensor_copy(nmb[:, 3:4], nmb[:, 2:3])
                        v.tensor_copy(ojnk[:], zeros[:, 0:4])
                    if kb == diag_kb:
                        r = qb % 4
                        v.tensor_add(Sm[:], ps_s[ikb % 2][:], masks[:, r * 512:(r + 1) * 512])
                        rsrc = Sm[:]
                        v.tensor_copy(ojnk[:], zeros[:, 0:4])
                    else:
                        rsrc = ps_s[ikb % 2][:]
                    v.reduce_max(nmb[:, 0:1], rsrc, axis=AX.X, negate=True)
                    v.tensor_copy(ojnk[:], zeros[:, 0:4])
                    v.tensor_scalar_mul(nmb[:, 1:2], nmb[:, 0:1], sc)
                    v.tensor_copy(ojnk[:], zeros[:, 0:4])
                    v.tensor_tensor(nmb[:, 2:3], nmb[:, 3:4], nmb[:, 1:2], ALU.min)
                    i = v.tensor_copy(ojnk[:], zeros[:, 0:4])
                    s_nm.inc(i, 1)
                    v.wait_ge(s_pt.h, ikb + 1)
                    if ikb >= 1:
                        v.wait_ge(s_pv.h, ikb)
                    v.tensor_copy(ptt[:], ps_pt[:, 0:512])
                    i = v.tensor_copy(ojnk[:], zeros[:, 0:4])
                    s_pte.inc(i, 1)
                    v.wait_ge(s_fac.h, ikb + 1)
                    v.scalar_tensor_tensor(lrow[:], lrow[:], fac[:, 0:1], lblk[:],
                                           ALU.mult, ALU.add)
                    v.wait_ge(s_pv.h, ikb + 1)
                    i = v.scalar_tensor_tensor(osb[:], osb[:], fac[:, 0:1],
                                               ps_o[ikb % 2][:], ALU.mult, ALU.add)
                    s_ost.inc(i, 1)
                    ikb += 1
                v.tensor_copy(ojnk[:], zeros[:, 0:4])
                v.reciprocal(nmb[:, 0:1], lrow[:])
                v.tensor_copy(ojnk[:], zeros[:, 0:4])
                v.tensor_scalar_mul(ofin[:], osb[:], nmb[:, 0:1])
                i = v.tensor_copy(ojnk[:], zeros[:, 0:4])
                s_fin.inc(i, 1)
                v.wait_ge(s_otr.h, (iq + 1) * 2)
                if iq >= 1:
                    v.wait_ge(s_go.h, iq * 16)
                for g_ in range(HG):
                    v.tensor_copy(oev[:, g_ * 128:(g_ + 1) * 128], ps_ot[g_][:, 0:128])
                i = v.tensor_copy(ojnk[:], zeros[:, 0:4])
                s_oev.inc(i, 1)

    @blk.scalar
    def _(s):
        s.wait_ge(s_ci.h, s_ci.n)
        ikb = 0
        for hh in range(2):
            for qb in range(NQB):
                diag_kb = qb // 4
                for kb in range(nkb(qb)):
                    s.wait_ge(s_nm.h, ikb + 1)
                    if ikb >= 1:
                        s.wait_ge(s_pt.h, ikb)
                    rsrc = Sm[:] if kb == diag_kb else ps_s[ikb % 2][:]
                    s.activation(Pt[:], rsrc, AF.Exp, scale=sc, bias=nmb[:, 2:3],
                                 accum_out=lblk[:])
                    s.activation(fac[:, 0:1], nmb[:, 3:4], AF.Exp, scale=-1.0,
                                 bias=nmb[:, 2:3])
                    i = s.copy(ajnk[:, 0:1], nmb[:, 0:1])
                    s_fac.inc(i, 1)
                    i = s.copy(ajnk[:, 1:2], nmb[:, 0:1])
                    s_P.inc(i, 1)
                    ikb += 1

    @blk.sync
    def _(sync):
        for hh in range(2):
            for qb in range(NQB):
                iq = hh * NQB + qb
                sync.wait_ge(s_oev.h, iq + 1)
                s_go.inc(sync.dma_start(
                    out=dram_fm(o_hbm, qb * 128, 128, HG, g0=hh * HG),
                    in_=sb_fm(oev, 128, HG)), 16)
        sync.wait_ge(s_go.h, s_go.n)
    return s_go


def emit_addstream(P, a_hbm, b_hbm, out_hbm, R, C, name="ad"):
    nc, blk = P.nc, P.blk
    CH = 2048
    nch = C // CH
    s_l = [P.sem(name), P.sem(name)]
    s_v = P.sem(name)
    s_o = P.sem(name)
    at = [P.sb([R, CH], F32, name + "a") for _ in range(2)]
    bt = [P.sb([R, CH], F32, name + "b") for _ in range(2)]
    ot = [P.sb([R, CH], F32, name + "o") for _ in range(2)]

    @blk.sync
    def _(sync):
        for j in range(nch):
            if j >= 2:
                sync.wait_ge(s_v.h, j - 1)
            s_l[j % 2].inc(sync.dma_start(out=at[j % 2][:],
                                          in_=_a(a_hbm)[:, j * CH:(j + 1) * CH]), 16)
            s_l[j % 2].inc(sync.dma_start(out=bt[j % 2][:],
                                          in_=_a(b_hbm)[:, j * CH:(j + 1) * CH]), 16)
            if j >= 1:
                sync.wait_ge(s_v.h, j)
                s_o.inc(sync.dma_start(out=_a(out_hbm)[:, (j - 1) * CH:j * CH],
                                       in_=ot[(j - 1) % 2][:]), 16)
        sync.wait_ge(s_v.h, nch)
        s_o.inc(sync.dma_start(out=_a(out_hbm)[:, (nch - 1) * CH:nch * CH],
                               in_=ot[(nch - 1) % 2][:]), 16)
        sync.wait_ge(s_o.h, s_o.n)

    @blk.vector
    def _(v):
        for j in range(nch):
            v.wait_ge(s_l[j % 2].h, ((j // 2) + 1) * 32)
            if j >= 2:
                v.wait_ge(s_o.h, (j - 1) * 16)
            i = v.tensor_add(ot[j % 2][:], at[j % 2][:], bt[j % 2][:])
            s_v.inc(i, 1)
    return s_o


def build_launch_D():
    P = Prog("D")
    nc = P.nc
    x1 = nc.dram_tensor("x1", [D, TL], F32, kind="ExternalInput").ap()
    nmw = nc.dram_tensor("nmw", [128, G], F32, kind="ExternalInput").ap()
    xn1b = nc.dram_tensor("xn1b", [D, TL], BF16, kind="ExternalOutput").ap()
    xn1f = nc.dram_tensor("xn1f", [D, TL], F32, kind="ExternalOutput").ap()
    xm1 = nc.dram_tensor("xm1", [D, TL], BF16, kind="ExternalOutput").ap()
    rows1 = nc.dram_tensor("rows1", [2, TL], F32, kind="Internal").ap()
    rows2 = nc.dram_tensor("rows2", [2, TL], F32, kind="Internal").ap()
    P.start_block()
    with ExitStack() as pctx:
        P.pctx = pctx
        emit_ln(P, x1, xn1b, rows1, xn_f32_hbm=xn1f, ncols=TL, name="ln1")
    P.barrier()
    with ExitStack() as pctx:
        P.pctx = pctx
        emit_ln(P, xn1f, xm1, rows2, ncols=TL, rms=True, eps=RMS_EPS,
                gamma_hbm=nmw, name="rm1")
    P.gctx.close()
    return nc


def build_launch_D2():
    P = Prog("D2")
    nc = P.nc
    xm1 = nc.dram_tensor("xm1", [D, L], BF16, kind="ExternalInput").ap()
    w1_in_x = nc.dram_tensor("w1_in_x", [D, DIH], F32, kind="ExternalInput").ap()
    w1_in_z = nc.dram_tensor("w1_in_z", [D, DIH], F32, kind="ExternalInput").ap()
    conv1_w = nc.dram_tensor("conv1_w", [128, GI * 4], F32, kind="ExternalInput").ap()
    conv1_b = nc.dram_tensor("conv1_b", [128, GI], F32, kind="ExternalInput").ap()
    w1_x = nc.dram_tensor("w1_x", [DIH, 33], F32, kind="ExternalInput").ap()
    u1 = nc.dram_tensor("u1", [DIH, L], F32, kind="ExternalOutput").ap()
    z1 = nc.dram_tensor("z1", [DIH, L], BF16, kind="ExternalOutput").ap()
    sp1 = nc.dram_tensor("sp1", [33, L], F32, kind="ExternalOutput").ap()
    P.start_block()
    with ExitStack() as pctx:
        P.pctx = pctx
        emit_mamba_front(P, xm1, w1_in_x, w1_in_z, conv1_w, conv1_b, w1_x,
                         u1, z1, sp1, Rsp=33, name="mf1")
    P.gctx.close()
    return nc


def build_launch_SPADD(Rsp):
    P = Prog("SP")
    nc = P.nc
    spa = nc.dram_tensor("spa", [Rsp, L], F32, kind="ExternalInput").ap()
    spb = nc.dram_tensor("spb", [Rsp, L], F32, kind="ExternalInput").ap()
    spo = nc.dram_tensor("spo", [Rsp, L], F32, kind="ExternalOutput").ap()
    P.start_block()
    with ExitStack() as pctx:
        P.pctx = pctx
        emit_addstream(P, spa, spb, spo, Rsp, L, name="sa")
    P.gctx.close()
    return nc


def build_launch_F():
    P = Prog("F")
    nc = P.nc
    g1 = nc.dram_tensor("g1", [DI, TL], BF16, kind="ExternalInput").ap()
    wop = nc.dram_tensor("wop", [DI, D], F32, kind="ExternalInput").ap()
    xn1f = nc.dram_tensor("xn1f", [D, TL], F32, kind="ExternalInput").ap()
    naw = nc.dram_tensor("naw", [128, G], F32, kind="ExternalInput").ap()
    m = nc.dram_tensor("m", [D, TL], F32, kind="ExternalOutput").ap()
    xn2 = nc.dram_tensor("xn2", [D, TL], BF16, kind="ExternalOutput").ap()
    rows = nc.dram_tensor("rows", [2, TL], F32, kind="Internal").ap()
    P.start_block()
    with ExitStack() as pctx:
        P.pctx = pctx
        emit_proj(P, g1, wop, m, DI, D, TL, res_hbm=xn1f, name="op1")
    P.barrier()
    with ExitStack() as pctx:
        P.pctx = pctx
        emit_ln(P, m, xn2, rows, ncols=TL, rms=True, eps=RMS_EPS,
                gamma_hbm=naw, name="rm2")
    P.gctx.close()
    return nc


def build_launch_F2():
    P = Prog("F2")
    nc = P.nc
    xn2 = nc.dram_tensor("xn2", [D, L], BF16, kind="ExternalInput").ap()
    qkvw = nc.dram_tensor("qkvw", [D, 1024], F32, kind="ExternalInput").ap()
    qkv = nc.dram_tensor("qkv", [1024, L], BF16, kind="ExternalOutput").ap()
    P.start_block()
    with ExitStack() as pctx:
        P.pctx = pctx
        emit_proj(P, xn2, qkvw, qkv, D, 1024, L, out_f32=False, name="qkv")
    P.gctx.close()
    return nc


def build_launch_G():
    P = Prog("G")
    nc = P.nc
    qkv = nc.dram_tensor("qkv", [1024, L], BF16, kind="ExternalInput").ap()
    o = nc.dram_tensor("o", [512, L], BF16, kind="ExternalOutput").ap()
    P.start_block()
    with ExitStack() as pctx:
        P.pctx = pctx
        emit_attn(P, qkv[0:512, :], qkv[512:768, :], qkv[768:1024, :], o, name="at")
    P.gctx.close()
    return nc


def build_launch_H(dump=False):
    P = Prog("H")
    nc = P.nc
    of = nc.dram_tensor("of", [D, TL], BF16, kind="ExternalInput").ap()
    wo2 = nc.dram_tensor("wo2", [D, D], F32, kind="ExternalInput").ap()
    m = nc.dram_tensor("m", [D, TL], F32, kind="ExternalInput").ap()
    nfw = nc.dram_tensor("nfw", [128, G], F32, kind="ExternalInput").ap()
    gwp = nc.dram_tensor("gwp", [D, 2816], F32, kind="ExternalInput").ap()
    uwp = nc.dram_tensor("uwp", [D, 2816], F32, kind="ExternalInput").ap()
    dwp = nc.dram_tensor("dwp", [2816, D], F32, kind="ExternalInput").ap()
    x1 = nc.dram_tensor("x1", [D, TL], F32, kind="ExternalInput").ap()
    xn1b = nc.dram_tensor("xn1b", [D, TL], BF16, kind="ExternalInput").ap()
    xn1f = nc.dram_tensor("xn1f", [D, TL], F32, kind="ExternalInput").ap()
    lng = nc.dram_tensor("lng", [128, G], F32, kind="ExternalInput").ap()
    lnb = nc.dram_tensor("lnb", [128, G], F32, kind="ExternalInput").ap()
    wr1 = nc.dram_tensor("wr1", [D, 8], F32, kind="ExternalInput").ap()
    br1 = nc.dram_tensor("br1", [1, 8], F32, kind="ExternalInput").ap()
    we1 = nc.dram_tensor("we1", [8, D, 4096], F32, kind="ExternalInput").ap()
    we2 = nc.dram_tensor("we2", [8, 4096, D], F32, kind="ExternalInput").ap()
    b1 = nc.dram_tensor("b1", [128, 8 * 32], F32, kind="ExternalInput").ap()
    b2 = nc.dram_tensor("b2", [128, 8 * 8], F32, kind="ExternalInput").ap()
    x2 = nc.dram_tensor("x2", [D, TL], F32, kind="ExternalOutput").ap()
    kd = "ExternalOutput" if dump else "Internal"
    aout = nc.dram_tensor("aout", [D, TL], F32, kind=kd).ap()
    xn3 = nc.dram_tensor("xn3", [D, TL], BF16, kind=kd).ap()
    hsw = nc.dram_tensor("hsw", [2816, TL], BF16, kind=kd).ap()
    xnln = nc.dram_tensor("xnln", [D, TL], BF16, kind=kd).ap()
    mg = nc.dram_tensor("mg", [8, TL], F32, kind=kd).ap()
    rows = nc.dram_tensor("rows", [2, TL], F32, kind="Internal").ap()
    rows2 = nc.dram_tensor("rows2", [2, TL], F32, kind="Internal").ap()
    P.start_block()
    with ExitStack() as pctx:
        P.pctx = pctx
        emit_proj(P, of, wo2, aout, D, D, TL, res_hbm=m, name="op2")
    P.barrier()
    with ExitStack() as pctx:
        P.pctx = pctx
        emit_ln(P, aout, xn3, rows, ncols=TL, rms=True, eps=RMS_EPS,
                gamma_hbm=nfw, name="rm3")
    P.barrier()
    with ExitStack() as pctx:
        P.pctx = pctx
        emit_swiglu(P, xn3, gwp, uwp, hsw, 2816, T=TL, name="sw")
    P.barrier()
    with ExitStack() as pctx:
        P.pctx = pctx
        emit_proj(P, hsw, dwp, x2, 2816, D, TL, res_hbm=aout, res2_hbm=x1, name="dn")
    P.barrier()
    with ExitStack() as pctx:
        P.pctx = pctx
        emit_ln(P, xn1f, xnln, rows2, ncols=TL, gamma_hbm=lng, beta_hbm=lnb,
                eps=1e-5, name="iln")
    P.barrier()
    with ExitStack() as pctx:
        P.pctx = pctx
        emit_router(P, xn1b, wr1, None, br1, mg, "l1", T=TL, name="rt1")
    P.barrier()
    with ExitStack() as pctx:
        P.pctx = pctx
        emit_moe(P, xnln, we1, None, we2, mg, x2, two_mats=False,
                 b1_hbm=b1, b2_hbm=b2, T=TL, name="moe1")
    P.gctx.close()
    return nc

def build_launch_C1(layer=0):
    P = Prog("C1")
    nc = P.nc
    gf = nc.dram_tensor("gf", [DI, TL], BF16, kind="ExternalInput").ap()
    wo = nc.dram_tensor("wo", [DI, D], F32, kind="ExternalInput").ap()
    res = nc.dram_tensor("res", [D, TL], F32, kind="ExternalInput").ap()
    x1 = nc.dram_tensor("x1", [D, TL], F32, kind="ExternalOutput").ap()
    P.start_block()
    with ExitStack() as pctx:
        P.pctx = pctx
        emit_proj(P, gf, wo, x1, DI, D, TL, res_hbm=res, name="wo")
    P.gctx.close()
    return nc

def pcol(a):
    a = np.asarray(a)
    if a.ndim == 1:
        a = a[:, None]
    n = a.shape[0] // 128
    return np.ascontiguousarray(a.reshape(n, 128, -1).transpose(1, 0, 2).reshape(128, -1))


def prep_inputs(inp):
    per = []
    A = lambda k: np.asarray(inp[k])
    xT = np.ascontiguousarray(A("L0_attn_hidden_states").transpose(0, 2, 1))
    w_in = A("L0_attn_W_in")
    conv0 = A("L0_attn_conv_w")
    w_x0 = A("L0_attn_W_x")
    in1T = np.ascontiguousarray(A("L1_attn_in_proj_w").T)
    conv1 = A("L1_attn_conv_w")[:, 0, :]
    xp1T = np.ascontiguousarray(A("L1_attn_x_proj_w").T)
    op1T = np.ascontiguousarray(A("L1_attn_out_proj_w").T)
    qT = np.ascontiguousarray(A("L1_attn_q_w").T)
    kT = np.ascontiguousarray(A("L1_attn_k_w").T)
    vT = np.ascontiguousarray(A("L1_attn_v_w").T)
    oT = np.ascontiguousarray(A("L1_attn_o_w").T)
    gwT = np.ascontiguousarray(A("L1_attn_gate_w").T)
    uwT = np.ascontiguousarray(A("L1_attn_up_w").T)
    dwT = np.ascontiguousarray(A("L1_attn_down_w").T)
    gwp = np.pad(gwT, ((0, 0), (0, 2816 - 2731)))
    uwp = np.pad(uwT, ((0, 0), (0, 2816 - 2731)))
    dwp = np.pad(dwT, ((0, 0), (0, 0))) if dwT.shape[0] == 2816 else \
        np.pad(dwT, ((0, 2816 - 2731), (0, 0)))
    b1all = np.concatenate([pcol(A("L1_ffn_b1")[e]) for e in range(8)], axis=1)
    b2all = np.concatenate([pcol(A("L1_ffn_b2")[e]) for e in range(8)], axis=1)
    for c in range(NCORES):
        b, q = divmod(c, 2)
        sl = slice(q * DIH, (q + 1) * DIH)
        d = {
            "x0": np.ascontiguousarray(xT[b]),
            "w0_in_x": np.ascontiguousarray(w_in[:, q * DIH:(q + 1) * DIH]),
            "w0_in_z": np.ascontiguousarray(w_in[:, DI + q * DIH:DI + (q + 1) * DIH]),
            "conv0_w": pcol(conv0[:, sl].T),
            "conv0_b": pcol(A("L0_attn_conv_b")[sl]),
            "w0_x": np.ascontiguousarray(w_x0[sl, :]),
            "wdt0": np.ascontiguousarray(A("L0_attn_W_dt")[:, sl]),
            "bdt0": pcol(A("L0_attn_b_dt")[sl]),
            "alog0": pcol(A("L0_attn_A_log")[sl]),
            "dvec0": pcol(A("L0_attn_D")[sl]),
            "wo0": A("L0_attn_W_out"),
            "wr0": A("L0_ffn_w_router"),
            "wg0": A("L0_ffn_w_gating"),
            "weg0": A("L0_ffn_w_gate"),
            "weu0": A("L0_ffn_w_up"),
            "wed0": A("L0_ffn_w_down"),
            "nmw": pcol(A("L1_attn_norm_m_w")),
            "w1_in_x": np.ascontiguousarray(in1T[:, q * DIH:(q + 1) * DIH]),
            "w1_in_z": np.ascontiguousarray(in1T[:, DI + q * DIH:DI + (q + 1) * DIH]),
            "conv1_w": pcol(conv1[sl, :]),
            "conv1_b": pcol(A("L1_attn_conv_b")[sl]),
            "w1_x": np.ascontiguousarray(xp1T[sl, :]),
            "dtw1": pcol(A("L1_attn_dt_w")[sl, 0]),
            "bdt1": pcol(A("L1_attn_dt_b")[sl]),
            "alog1": pcol(A("L1_attn_A_log")[sl]),
            "dvec1": pcol(A("L1_attn_D_skip")[sl]),
            "wop": op1T,
            "naw": pcol(A("L1_attn_norm_attn_w")),
            "qkvw": np.ascontiguousarray(np.concatenate(
                [qT[:, q * 512:(q + 1) * 512],
                 kT[:, q * 256:(q + 1) * 256],
                 vT[:, q * 256:(q + 1) * 256]], axis=1)),
            "wo2": oT,
            "nfw": pcol(A("L1_attn_norm_ffn_w")),
            "gwp": gwp, "uwp": uwp, "dwp": dwp,
            "lng": pcol(A("L1_ffn_ln_g")),
            "lnb": pcol(A("L1_ffn_ln_b")),
            "wr1": A("L1_ffn_Wr"),
            "br1": A("L1_ffn_br").reshape(1, 8),
            "we1": A("L1_ffn_W1"),
            "we2": A("L1_ffn_W2"),
            "b1": b1all, "b2": b2all,
        }
        per.append(d)
    return per


def build_launch_A(only_ln=False):
    P = Prog("A")
    nc = P.nc
    x0 = nc.dram_tensor("x0", [D, L], F32, kind="ExternalInput").ap()
    w0_in_x = nc.dram_tensor("w0_in_x", [D, DIH], F32, kind="ExternalInput").ap()
    w0_in_z = nc.dram_tensor("w0_in_z", [D, DIH], F32, kind="ExternalInput").ap()
    conv0_w = nc.dram_tensor("conv0_w", [128, GI * 4], F32, kind="ExternalInput").ap()
    conv0_b = nc.dram_tensor("conv0_b", [128, GI], F32, kind="ExternalInput").ap()
    w0_x = nc.dram_tensor("w0_x", [DIH, 96], F32, kind="ExternalInput").ap()
    xn0 = nc.dram_tensor("xn0", [D, L], BF16, kind="ExternalOutput").ap()
    u0 = nc.dram_tensor("u0", [DIH, L], F32, kind="ExternalOutput").ap()
    z0 = nc.dram_tensor("z0", [DIH, L], BF16, kind="ExternalOutput").ap()
    sp0 = nc.dram_tensor("sp0", [96, L], F32, kind="ExternalOutput").ap()

    rows0 = nc.dram_tensor("rows0", [2, L], F32, kind="Internal").ap()
    P.start_block()
    with ExitStack() as pctx:
        P.pctx = pctx
        emit_ln(P, x0, xn0, rows0, ncols=L, name="ln0")
    if not only_ln:
        P.barrier()
        with ExitStack() as pctx:
            P.pctx = pctx
            emit_mamba_front(P, xn0, w0_in_x, w0_in_z, conv0_w, conv0_b, w0_x,
                             u0, z0, sp0, Rsp=96, name="mf0")
    P.gctx.close()
    return nc


def build_launch_B(layer):
    P = Prog("B")
    nc = P.nc
    u = nc.dram_tensor("u", [DIH, L], F32, kind="ExternalInput").ap()
    z = nc.dram_tensor("z", [DIH, L], BF16, kind="ExternalInput").ap()
    Rsp = 96 if layer == 0 else 33
    sp = nc.dram_tensor("sp", [Rsp, L], F32, kind="ExternalInput").ap()
    alog = nc.dram_tensor("alog", [128, GI * NST], F32, kind="ExternalInput").ap()
    bdt = nc.dram_tensor("bdt", [128, GI], F32, kind="ExternalInput").ap()
    dvec = nc.dram_tensor("dvec", [128, GI], F32, kind="ExternalInput").ap()
    if layer == 0:
        wdt = nc.dram_tensor("wdt", [64, DIH], F32, kind="ExternalInput").ap()
        dtw = None
    else:
        dtw = nc.dram_tensor("dtw", [128, GI], F32, kind="ExternalInput").ap()
        wdt = None
    gout = nc.dram_tensor("gout", [DIH, L], BF16, kind="ExternalOutput").ap()

    P.start_block()
    with ExitStack() as pctx:
        P.pctx = pctx
        emit_scan(P, u, z, sp, gout, wdt, alog, bdt, dvec, dtw, layer, name="sc")
    P.gctx.close()
    return nc


def build_launch_C(layer=0, parts="rpm", cw_input=False):
    P = Prog("C")
    nc = P.nc
    xn = nc.dram_tensor("xn", [D, TL], BF16, kind="ExternalInput").ap()
    gf = nc.dram_tensor("gf", [DI, TL], BF16, kind="ExternalInput").ap()
    wo = nc.dram_tensor("wo", [DI, D], F32, kind="ExternalInput").ap()
    res = nc.dram_tensor("res", [D, TL], F32, kind="ExternalInput").ap()
    wr = nc.dram_tensor("wr", [D, 8], F32, kind="ExternalInput").ap()
    wg2 = nc.dram_tensor("wg2", [D, 8], F32, kind="ExternalInput").ap()
    weg = nc.dram_tensor("weg", [8, D, 4096], F32, kind="ExternalInput").ap()
    weu = nc.dram_tensor("weu", [8, D, 4096], F32, kind="ExternalInput").ap()
    wed = nc.dram_tensor("wed", [8, 4096, D], F32, kind="ExternalInput").ap()
    x1 = nc.dram_tensor("x1", [D, TL], F32, kind="ExternalOutput").ap()
    cw = nc.dram_tensor("cw", [8, TL], F32,
                        kind="ExternalInput" if cw_input else "Internal").ap()

    P.start_block()
    if "r" in parts:
        with ExitStack() as pctx:
            P.pctx = pctx
            emit_router(P, xn, wr, wg2, None, cw, "l0", T=TL, name="rt")
        P.barrier()
    if "p" in parts:
        with ExitStack() as pctx:
            P.pctx = pctx
            emit_proj(P, gf, wo, x1, DI, D, TL, res_hbm=res, name="wo")
        P.barrier()
    if "m" in parts:
        with ExitStack() as pctx:
            P.pctx = pctx
            emit_moe(P, xn, weg, weu, wed, cw, x1, two_mats=True, T=TL, name="moe")
    P.gctx.close()
    return nc


def _np(x):
    return np.asarray(x)


def kernel(**inputs):
    per = prep_inputs(inputs)
    C8 = list(range(NCORES))

    def run(nc, names, extra):
        im = [{**{k: per[c][k] for k in names}, **extra[c]} for c in range(NCORES)]
        return run_bass_kernel_spmd(nc, im, core_ids=C8).results

    rA = run(build_launch_A(),
             ["x0", "w0_in_x", "w0_in_z", "conv0_w", "conv0_b", "w0_x"],
             [{} for _ in C8])
    ncsp = build_launch_SPADD(96)
    rSP = run_bass_kernel_spmd(ncsp, [
        {"spa": _np(rA[c]["sp0"]), "spb": _np(rA[c ^ 1]["sp0"])} for c in C8],
        core_ids=C8).results
    rB = run(build_launch_B(0), [],
             [{"u": _np(rA[c]["u0"]), "z": _np(rA[c]["z0"]),
               "sp": _np(rSP[c]["spo"]),
               "wdt": per[c]["wdt0"], "bdt": per[c]["bdt0"],
               "alog": per[c]["alog0"], "dvec": per[c]["dvec0"]} for c in C8])
    extraC = []
    for c in C8:
        b, q = divmod(c, 2)
        gfull = np.concatenate([_np(rB[2 * b]["gout"]), _np(rB[2 * b + 1]["gout"])],
                               axis=0)[:, q * TL:(q + 1) * TL]
        extraC.append({
            "xn": np.ascontiguousarray(_np(rA[c]["xn0"])[:, q * TL:(q + 1) * TL]),
            "gf": np.ascontiguousarray(gfull),
            "wo": per[c]["wo0"],
            "res": np.ascontiguousarray(per[c]["x0"][:, q * TL:(q + 1) * TL]),
            "wr": per[c]["wr0"], "wg2": per[c]["wg0"],
            "weg": per[c]["weg0"], "weu": per[c]["weu0"], "wed": per[c]["wed0"],
        })
    rC = run(build_launch_C(0), [], extraC)
    rD = run(build_launch_D(), ["nmw"],
             [{"x1": _np(rC[c]["x1"])} for c in C8])
    extraD2 = []
    for c in C8:
        b, q = divmod(c, 2)
        xm_full = np.concatenate([_np(rD[2 * b]["xm1"]), _np(rD[2 * b + 1]["xm1"])],
                                 axis=1)
        extraD2.append({"xm1": np.ascontiguousarray(xm_full)})
    rD2 = run(build_launch_D2(),
              ["w1_in_x", "w1_in_z", "conv1_w", "conv1_b", "w1_x"], extraD2)
    ncsp1 = build_launch_SPADD(33)
    rSP1 = run_bass_kernel_spmd(ncsp1, [
        {"spa": _np(rD2[c]["sp1"]), "spb": _np(rD2[c ^ 1]["sp1"])} for c in C8],
        core_ids=C8).results
    rB1 = run(build_launch_B(1), [],
              [{"u": _np(rD2[c]["u1"]), "z": _np(rD2[c]["z1"]),
                "sp": _np(rSP1[c]["spo"]),
                "dtw": per[c]["dtw1"], "bdt": per[c]["bdt1"],
                "alog": per[c]["alog1"], "dvec": per[c]["dvec1"]} for c in C8])
    extraF = []
    for c in C8:
        b, q = divmod(c, 2)
        g1full = np.concatenate([_np(rB1[2 * b]["gout"]), _np(rB1[2 * b + 1]["gout"])],
                                axis=0)[:, q * TL:(q + 1) * TL]
        extraF.append({"g1": np.ascontiguousarray(g1full),
                       "xn1f": _np(rD[c]["xn1f"])})
    rF = run(build_launch_F(), ["wop", "naw"], extraF)
    extraF2 = []
    for c in C8:
        b, q = divmod(c, 2)
        xn2full = np.concatenate([_np(rF[2 * b]["xn2"]), _np(rF[2 * b + 1]["xn2"])],
                                 axis=1)
        extraF2.append({"xn2": np.ascontiguousarray(xn2full)})
    rF2 = run(build_launch_F2(), ["qkvw"], extraF2)
    rG = run(build_launch_G(), [], [{"qkv": _np(rF2[c]["qkv"])} for c in C8])
    extraH = []
    for c in C8:
        b, q = divmod(c, 2)
        ofull = np.concatenate([_np(rG[2 * b]["o"]), _np(rG[2 * b + 1]["o"])],
                               axis=0)[:, q * TL:(q + 1) * TL]
        extraH.append({"of": np.ascontiguousarray(ofull),
                       "m": _np(rF[c]["m"]),
                       "x1": _np(rC[c]["x1"]),
                       "xn1b": _np(rD[c]["xn1b"]),
                       "xn1f": _np(rD[c]["xn1f"])})
    rH = run(build_launch_H(),
             ["wo2", "nfw", "gwp", "uwp", "dwp", "lng", "lnb", "wr1", "br1",
              "we1", "we2", "b1", "b2"], extraH)
    out = np.zeros((BATCH, L, D), np.float32)
    for c in C8:
        b, q = divmod(c, 2)
        out[b, q * TL:(q + 1) * TL, :] = _np(rH[c]["x2"]).T
    return out


# version 02 (final)
# speedup vs baseline: 1.0029x; 1.0029x over previous
import numpy as np
from contextlib import ExitStack

import concourse.bass as bass
import concourse.mybir as mybir
from concourse.bass_utils import run_bass_kernel_spmd

F32 = mybir.dt.float32
BF16 = mybir.dt.bfloat16
I32 = mybir.dt.int32
AF = mybir.ActivationFunctionType
ALU = mybir.AluOpType
AX = mybir.AxisListType

NCORES = 8
D = 1024
L = 8192
BATCH = 4
G = D // 128
TL = L // 2
DI = 2048
DIH = DI // 2
GI = DIH // 128
NST = 16
TCH = 512
NCH_F = L // TCH
NCH_H = TL // TCH
LN_EPS = 1e-5
RMS_EPS = 1e-6


def _a(x):
    return x if isinstance(x, bass.AP) else x.ap()


def dram_fm(x_hbm, col0, ncols, ngroups, g0=0):
    ap = _a(x_hbm)
    return ap[g0 * 128:(g0 + ngroups) * 128, col0:col0 + ncols].rearrange(
        "(g p) t -> p g t", p=128)


def dram_vec(v_hbm, ngroups, g0=0):
    ap = _a(v_hbm)
    return ap[g0 * 128:(g0 + ngroups) * 128].rearrange("(g p) -> p g", p=128)


def sb_fm(tile, ncols, ngroups, col0=0):
    return _a(tile)[:, col0:col0 + ngroups * ncols].rearrange(
        "p (g t) -> p g t", g=ngroups)


class SemPair:
    def __init__(self, P, name):
        self.s = [P.sem(name + "a"), P.sem(name + "b")]

    def inc(self, inst, j):
        self.s[j % 2].inc(inst, 16)

    def wait(self, eng, j):
        eng.wait_ge(self.s[j % 2].h, (j // 2 + 1) * 16)


class SemRef:
    def __init__(self, raw, base):
        self.raw = raw
        self.base = base


class Sem:
    def __init__(self, ent):
        self._ent = ent
        self.h = SemRef(ent[0], ent[1])
        self.n = 0

    @property
    def raw(self):
        return self._ent[0]

    @property
    def base(self):
        return self.h.base

    def inc(self, inst, k):
        inst.then_inc(self._ent[0], k)
        self.n += k
        return self.n


class EngProxy:
    def __init__(self, eng):
        self._e = eng

    def __getattr__(self, name):
        return getattr(self._e, name)

    def wait_ge(self, h, v, **kw):
        if isinstance(h, SemRef):
            return self._e.wait_ge(h.raw, v + h.base, **kw)
        return self._e.wait_ge(h, v, **kw)

    def sem_inc(self, h, v):
        if isinstance(h, SemRef):
            return self._e.sem_inc(h.raw, v)
        return self._e.sem_inc(h, v)


class BlkProxy:
    def __init__(self, blk):
        self._blk = blk

    def __getattr__(self, name):
        real = getattr(self._blk, name)

        def deco(fn):
            return real(lambda e: fn(EngProxy(e)))
        return deco


class Prog:
    def __init__(self, name="mk"):
        self.nc = bass.Bass("TRN2", target_bir_lowering=False, detect_race_conditions=False)
        self.gctx = ExitStack()
        self.pctx = None
        self.blk = None
        self._n = 0
        self._consts = {}
        self._free = []
        self._phase_sems = []
        self._bar = None
        self._bar_n = 0

    def start_block(self):
        self.blk = BlkProxy(self.gctx.enter_context(self.nc.Block()))
        return self.blk

    def sem(self, name, pooled=True):
        if pooled and self._free:
            ent = self._free.pop()
        else:
            self._n += 1
            ent = [self.gctx.enter_context(self.nc.semaphore(f"{name}{self._n}")), 0]
        s = Sem(ent)
        if pooled:
            self._phase_sems.append((ent, s))
        return s

    def end_phase(self):
        for ent, s in self._phase_sems:
            ent[1] += s.n
            self._free.append(ent)
        self._phase_sems = []
        self.barrier()

    def gsb(self, shape, dtype=F32, name="g"):
        self._n += 1
        return self.gctx.enter_context(self.nc.sbuf_tensor(f"{name}{self._n}", shape, dtype))

    def sb(self, shape, dtype=F32, name="t"):
        self._n += 1
        return self.pctx.enter_context(self.nc.sbuf_tensor(f"{name}{self._n}", shape, dtype))

    def ps(self, shape, name="p"):
        self._n += 1
        cols = max(shape[1], 512)
        t = self.pctx.enter_context(self.nc.psum_tensor(f"{name}{self._n}",
                                                        [shape[0], cols], F32))
        return t.ap()[:, 0:shape[1]] if cols != shape[1] else t

    def psb(self, shape, name="pb"):
        self._n += 1
        cols = max(shape[1], 1024)
        t = self.pctx.enter_context(self.nc.psum_tensor(f"{name}{self._n}",
                                                        [shape[0], cols], BF16))
        return t.ap()[:, 0:shape[1]] if cols != shape[1] else t

    def const(self, val):
        if val in (0.0, 1.0):
            return val
        if val not in self._consts:
            tile = self.gsb([128, 1], F32, "c")
            self._consts[val] = tile
            self.nc.const_aps.register(F32, float(val), tile[:, 0:1])
        return float(val)

    def ensure_consts(self, eng, sem):
        if not hasattr(self, "_consts_done"):
            self._consts_done = set()
        for val, tile in self._consts.items():
            if val not in self._consts_done:
                sem.inc(eng.memset(tile[:, 0:1], float(val)), 1)
                self._consts_done.add(val)

    def barrier(self, name="bar"):
        if self._bar is None:
            self._bar = self.sem(name, pooled=False)
        engines = ["sync", "vector", "scalar", "gpsimd", "tensor"]
        for e in engines:
            getattr(self.blk, e)(lambda eng: eng.sem_inc(self._bar.raw, 1))
        self._bar_n += len(engines)
        tgt = self._bar_n
        for e in engines:
            getattr(self.blk, e)(lambda eng: eng.wait_ge(self._bar.raw, tgt))
        return self._bar



def emit_ln(P, x_hbm, xn_bf_hbm, rows_hbm, xn_f32_hbm=None, ncols=L, in_bf16=False,
            gamma_hbm=None, beta_hbm=None, rms=False, eps=LN_EPS, name="ln"):
    nc, blk = P.nc, P.blk
    nch = ncols // TCH
    s_ld = SemPair(P, name)
    s_w = P.sem(name)
    s_ini = P.sem(name)
    s_sq = P.sem(name)
    s_mm = P.sem(name)
    s_row = P.sem(name)
    s_rw = P.sem(name)
    s_bc = P.sem(name)
    s_ap = P.sem(name)
    s_st = P.sem(name)
    nst_per = 32 if xn_f32_hbm is not None else 16

    xt = [P.sb([128, G * TCH], BF16 if in_bf16 else F32, name + "x") for _ in range(2)]
    xsq = P.sb([128, G * TCH], F32, name + "sq")
    ones_col = P.sb([128, 1], F32, name + "oc")
    rowbuf = P.sb([1, 4 * TCH], F32, name + "rows")
    mean = rowbuf[:, 0:TCH]
    rstd = rowbuf[:, TCH:2 * TCH]
    msq = rowbuf[:, 2 * TCH:3 * TCH]
    var = rowbuf[:, 3 * TCH:4 * TCH]
    bc_t = P.sb([128, 2 * TCH], F32, name + "bc")
    t0 = P.sb([128, TCH], F32, name + "t0")
    xnb = P.sb([128, G * TCH], BF16, name + "nb")
    xnf = P.sb([128, G * TCH], F32, name + "nf") if xn_f32_hbm is not None else None
    gam = P.sb([128, G], F32, name + "gam") if gamma_hbm is not None else None
    bet = P.sb([128, G], F32, name + "bet") if beta_hbm is not None else None
    ps_sum = P.ps([1, TCH], name + "psum")
    ps_ssq = P.ps([1, TCH], name + "pssq")

    @blk.vector
    def _(v):
        s_ini.inc(v.memset(ones_col[:], 1.0), 1)

    @blk.gpsimd
    def _(g):
        if gam is not None:
            s_w.inc(g.dma_start(out=gam[:], in_=_a(gamma_hbm)), 16)
        if bet is not None:
            s_w.inc(g.dma_start(out=bet[:], in_=_a(beta_hbm)), 16)

    def emit_store(sync, j):
        sync.wait_ge(s_ap.h, j + 1)
        s_st.inc(sync.dma_start(out=dram_fm(xn_bf_hbm, j * TCH, TCH, G),
                                in_=sb_fm(xnb, TCH, G)), 16)
        if xn_f32_hbm is not None:
            s_st.inc(sync.dma_start(out=dram_fm(xn_f32_hbm, j * TCH, TCH, G),
                                    in_=sb_fm(xnf, TCH, G)), 16)

    @blk.sync
    def _(sync):
        for j in range(nch):
            if j >= 2:
                sync.wait_ge(s_ap.h, j - 1)
            s_ld.inc(sync.dma_start(out=xt[j % 2][:],
                                    in_=dram_fm(x_hbm, j * TCH, TCH, G)), j)
            if j >= 1:
                emit_store(sync, j - 1)
        emit_store(sync, nch - 1)
        sync.wait_ge(s_st.h, s_st.n)

    @blk.scalar
    def _(s):
        for j in range(nch):
            s_ld.wait(s, j)
            if j >= 1:
                s.wait_ge(s_mm.h, j)
            s_sq.inc(s.activation(xsq[:], xt[j % 2][:], AF.Square), 1)
            s.wait_ge(s_row.h, 2 * j + 1)
            s_row.inc(s.activation(rstd, var, AF.Sqrt), 1)

    @blk.tensor
    def _(t):
        t.wait_ge(s_ini.h, 1)
        for j in range(nch):
            t.wait_ge(s_sq.h, j + 1)
            if j >= 1:
                t.wait_ge(s_row.h, 2 * j)
            if not rms:
                for g_ in range(G):
                    t.matmul(ps_sum[:], ones_col[:], xt[j % 2][:, g_ * TCH:(g_ + 1) * TCH],
                             start=(g_ == 0), stop=(g_ == G - 1))
            for g_ in range(G):
                i = t.matmul(ps_ssq[:], ones_col[:], xsq[:, g_ * TCH:(g_ + 1) * TCH],
                             start=(g_ == 0), stop=(g_ == G - 1))
            s_mm.inc(i, 1)

    @blk.vector
    def _(v):
        c_inv = 1.0 / D
        for j in range(nch):
            v.wait_ge(s_mm.h, j + 1)
            if j >= 1:
                v.wait_ge(s_rw.h, 32 * j)
            if not rms:
                v.tensor_scalar_mul(mean, ps_sum[:], c_inv)
                v.tensor_scalar(var, ps_ssq[:], c_inv, eps, ALU.mult, ALU.add)
                v.tensor_mul(msq, mean, mean)
                v.tensor_sub(var, var, msq)
            else:
                v.tensor_scalar(var, ps_ssq[:], c_inv, eps, ALU.mult, ALU.add)
                v.memset(msq, 0.0)
            i = v.reciprocal(var, var)
            s_row.inc(i, 1)
            v.wait_ge(s_bc.h, (j + 1) * 16)
            if gam is not None:
                v.wait_ge(s_w.h, s_w.n)
            if j >= 1:
                v.wait_ge(s_st.h, j * nst_per)
            m_bc = bc_t[:, 0:TCH]
            r_bc = bc_t[:, TCH:2 * TCH]
            for g_ in range(G):
                sl = slice(g_ * TCH, (g_ + 1) * TCH)
                if not rms:
                    v.tensor_sub(t0[:], xt[j % 2][:, sl], m_bc)
                    src = t0[:]
                else:
                    src = xt[j % 2][:, sl]
                if gam is None:
                    i = v.tensor_mul(xnb[:, sl], src, r_bc)
                    if xnf is not None:
                        i = v.tensor_mul(xnf[:, sl], src, r_bc)
                else:
                    v.tensor_mul(t0[:], src, r_bc)
                    gcol = gam[:, g_:g_ + 1]
                    if bet is None:
                        i = v.tensor_scalar_mul(xnb[:, sl], t0[:], gcol)
                        if xnf is not None:
                            i = v.tensor_scalar_mul(xnf[:, sl], t0[:], gcol)
                    else:
                        i = v.tensor_scalar(xnb[:, sl], t0[:], gcol, bet[:, g_:g_ + 1],
                                            ALU.mult, ALU.add)
                        if xnf is not None:
                            i = v.tensor_scalar(xnf[:, sl], t0[:], gcol, bet[:, g_:g_ + 1],
                                                ALU.mult, ALU.add)
            s_ap.inc(i, 1)

    @blk.gpsimd
    def _(g):
        rows_ap = _a(rows_hbm)
        for j in range(nch):
            g.wait_ge(s_row.h, 2 * (j + 1))
            s_rw.inc(g.dma_start(out=rows_ap[0:1, j * TCH:(j + 1) * TCH], in_=mean), 16)
            s_rw.inc(g.dma_start(out=rows_ap[1:2, j * TCH:(j + 1) * TCH], in_=rstd), 16)
            g.wait_ge(s_rw.h, 32 * (j + 1))
            if j >= 1:
                g.wait_ge(s_ap.h, j)
            rep_src = bass.AP(rows_hbm.tensor if isinstance(rows_hbm, bass.AP)
                              else rows_hbm, j * TCH,
                              [[0, 128], [ncols, 2], [1, TCH]])
            s_bc.inc(g.dma_start(
                out=_a(bc_t).rearrange("p (r t) -> p r t", r=2), in_=rep_src), 16)

    return s_st


def load_w_bf16(P, s_w, w_hbm, rows, cols, name):
    kg = rows // 128
    wsb = P.sb([128, kg * cols], BF16, name)

    @P.blk.gpsimd
    def _(g):
        s_w.inc(g.dma_start(out=sb_fm(wsb, cols, kg), in_=dram_fm(w_hbm, 0, cols, kg)), 16)
    return wsb


def emit_mamba_front(P, xn_hbm, w_in_x_hbm, w_in_z_hbm, conv_w_hbm, conv_b_hbm,
                     w_x_hbm, u_hbm, z_hbm, sp_hbm, Rsp, name="mf"):
    nc, blk = P.nc, P.blk
    s_w = P.sem(name)
    s_ld = SemPair(P, name)
    s_mmx = P.sem(name)
    s_evx = P.sem(name)
    s_silu = P.sem(name)
    s_ub = P.sem(name)
    s_mmz = P.sem(name)
    s_evz = P.sem(name)
    s_mmsp = P.sem(name)
    s_evsp = P.sem(name)
    s_odu = P.sem(name)
    s_odz = P.sem(name)
    s_odsp = P.sem(name)

    wx_sb = load_w_bf16(P, s_w, w_in_x_hbm, 1024, 1024, name + "wx")
    wz_sb = load_w_bf16(P, s_w, w_in_z_hbm, 1024, 1024, name + "wz")
    wsp_sb = load_w_bf16(P, s_w, w_x_hbm, 1024, Rsp, name + "wsp")
    cw_sb = P.sb([128, 4 * GI], F32, name + "cw")
    cb_sb = P.sb([128, GI], F32, name + "cb")

    @blk.gpsimd
    def _(g):
        s_w.inc(g.dma_start(out=cw_sb[:], in_=_a(conv_w_hbm)), 16)
        s_w.inc(g.dma_start(out=cb_sb[:], in_=_a(conv_b_hbm)), 16)

    xt = [P.sb([128, G * TCH], BF16, name + "xt") for _ in range(2)]
    ut = P.sb([128, GI * TCH], F32, name + "ut")
    ub = P.sb([128, GI * TCH], BF16, name + "ub")
    zt = P.sb([128, GI * TCH], BF16, name + "zt")
    xp_ext = P.sb([128, TCH + 3], F32, name + "xpe")
    hal = P.sb([128, 3 * GI], F32, name + "hal")
    spt = P.sb([Rsp, TCH], F32, name + "spt")
    ps_x = [P.ps([128, TCH], name + "psx") for _ in range(2)]
    ps_z = [P.ps([128, TCH], name + "psz") for _ in range(2)]
    ps_sp = P.ps([Rsp, TCH], name + "pssp")

    @blk.vector
    def _(v):
        v.memset(hal[:], 0.0)

    def emit_store(sync, j):
        sync.wait_ge(s_silu.h, (j + 1) * GI)
        sync.wait_ge(s_ub.h, j + 1)
        s_odu.inc(sync.dma_start(out=dram_fm(u_hbm, j * TCH, TCH, GI),
                                 in_=sb_fm(ut, TCH, GI)), 16)
        sync.wait_ge(s_evz.h, (j + 1) * GI)
        s_odz.inc(sync.dma_start(out=dram_fm(z_hbm, j * TCH, TCH, GI),
                                 in_=sb_fm(zt, TCH, GI)), 16)
        sync.wait_ge(s_evsp.h, j + 1)
        s_odsp.inc(sync.dma_start(out=_a(sp_hbm)[:, j * TCH:(j + 1) * TCH],
                                  in_=spt[:]), 16)

    @blk.sync
    def _(sync):
        sync.wait_ge(s_w.h, s_w.n)
        for j in range(NCH_F):
            if j >= 2:
                sync.wait_ge(s_mmz.h, (j - 1) * GI)
            s_ld.inc(sync.dma_start(out=xt[j % 2][:],
                                    in_=dram_fm(xn_hbm, j * TCH, TCH, G)), j)
            if j >= 1:
                emit_store(sync, j - 1)
        emit_store(sync, NCH_F - 1)
        sync.wait_ge(s_odu.h, s_odu.n)
        sync.wait_ge(s_odz.h, s_odz.n)
        sync.wait_ge(s_odsp.h, s_odsp.n)

    @blk.tensor
    def _(t):
        t.wait_ge(s_w.h, s_w.n)
        for j in range(NCH_F):
            s_ld.wait(t, j)
            for g in range(GI):
                gi = j * GI + g
                if gi >= 1:
                    t.wait_ge(s_evx.h, gi - 1)
                for kg in range(G):
                    i = t.matmul(ps_x[gi % 2][:],
                                 wx_sb[:, (kg * GI + g) * 128:(kg * GI + g + 1) * 128],
                                 xt[j % 2][:, kg * TCH:(kg + 1) * TCH],
                                 start=(kg == 0), stop=(kg == G - 1))
                s_mmx.inc(i, 1)
            for g in range(GI):
                gi = j * GI + g
                if gi >= 1:
                    t.wait_ge(s_evz.h, gi - 1)
                for kg in range(G):
                    i = t.matmul(ps_z[gi % 2][:],
                                 wz_sb[:, (kg * GI + g) * 128:(kg * GI + g + 1) * 128],
                                 xt[j % 2][:, kg * TCH:(kg + 1) * TCH],
                                 start=(kg == 0), stop=(kg == G - 1))
                s_mmz.inc(i, 1)
            t.wait_ge(s_ub.h, j + 1)
            if j >= 1:
                t.wait_ge(s_evsp.h, j)
            for kg in range(GI):
                i = t.matmul(ps_sp[:], wsp_sb[:, kg * Rsp:(kg + 1) * Rsp],
                             ub[:, kg * TCH:(kg + 1) * TCH],
                             start=(kg == 0), stop=(kg == GI - 1))
            s_mmsp.inc(i, 1)

    @blk.vector
    def _(v):
        for j in range(NCH_F):
            if j >= 1:
                v.wait_ge(s_odu.h, j * 16)
            for g in range(GI):
                gi = j * GI + g
                v.wait_ge(s_mmx.h, gi + 1)
                hs = slice(3 * g, 3 * g + 3)
                v.tensor_copy(xp_ext[:, 0:3], hal[:, hs])
                v.tensor_copy(xp_ext[:, 3:3 + TCH], ps_x[gi % 2][:])
                v.tensor_copy(hal[:, hs], ps_x[gi % 2][:, TCH - 3:TCH])
                sl = slice(g * TCH, (g + 1) * TCH)
                v.tensor_scalar_mul(ut[:, sl], xp_ext[:, 0:TCH], cw_sb[:, 4 * g:4 * g + 1])
                for k in range(1, 4):
                    i = v.scalar_tensor_tensor(ut[:, sl], xp_ext[:, k:k + TCH],
                                               cw_sb[:, 4 * g + k:4 * g + k + 1],
                                               ut[:, sl], ALU.mult, ALU.add)
                s_evx.inc(i, 1)
            v.wait_ge(s_silu.h, (j + 1) * GI)
            i = v.tensor_copy(ub[:], ut[:])
            s_ub.inc(i, 1)
            v.wait_ge(s_mmsp.h, j + 1)
            if j >= 1:
                v.wait_ge(s_odsp.h, j * 16)
            i = v.tensor_copy(spt[:], ps_sp[:])
            s_evsp.inc(i, 1)

    @blk.scalar
    def _(s):
        s.wait_ge(s_w.h, s_w.n)
        for j in range(NCH_F):
            for g in range(GI):
                gi = j * GI + g
                sl = slice(g * TCH, (g + 1) * TCH)
                s.wait_ge(s_evx.h, gi + 1)
                i = s.activation(ut[:, sl], ut[:, sl], AF.Silu, bias=cb_sb[:, g:g + 1])
                s_silu.inc(i, 1)
                s.wait_ge(s_mmz.h, gi + 1)
                if j >= 1 and g == 0:
                    s.wait_ge(s_odz.h, j * 16)
                i = s.copy(zt[:, sl], ps_z[gi % 2][:])
                s_evz.inc(i, 1)

    return s_odu


TS = 1024
NCHS = L // TS


def emit_scan(P, u_hbm, z_hbm, sp_hbm, g_hbm, wdt_hbm, alog_hbm, bdt_hbm,
              dvec_hbm, dtw_hbm, layer, name="sc"):
    nc, blk = P.nc, P.blk
    P.const(-1.0)
    nchs = L // TS
    NSUB = TS // 512
    Boff, Coff = (64, 80) if layer == 0 else (0, 16)

    s_w = P.sem(name)
    s_ci = P.sem(name)
    s_ini = P.sem(name)
    s_aex = P.sem(name)
    s_uld = [P.sem(name), P.sem(name)]
    s_zld = [P.sem(name), P.sem(name)]
    s_dl = P.sem(name)
    s_mmdt = P.sem(name)
    s_pre = P.sem(name)
    s_dtev = P.sem(name)
    s_dt = P.sem(name)
    s_du = P.sem(name)
    s_bc = [P.sem(name), P.sem(name)]
    s_a = P.sem(name)
    s_b = P.sem(name)
    s_sc = P.sem(name)
    s_nv = P.sem(name)
    s_yp = P.sem(name)
    s_si = P.sem(name)
    s_g0 = P.sem(name)
    s_go = P.sem(name)

    acol = P.sb([128, GI * NST], F32, name + "acol")
    bdt_c = P.sb([128, GI], F32, name + "bdt")
    d_c = P.sb([128, GI], F32, name + "dc")
    state = P.sb([128, GI * NST], F32, name + "st")
    if layer == 0:
        wdt_sb = P.sb([64, DIH], BF16, name + "wdt")
        dtlow = [P.sb([64, L // 2], BF16, name + "dtl") for _ in range(2)]
        ps_dt = [P.ps([128, 512], name + "pdt") for _ in range(2)]
        dtw_c = dtraw_bc = None
    else:
        dtw_c = P.sb([128, GI], F32, name + "dtw")
        dtraw_bc = [P.sb([128, TS], F32, name + "dtr") for _ in range(2)]
        wdt_sb = dtlow = ps_dt = None

    dtt = P.sb([128, GI * TS], F32, name + "dtt")
    du = P.sb([128, GI * TS], F32, name + "du")
    yb = P.sb([128, GI * TS], F32, name + "yb")
    sptmp = P.sb([128, 2 * 512], F32, name + "sptmp")
    spabs = P.sb([128, 512], F32, name + "spabs")
    jnk = P.sb([128, 8], F32, name + "jnk")
    ut = [P.sb([128, TS], F32, name + "ut") for _ in range(2)]
    at = [P.sb([128, TS], F32, name + "at") for _ in range(2)]
    bt = [P.sb([128, TS], F32, name + "bt") for _ in range(2)]
    ht = [P.sb([128, TS], F32, name + "ht") for _ in range(2)]
    tmp = P.sb([128, TS], F32, name + "tmp")
    bcb = [P.sb([128, 2 * TS], F32, name + "bcb") for _ in range(2)]
    zt = [P.sb([128, TS], BF16, name + "zt") for _ in range(2)]
    sz = [P.sb([128, TS], F32, name + "sz") for _ in range(2)]
    g0b = P.sb([128, GI * TS], BF16, name + "g0")

    sp_t = _a(sp_hbm).tensor

    @blk.gpsimd
    def _(g):
        s_w.inc(g.dma_start(out=acol[:], in_=_a(alog_hbm)), 16)
        s_w.inc(g.dma_start(out=bdt_c[:], in_=_a(bdt_hbm)), 16)
        s_w.inc(g.dma_start(out=d_c[:], in_=_a(dvec_hbm)), 16)
        if layer == 0:
            s_w.inc(g.dma_start(out=wdt_sb[:], in_=_a(wdt_hbm)), 16)
            s_dl.inc(g.dma_start(out=dtlow[0][:], in_=_a(sp_hbm)[0:64, 0:L // 2]), 16)
        else:
            s_w.inc(g.dma_start(out=dtw_c[:], in_=_a(dtw_hbm)), 16)
        P.ensure_consts(g, s_ci)

    @blk.scalar
    def _(s):
        s.wait_ge(s_w.h, s_w.n)
        s_aex.inc(s.activation(acol[:], acol[:], AF.Exp), 1)

    @blk.vector
    def _(v):
        v.wait_ge(s_aex.h, 1)
        v.tensor_scalar_mul(acol[:], acol[:], -1.0)
        s_ini.inc(v.memset(state[:], 0.0), 1)

    @blk.sync
    def _(sync):
        sync.wait_ge(s_w.h, s_w.n)
        for c in range(nchs):
            col0 = c * TS
            if layer == 1:
                if c >= 2:
                    sync.wait_ge(s_pre.h, (c - 1) * GI)
                rep = bass.AP(sp_t, 32 * L + col0, [[0, 128], [1, TS]])
                s_dl.inc(sync.dma_start(out=dtraw_bc[c % 2][:], in_=rep), 16)
            for g in range(GI):
                ic = c * GI + g
                if ic >= 2:
                    sync.wait_ge(s_du.h, ic - 1)
                s_uld[ic % 2].inc(sync.dma_start(
                    out=ut[ic % 2][:], in_=dram_fm(u_hbm, col0, TS, 1, g0=g)), 16)
            for n in range(NST):
                icn = c * NST + n
                if icn >= 2:
                    sync.wait_ge(s_nv.h, icn - 1)
                rep = bass.AP(sp_t, (Boff + n) * L + col0, [[0, 128], [1, TS]])
                s_bc[icn % 2].inc(sync.dma_start(out=bcb[icn % 2][:, 0:TS], in_=rep), 16)
                rep = bass.AP(sp_t, (Coff + n) * L + col0, [[0, 128], [1, TS]])
                s_bc[icn % 2].inc(sync.dma_start(out=bcb[icn % 2][:, TS:2 * TS], in_=rep), 16)
            for g in range(GI):
                ic = c * GI + g
                if ic >= 2:
                    sync.wait_ge(s_si.h, ic - 1)
                s_zld[ic % 2].inc(sync.dma_start(
                    out=zt[ic % 2][:], in_=dram_fm(z_hbm, col0, TS, 1, g0=g)), 16)
            sync.wait_ge(s_g0.h, (c + 1) * GI)
            s_go.inc(sync.dma_start(out=dram_fm(g_hbm, col0, TS, GI),
                                    in_=sb_fm(g0b, TS, GI)), 16)
        sync.wait_ge(s_go.h, s_go.n)

    if layer == 0:
        @blk.tensor
        def _(t):
            t.wait_ge(s_w.h, s_w.n)
            for c in range(nchs):
                for g in range(GI):
                    for sub in range(NSUB):
                        isub = (c * GI + g) * NSUB + sub
                        if isub >= 2:
                            t.wait_ge(s_dtev.h, isub - 1)
                        half = c // (nchs // 2)
                        t.wait_ge(s_dl.h, (half + 1) * 16)
                        c0 = (c % (nchs // 2)) * TS + sub * 512
                        i = t.matmul(ps_dt[isub % 2][:],
                                     wdt_sb[:, g * 128:(g + 1) * 128],
                                     dtlow[half][:, c0:c0 + 512],
                                     start=True, stop=True)
                        s_mmdt.inc(i, 1)

    @blk.scalar
    def _(s):
        s.wait_ge(s_ci.h, s_ci.n)
        for c in range(nchs):
            for g in range(GI):
                ic = c * GI + g
                for sub in range(NSUB):
                    isub = ic * NSUB + sub
                    sl = slice(g * TS + sub * 512, g * TS + (sub + 1) * 512)
                    tsl = slice(sub * 512, (sub + 1) * 512)
                    if layer == 0:
                        s.wait_ge(s_mmdt.h, isub + 1)
                        src = ps_dt[isub % 2][:]
                        bias = bdt_c[:, g:g + 1]
                    else:
                        if sub == 0:
                            s.wait_ge(s_pre.h, ic + 1)
                        src = dtt[:, sl]
                        bias = 0.0
                    if ic >= 1 and sub == 0:
                        s.wait_ge(s_dt.h, ic - 1 + 1)
                    s.activation(spabs[:], src, AF.Abs, bias=bias)
                    s.activation(spabs[:], spabs[:], AF.Exp, scale=-1.0)
                    s.activation(sptmp[:, tsl], spabs[:], AF.Ln, bias=1.0)
                    i = s.activation(dtt[:, sl], src, AF.Relu, bias=bias)
                    s_dtev.inc(i, 1)
            for n in range(NST):
                for g in range(GI):
                    icng = (c * NST + n) * GI + g
                    if icng == 0:
                        s.wait_ge(s_ini.h, 1)
                    s.wait_ge(s_dt.h, c * GI + g + 1)
                    if icng >= 2:
                        s.wait_ge(s_sc.h, icng - 1)
                    i = s.activation(at[icng % 2][:], dtt[:, g * TS:(g + 1) * TS],
                                     AF.Exp, scale=acol[:, g * NST + n:g * NST + n + 1])
                    s_a.inc(i, 1)
            for g in range(GI):
                ic = c * GI + g
                s.wait_ge(s_zld[ic % 2].h, ((ic // 2) + 1) * 16)
                if ic >= 2:
                    s.wait_ge(s_g0.h, ic - 1)
                i = s.activation(sz[ic % 2][:], zt[ic % 2][:], AF.Silu)
                s_si.inc(i, 1)

    @blk.gpsimd
    def _(g):
        for c in range(nchs):
            if layer == 0 and c == nchs // 2:
                g.wait_ge(s_mmdt.h, (nchs // 2) * GI * NSUB)
                s_dl.inc(g.dma_start(out=dtlow[1][:],
                                     in_=_a(sp_hbm)[0:64, L // 2:L]), 16)
            for n in range(NST):
                icn = c * NST + n
                for gg in range(GI):
                    icng = icn * GI + gg
                    g.wait_ge(s_du.h, c * GI + gg + 1)
                    g.wait_ge(s_bc[icn % 2].h, ((icn // 2) + 1) * 32)
                    if icng >= 2:
                        g.wait_ge(s_sc.h, icng - 1)
                    i = g.tensor_mul(bt[icng % 2][:], du[:, gg * TS:(gg + 1) * TS],
                                     bcb[icn % 2][:, 0:TS])
                    s_b.inc(i, 1)

    @blk.vector
    def _(v):
        def yops(gg, icng):
            slg = slice(gg * TS, (gg + 1) * TS)
            cbuf = bcb[(icng // GI) % 2][:, TS:2 * TS]
            v.tensor_mul(tmp[:], ht[gg % 2][:], cbuf)
            v.tensor_add(yb[:, slg], yb[:, slg], tmp[:])
            n_ = (icng // GI) % NST
            col = gg * NST + n_
            return v.tensor_copy(state[:, col:col + 1], ht[gg % 2][:, TS - 1:TS])

        for c in range(nchs):
            for g in range(GI):
                ic = c * GI + g
                sl = slice(g * TS, (g + 1) * TS)
                if layer == 1:
                    v.wait_ge(s_dl.h, (c + 1) * 16)
                    if c >= 1:
                        v.wait_ge(s_a.h, c * NST * GI)
                    i = v.tensor_scalar(dtt[:, sl], dtraw_bc[c % 2][:],
                                        dtw_c[:, g:g + 1], bdt_c[:, g:g + 1],
                                        ALU.mult, ALU.add)
                    s_pre.inc(i, 1)
                v.wait_ge(s_dtev.h, ic * NSUB + NSUB)
                if layer == 0 and c >= 1 and g == 0:
                    v.wait_ge(s_a.h, c * NST * GI)
                for sub in range(NSUB):
                    ssl = slice(g * TS + sub * 512, g * TS + (sub + 1) * 512)
                    i = v.tensor_add(dtt[:, ssl], dtt[:, ssl],
                                     sptmp[:, sub * 512:(sub + 1) * 512])
                s_dt.inc(i, 1)
                v.wait_ge(s_uld[ic % 2].h, ((ic // 2) + 1) * 16)
                if c >= 1 and g == 0:
                    v.wait_ge(s_b.h, c * NST * GI)
                v.tensor_mul(du[:, sl], dtt[:, sl], ut[ic % 2][:])
                i = v.tensor_scalar_mul(yb[:, sl], ut[ic % 2][:], d_c[:, g:g + 1])
                s_du.inc(i, 1)
            for n in range(NST):
                icn = c * NST + n
                prev = None
                for g in range(GI):
                    icng = icn * GI + g
                    col = g * NST + n
                    v.wait_ge(s_a.h, icng + 1)
                    v.wait_ge(s_b.h, icng + 1)
                    if icng == 0:
                        v.wait_ge(s_ini.h, 1)
                    i = v.tensor_tensor_scan(ht[g % 2][:], at[icng % 2][:],
                                             bt[icng % 2][:], state[:, col:col + 1],
                                             ALU.mult, ALU.add)
                    s_sc.inc(i, 1)
                    if prev is not None:
                        yops(prev[0], prev[1])
                    prev = (g, icng)
                v.tensor_copy(jnk[:], at[prev[1] % 2][:, 0:8])
                i = yops(prev[0], prev[1])
                s_nv.inc(i, 1)
            for g in range(GI):
                ic = c * GI + g
                sl = slice(g * TS, (g + 1) * TS)
                v.wait_ge(s_si.h, ic + 1)
                if c >= 1 and g == 0:
                    v.wait_ge(s_go.h, c * 16)
                i = v.tensor_mul(g0b[:, sl], yb[:, sl], sz[ic % 2][:])
                s_g0.inc(i, 1)
    return s_go



def emit_proj(P, rhs_hbm, w_hbm, out_hbm, K, M, T, res_hbm=None, res2_hbm=None,
              out_f32=True, name="pj"):
    nc, blk = P.nc, P.blk
    KG, MG = K // 128, M // 128
    nch = T // TCH
    s_w = P.sem(name)
    s_ld = SemPair(P, name)
    s_res = SemPair(P, name)
    s_res2 = SemPair(P, name)
    s_mm = P.sem(name)
    s_ev = P.sem(name)
    s_st = P.sem(name)

    wsb = load_w_bf16(P, s_w, w_hbm, K, M, name + "w")
    rt = [P.sb([128, KG * TCH], BF16, name + "r") for _ in range(2)]
    rest = [P.sb([128, MG * TCH], F32, name + "res") for _ in range(2)]         if res_hbm is not None else None
    res2t = [P.sb([128, MG * TCH], F32, name + "res2") for _ in range(2)] \
        if res2_hbm is not None else None
    ot = P.sb([128, MG * TCH], F32 if out_f32 else BF16, name + "o")
    ps = [P.ps([128, TCH], name + "ps") for _ in range(2)]

    @blk.sync
    def _(sync):
        for j in range(nch):
            if j >= 2:
                sync.wait_ge(s_mm.h, (j - 1) * MG)
            s_ld.inc(sync.dma_start(out=rt[j % 2][:],
                                    in_=dram_fm(rhs_hbm, j * TCH, TCH, KG)), j)
            if res_hbm is not None:
                if j >= 2:
                    sync.wait_ge(s_ev.h, (j - 1) * MG)
                s_res.inc(sync.dma_start(out=rest[j % 2][:],
                                         in_=dram_fm(res_hbm, j * TCH, TCH, MG)), j)
            if res2_hbm is not None:
                s_res2.inc(sync.dma_start(out=res2t[j % 2][:],
                                          in_=dram_fm(res2_hbm, j * TCH, TCH, MG)), j)
            if j >= 1:
                sync.wait_ge(s_ev.h, j * MG)
                s_st.inc(sync.dma_start(out=dram_fm(out_hbm, (j - 1) * TCH, TCH, MG),
                                        in_=sb_fm(ot, TCH, MG)), 16)
        sync.wait_ge(s_ev.h, nch * MG)
        s_st.inc(sync.dma_start(out=dram_fm(out_hbm, (nch - 1) * TCH, TCH, MG),
                                in_=sb_fm(ot, TCH, MG)), 16)
        sync.wait_ge(s_st.h, s_st.n)

    @blk.tensor
    def _(t):
        t.wait_ge(s_w.h, s_w.n)
        for j in range(nch):
            s_ld.wait(t, j)
            for mg in range(MG):
                imm = j * MG + mg
                if imm >= 2:
                    t.wait_ge(s_ev.h, imm - 1)
                for kg in range(KG):
                    i = t.matmul(ps[imm % 2][:],
                                 wsb[:, (kg * MG + mg) * 128:(kg * MG + mg + 1) * 128],
                                 rt[j % 2][:, kg * TCH:(kg + 1) * TCH],
                                 start=(kg == 0), stop=(kg == KG - 1))
                s_mm.inc(i, 1)

    @blk.vector
    def _(v):
        for j in range(nch):
            if res_hbm is not None:
                s_res.wait(v, j)
            if res2_hbm is not None:
                s_res2.wait(v, j)
            if j >= 1:
                v.wait_ge(s_st.h, j * 16)
            for mg in range(MG):
                imm = j * MG + mg
                v.wait_ge(s_mm.h, imm + 1)
                sl = slice(mg * TCH, (mg + 1) * TCH)
                if res2_hbm is not None:
                    v.tensor_add(ot[:, sl], rest[j % 2][:, sl], ps[imm % 2][:])
                    i = v.tensor_add(ot[:, sl], ot[:, sl], res2t[j % 2][:, sl])
                elif res_hbm is not None:
                    i = v.tensor_add(ot[:, sl], rest[j % 2][:, sl], ps[imm % 2][:])
                else:
                    i = v.tensor_copy(ot[:, sl], ps[imm % 2][:])
                s_ev.inc(i, 1)
    return s_st



def emit_ident(P, blk, s_done, dtype=F32, name="id"):
    idt = P.sb([128, 128], dtype, name + "t")
    i0 = P.sb([128, 128], F32, name + "i0")
    i1 = P.sb([128, 1], F32, name + "i1")

    @blk.gpsimd
    def _(g):
        g.iota(i0[:], [[1, 128]], channel_multiplier=0, allow_small_or_imprecise_dtypes=True)
        s_done.inc(g.iota(i1[:], [[1, 1]], channel_multiplier=1,
                          allow_small_or_imprecise_dtypes=True), 1)

    @blk.vector
    def _(v):
        v.wait_ge(s_done.h, 1)
        s_done.inc(v.tensor_scalar(idt[:], i0[:], i1[:, 0:1], None, ALU.is_equal), 1)
    return idt


def emit_router(P, xn_hbm, wr_hbm, w2_hbm, br_hbm, cw_hbm, mode, T=TL, name="rt",
                debug=None):
    nc, blk = P.nc, P.blk
    P.const(-1.0)
    if mode == "l0":
        P.const(10.0)
    NE = 8
    nblk = T // 128
    GSZ = min(16, nblk)
    ngrp = nblk // GSZ
    s_w = P.sem(name)
    s_ci = P.sem(name)
    s_id = P.sem(name)
    s_ld = SemPair(P, name)
    s_mm = P.sem(name)
    s_sm = P.sem(name)
    s_ex = P.sem(name)
    s_pp = P.sem(name)
    s_p2 = P.sem(name)
    s_cw = P.sem(name)
    s_tr = P.sem(name)
    s_ote = P.sem(name)
    s_st = P.sem(name)

    ident = emit_ident(P, blk, s_id, BF16, name + "id") if debug not in ("notr", "mm", "p1", "p2") else None
    nw = 2 if mode == "l0" else 1
    wr_sb = P.sb([128, G * NE * nw], BF16, name + "wr")
    br_rep = P.sb([128, NE], F32, name + "br") if mode == "l1" else None
    xt = [P.sb([128, G * 512], BF16, name + "x") for _ in range(2)]
    lg = P.sb([128, 2 * NE], F32, name + "lg")
    mx = P.sb([128, 2], F32, name + "mx")
    ex = P.sb([128, 2 * NE], F32, name + "ex")
    sm = P.sb([128, 2], F32, name + "sm")
    rcp = P.sb([128, 2], F32, name + "rcp")
    cwt = P.sb([128, NE], F32, name + "cw")
    cws = P.sb([128, 4], F32, name + "cws")
    spc = P.sb([128, 2], F32, name + "spc")
    spc2 = P.sb([128, 8], F32, name + "spc2")
    cwt16 = P.sb([128, GSZ * NE], F32 if debug in ("notr", "mm", "p1", "p2") else BF16, name + "cw16")
    cw128 = P.sb([128, GSZ * NE], F32, name + "cw128")
    ps_r = [P.ps([128, NE * nw], name + "pr") for _ in range(2)]
    pst = P.psb([128, GSZ * NE], name + "pst")

    @blk.gpsimd
    def _(g):
        wv = _a(wr_sb)
        s_w.inc(g.dma_start(
            out=bass.AP(wv.tensor, wv.offset, [[NE * nw * G, 128], [NE * nw, G], [1, NE]]),
            in_=dram_fm(wr_hbm, 0, NE, G)), 16)
        if mode == "l0":
            s_w.inc(g.dma_start(
                out=bass.AP(wv.tensor, wv.offset + NE,
                            [[NE * nw * G, 128], [NE * nw, G], [1, NE]]),
                in_=dram_fm(w2_hbm, 0, NE, G)), 16)
        else:
            rep = bass.AP(_a(br_hbm).tensor, 0, [[0, 128], [1, NE]])
            s_w.inc(g.dma_start(out=br_rep[:], in_=rep), 16)
        P.ensure_consts(g, s_ci)
        if debug in ("notr", "mm", "p1", "p2"):
            for blki in range(nblk):
                g.wait_ge(s_cw.h, blki + 1)
                dst = bass.AP(_a(cw_hbm).tensor, blki * 128 * NE, [[NE, 128], [1, NE]])
                s_st.inc(g.dma_start(out=dst, in_=cwt16[:, 0:NE]), 16)
            g.wait_ge(s_st.h, s_st.n)

    @blk.sync
    def _(sync):
        nch = T // 512
        for j in range(nch):
            if j >= 2:
                sync.wait_ge(s_mm.h, (j - 1) * 4)
            s_ld.inc(sync.dma_start(out=xt[j % 2][:],
                                    in_=dram_fm(xn_hbm, j * 512, 512, G)), j)
        if debug in ("notr", "mm", "p1", "p2"):
            return
        for gr in range(ngrp):
            sync.wait_ge(s_ote.h, gr + 1)
            for e in range(NE):
                dst = bass.AP(_a(cw_hbm).tensor, e * T + gr * GSZ * 128,
                              [[128, GSZ], [1, 128]])
                s_st.inc(sync.dma_start(
                    out=dst, in_=cw128[e * GSZ:(e + 1) * GSZ, :]), 16)
        sync.wait_ge(s_st.h, s_st.n)

    @blk.tensor
    def _(t):
        t.wait_ge(s_w.h, s_w.n)
        if debug not in ("notr", "mm", "p1", "p2"):
            t.wait_ge(s_id.h, 2)
        for blki in range(nblk):
            j = blki // 4
            sub = blki % 4
            s_ld.wait(t, j)
            if blki >= 2:
                t.wait_ge(s_sm.h, blki - 1)
            xsl = lambda kg: xt[j % 2][:, kg * 512 + sub * 128:kg * 512 + (sub + 1) * 128]
            for kg in range(G):
                i = t.matmul(ps_r[blki % 2][:], xsl(kg),
                             wr_sb[:, kg * NE * nw:(kg + 1) * NE * nw],
                             start=(kg == 0), stop=(kg == G - 1))
            s_mm.inc(i, 1)
            if debug not in ("notr", "mm", "p1", "p2") and blki % GSZ == GSZ - 1:
                gr = blki // GSZ
                t.wait_ge(s_cw.h, blki + 1)
                if gr >= 1:
                    t.wait_ge(s_ote.h, gr)
                i = t.transpose(pst[:], cwt16[:], ident[:])
                s_tr.inc(i, 1)

    @blk.scalar
    def _(s):
        s.wait_ge(s_ci.h, s_ci.n)
        if debug == "mm" or debug == "p1":
            return
        import os
        _v = os.environ.get("RT_ACT_VARIANT", "full")
        for blki in range(nblk):
            s.wait_ge(s_sm.h, blki + 1)
            if mode == "l0":
                if _v == "noaccum":
                    s.activation(ex[:, 0:NE], lg[:, 0:NE], AF.Exp, bias=mx[:, 0:1])
                    i = s.activation(ex[:, NE:2 * NE], lg[:, NE:2 * NE], AF.Exp,
                                     bias=mx[:, 1:2])
                elif _v == "nobias":
                    s.activation(ex[:, 0:NE], lg[:, 0:NE], AF.Exp,
                                 accum_out=sm[:, 0:1])
                    i = s.activation(ex[:, NE:2 * NE], lg[:, NE:2 * NE], AF.Exp,
                                     accum_out=sm[:, 1:2])
                elif _v == "one":
                    i = s.activation(ex[:, 0:NE], lg[:, 0:NE], AF.Exp, bias=mx[:, 0:1],
                                     accum_out=sm[:, 0:1])
                else:
                    s.activation(ex[:, 0:NE], lg[:, 0:NE], AF.Exp, bias=mx[:, 0:1],
                                 accum_out=sm[:, 0:1])
                    i = s.activation(ex[:, NE:2 * NE], lg[:, NE:2 * NE], AF.Exp,
                                     bias=mx[:, 1:2], accum_out=sm[:, 1:2])
            else:
                i = s.activation(ex[:, 0:NE], lg[:, 0:NE], AF.Exp, bias=mx[:, 0:1],
                                 accum_out=sm[:, 0:1])
            s_ex.inc(i, 1)
            if mode == "l0" and debug is None:
                s.wait_ge(s_pp.h, blki + 1)
                i = s.activation(spc2[:, 0:NE], cwt[:], AF.Copy, accum_out=cws[:, 2:3])
                s_p2.inc(i, 1)

    @blk.vector
    def _(v):
        if debug == "p2":
            for blki in range(nblk):
                v.wait_ge(s_mm.h, blki + 1)
                if blki >= 1:
                    v.wait_ge(s_st.h, blki * 16)
                v.tensor_copy(lg[:, 0:NE], ps_r[blki % 2][:, 0:NE])
                v.tensor_scalar_mul(lg[:, NE:2 * NE], ps_r[blki % 2][:, NE:2 * NE], 10.0)
                v.reduce_max(mx[:, 0:1], lg[:, 0:NE], axis=AX.X, negate=True)
                i = v.reduce_max(mx[:, 1:2], lg[:, NE:2 * NE], axis=AX.X, negate=True)
                s_sm.inc(i, 1)
                import os as _os
                _w = _os.environ.get("RT_P2V", "wait1")
                if _w == "wait1":
                    v.wait_ge(s_ex.h, blki + 1)
                    i = v.tensor_copy(cwt16[:, 0:NE], lg[:, 0:NE])
                elif _w == "copy2":
                    v.tensor_copy(cwt[:], lg[:, 0:NE])
                    i = v.tensor_copy(cwt16[:, 0:NE], cwt[:])
                else:
                    i = v.tensor_copy(cwt16[:, 0:NE], lg[:, 0:NE])
                s_cw.inc(i, 1)
            return
        if debug == "p1":
            for blki in range(nblk):
                v.wait_ge(s_mm.h, blki + 1)
                if blki >= 1:
                    v.wait_ge(s_st.h, blki * 16)
                v.tensor_copy(lg[:, 0:NE], ps_r[blki % 2][:, 0:NE])
                v.tensor_scalar_mul(lg[:, NE:2 * NE], ps_r[blki % 2][:, NE:2 * NE], 10.0)
                v.reduce_max(mx[:, 0:1], lg[:, 0:NE], axis=AX.X, negate=True)
                i = v.reduce_max(mx[:, 1:2], lg[:, NE:2 * NE], axis=AX.X, negate=True)
                s_sm.inc(i, 1)
                i = v.tensor_copy(cwt16[:, 0:NE], lg[:, 0:NE])
                s_cw.inc(i, 1)
            return
        if debug == "mm":
            for blki in range(nblk):
                v.wait_ge(s_mm.h, blki + 1)
                if blki >= 1:
                    v.wait_ge(s_st.h, blki * 16)
                i = v.tensor_copy(cwt16[:, 0:NE], ps_r[blki % 2][:, 0:NE])
                s_sm.inc(i, 1)
                i = v.tensor_copy(spc[:], ps_r[blki % 2][:, 0:2])
                s_cw.inc(i, 1)
            return
        for blki in range(nblk):
            v.wait_ge(s_mm.h, blki + 1)
            if mode == "l0":
                v.tensor_copy(lg[:, 0:NE], ps_r[blki % 2][:, 0:NE])
                v.tensor_scalar_mul(lg[:, NE:2 * NE], ps_r[blki % 2][:, NE:2 * NE], 10.0)
                v.reduce_max(mx[:, 0:1], lg[:, 0:NE], axis=AX.X, negate=True)
                i = v.reduce_max(mx[:, 1:2], lg[:, NE:2 * NE], axis=AX.X, negate=True)
            else:
                v.tensor_add(lg[:, 0:NE], ps_r[blki % 2][:, 0:NE], br_rep[:])
                v.tensor_copy(spc[:], br_rep[:, 0:2])
                i = v.reduce_max(mx[:, 0:1], lg[:, 0:NE], axis=AX.X, negate=True)
            s_sm.inc(i, 1)
            v.wait_ge(s_ex.h, blki + 1)
            gr = blki // GSZ
            if blki % GSZ == 0 and gr >= 1:
                v.wait_ge(s_tr.h, gr)
            cw16_view = bass.AP(_a(cwt16).tensor, _a(cwt16).offset + (blki % GSZ),
                                [[GSZ * NE, 128], [GSZ, NE]])
            if debug == "logits":
                v.tensor_copy(cwt[:], lg[:, 0:NE])
            elif debug == "exp":
                v.tensor_copy(cwt[:], ex[:, 0:NE])
            elif mode == "l0":
                v.tensor_mul(cws[:, 0:1], sm[:, 0:1], sm[:, 1:2])
                i = v.tensor_mul(cwt[:], ex[:, 0:NE], ex[:, NE:2 * NE])
                s_pp.inc(i, 1)
                v.wait_ge(s_p2.h, blki + 1)
                v.scalar_tensor_tensor(cws[:, 1:2], cws[:, 0:1], 1e-9,
                                       cws[:, 2:3], ALU.mult, ALU.add)
                v.tensor_copy(spc[:], sm[:])
                v.reciprocal(cws[:, 3:4], cws[:, 1:2])
                v.tensor_copy(spc[:], sm[:])
                v.tensor_scalar_mul(cwt[:], cwt[:], cws[:, 3:4])
                v.tensor_copy(spc[:], sm[:])
            else:
                v.reciprocal(rcp[:, 0:1], sm[:, 0:1])
                v.tensor_scalar_mul(mx[:, 1:2], mx[:, 0:1], -1.0)
                v.tensor_copy(spc[:], lg[:, 0:2])
                v.tensor_scalar(cwt[:], lg[:, 0:NE], mx[:, 1:2], None, ALU.is_ge)
                v.tensor_copy(spc[:], lg[:, 0:2])
                v.tensor_scalar_mul(cwt[:], cwt[:], rcp[:, 0:1])
                v.tensor_copy(spc[:], sm[:])
            i = v.tensor_copy(cw16_view, cwt[:])
            s_cw.inc(i, 1)
            if debug != "notr" and blki % GSZ == GSZ - 1:
                v.wait_ge(s_tr.h, gr + 1)
                if gr >= 1:
                    v.wait_ge(s_st.h, gr * 16)
                i = v.tensor_copy(cw128[:], pst[:])
                s_ote.inc(i, 1)
    return s_st

def emit_moe(P, xn_hbm, wg_hbm, wu_hbm, wd_hbm, cw_hbm, out_hbm, two_mats=True,
             b1_hbm=None, b2_hbm=None, T=TL, name="moe"):
    nc, blk = P.nc, P.blk
    NE, F = 8, 4096
    TB = min(1024, T)
    NTC = TB // 512
    ntb = T // TB
    FG = F // 128
    MB = 4
    MH = FG // MB
    s_xn = [P.sem(name), P.sem(name)]
    s_wgu = [P.sem(name), P.sem(name)]
    s_wd = P.sem(name)
    s_gu = P.sem(name)
    s_sil = P.sem(name)
    s_hev = P.sem(name)
    s_dmm = P.sem(name)
    s_dev = P.sem(name)
    s_acc = P.sem(name)
    s_cwb = [P.sem(name), P.sem(name)]

    xnt = [P.sb([128, G * TB], BF16, name + "xn") for _ in range(2)]
    wbuf = P.sb([128, 2 * 16384], BF16, name + "wb")
    hsb = P.sb([128, FG * TB], BF16, name + "h")
    cwbc = [P.sb([128, TB], F32, name + "cwb") for _ in range(2)]
    sil = P.sb([128, 512], F32, name + "sil")
    mjnk = P.sb([128, 4], F32, name + "jnk")
    dtmp = [P.sb([128, 512], F32, name + "dt") for _ in range(2)]
    b1c = P.sb([128, NE * FG], F32, name + "b1") if b1_hbm is not None else None
    b2c = P.sb([128, NE * G], F32, name + "b2") if b2_hbm is not None else None
    ps_g = [P.ps([128, 512], name + "pg") for _ in range(2)]
    ps_u = [P.ps([128, 512], name + "pu") for _ in range(2)] if two_mats else None
    ps_d = [P.ps([128, 512], name + "pd") for _ in range(2)]

    def wgu_ap(par, which):
        base = par * 16384 + which * 8192
        return _a(wbuf)[:, base:base + 8192]

    def wd_ap(fh):
        return _a(wbuf)[:, fh * 1024:(fh + 1) * 1024]

    s_w0 = P.sem(name)

    @blk.gpsimd
    def _(g):
        if b1_hbm is not None:
            s_w0.inc(g.dma_start(out=b1c[:], in_=_a(b1_hbm)), 16)
        if b2_hbm is not None:
            s_w0.inc(g.dma_start(out=b2c[:], in_=_a(b2_hbm)), 16)
        for e in range(NE):
            for tb in range(ntb):
                it = e * ntb + tb
                if it >= 2:
                    g.wait_ge(s_gu.h, (it - 1) * MB * MH * NTC)
                s_xn[it % 2].inc(g.dma_start(
                    out=xnt[it % 2][:], in_=dram_fm(xn_hbm, tb * TB, TB, G)), 16)
                for mb in range(MB):
                    imb = it * MB + mb
                    if imb >= 1:
                        g.wait_ge(s_dmm.h, it * G * NTC)
                    if imb >= 2:
                        g.wait_ge(s_gu.h, (imb - 1) * MH * NTC)
                    par = imb % 2
                    wge = _a(wg_hbm)[e]
                    s_wgu[par].inc(g.dma_start(
                        out=wgu_ap(par, 0).rearrange("p (k m) -> p k m", k=G),
                        in_=dram_fm(wge, mb * TB, TB, G)), 16)
                    if two_mats:
                        wue = _a(wu_hbm)[e]
                        s_wgu[par].inc(g.dma_start(
                            out=wgu_ap(par, 1).rearrange("p (k m) -> p k m", k=G),
                            in_=dram_fm(wue, mb * TB, TB, G)), 16)
                g.wait_ge(s_gu.h, (it + 1) * MB * MH * NTC)
                wde = _a(wd_hbm)[e]
                for fh in range(FG):
                    s_wd.inc(g.dma_start(
                        out=wd_ap(fh).rearrange("p (o m) -> p o m", o=1),
                        in_=dram_fm(wde, 0, D, 1, g0=fh)), 16)
                for dmh in range(G):
                    for tc in range(NTC):
                        idt = (it * G + dmh) * NTC + tc
                        g.wait_ge(s_dev.h, idt + 1)
                        import os as _os
                        if _os.environ.get("MOE_SERIAL_ACC"):
                            g.wait_ge(s_acc.h, s_acc.n)
                        elif e >= 1:
                            g.wait_ge(s_acc.h, e * ntb * G * NTC * 16)
                        s_acc.inc(g.dma_start(
                            out=dram_fm(out_hbm, tb * TB + tc * 512, 512, 1, g0=dmh),
                            in_=dtmp[idt % 2][:], accum_op=ALU.add), 16)
        g.wait_ge(s_acc.h, s_acc.n)

    @blk.sync
    def _(sync):
        for e in range(NE):
            for tb in range(ntb):
                it = e * ntb + tb
                if it >= 2:
                    sync.wait_ge(s_dev.h, (it - 1) * G * NTC)
                rep = bass.AP(_a(cw_hbm).tensor, e * T + tb * TB, [[0, 128], [1, TB]])
                s_cwb[it % 2].inc(sync.dma_start(out=cwbc[it % 2][:], in_=rep), 16)

    @blk.tensor
    def _(t):
        t.wait_ge(s_w0.h, s_w0.n)
        for e in range(NE):
            for tb in range(ntb):
                it = e * ntb + tb
                t.wait_ge(s_xn[it % 2].h, ((it // 2) + 1) * 16)
                for mb in range(MB):
                    imb = it * MB + mb
                    par = imb % 2
                    nw = 32 if two_mats else 16
                    t.wait_ge(s_wgu[par].h, ((imb // 2) + 1) * nw)
                    for mh in range(MH):
                        for tc in range(NTC):
                            ih = ((it * MB + mb) * MH + mh) * NTC + tc
                            if ih >= 2:
                                t.wait_ge(s_hev.h, ih - 1)
                            for kg in range(G):
                                i = t.matmul(
                                    ps_g[ih % 2][:],
                                    wgu_ap(par, 0)[:, kg * TB + mh * 128:kg * TB + (mh + 1) * 128],
                                    xnt[it % 2][:, kg * TB + tc * 512:kg * TB + (tc + 1) * 512],
                                    start=(kg == 0), stop=(kg == G - 1))
                            if two_mats:
                                for kg in range(G):
                                    i = t.matmul(
                                        ps_u[ih % 2][:],
                                        wgu_ap(par, 1)[:, kg * TB + mh * 128:kg * TB + (mh + 1) * 128],
                                        xnt[it % 2][:, kg * TB + tc * 512:kg * TB + (tc + 1) * 512],
                                        start=(kg == 0), stop=(kg == G - 1))
                            s_gu.inc(i, 1)
                t.wait_ge(s_hev.h, (it + 1) * MB * MH * NTC)
                t.wait_ge(s_wd.h, (it + 1) * FG * 16)
                for dmh in range(G):
                    for tc in range(NTC):
                        idt = (it * G + dmh) * NTC + tc
                        if idt >= 2:
                            t.wait_ge(s_dev.h, idt - 1)
                        for fh in range(FG):
                            i = t.matmul(
                                ps_d[idt % 2][:],
                                wd_ap(fh)[:, dmh * 128:(dmh + 1) * 128],
                                hsb[:, fh * TB + tc * 512:fh * TB + (tc + 1) * 512],
                                start=(fh == 0), stop=(fh == FG - 1))
                        s_dmm.inc(i, 1)

    @blk.scalar
    def _(s):
        for e in range(NE):
            for tb in range(ntb):
                it = e * ntb + tb
                if it >= 1:
                    s.wait_ge(s_dmm.h, it * G * NTC)
                for mb in range(MB):
                    for mh in range(MH):
                        for tc in range(NTC):
                            ih = ((it * MB + mb) * MH + mh) * NTC + tc
                            s.wait_ge(s_gu.h, ih + 1)
                            fh = mb * MH + mh
                            hsl = slice(fh * TB + tc * 512, fh * TB + (tc + 1) * 512)
                            if two_mats:
                                if ih >= 1:
                                    s.wait_ge(s_hev.h, ih)
                                i = s.activation(sil[:], ps_g[ih % 2][:], AF.Silu)
                            else:
                                b = b1c[:, e * FG + fh:e * FG + fh + 1]
                                s.activation(hsb[:, hsl], ps_g[ih % 2][:],
                                             AF.Relu, bias=b)
                                i = s.copy(sil[:, 0:4], ps_g[ih % 2][:, 0:4])
                            s_sil.inc(i, 1)

    @blk.vector
    def _(v):
        for e in range(NE):
            for tb in range(ntb):
                it = e * ntb + tb
                if it >= 1 and two_mats:
                    v.wait_ge(s_dmm.h, it * G * NTC)
                for mb in range(MB):
                    for mh in range(MH):
                        for tc in range(NTC):
                            ih = ((it * MB + mb) * MH + mh) * NTC + tc
                            fh = mb * MH + mh
                            hsl = slice(fh * TB + tc * 512, fh * TB + (tc + 1) * 512)
                            v.wait_ge(s_sil.h, ih + 1)
                            if two_mats:
                                v.tensor_mul(hsb[:, hsl], sil[:], ps_u[ih % 2][:])
                                i = v.tensor_copy(mjnk[:], sil[:, 0:4])
                            else:
                                i = v.tensor_copy(mjnk[:, 0:1], hsb[:, hsl][:, 0:1])
                            s_hev.inc(i, 1)
                for dmh in range(G):
                    for tc in range(NTC):
                        idt = (it * G + dmh) * NTC + tc
                        v.wait_ge(s_dmm.h, idt + 1)
                        v.wait_ge(s_cwb[it % 2].h, ((it // 2) + 1) * 16)
                        if idt >= 2:
                            v.wait_ge(s_acc.h, (idt - 1) * 16)
                        csl = cwbc[it % 2][:, tc * 512:(tc + 1) * 512]
                        if b2_hbm is not None:
                            v.scalar_tensor_tensor(
                                dtmp[idt % 2][:], ps_d[idt % 2][:],
                                b2c[:, e * G + dmh:e * G + dmh + 1], csl,
                                ALU.add, ALU.mult)
                        else:
                            v.tensor_mul(dtmp[idt % 2][:], ps_d[idt % 2][:], csl)
                        i = v.tensor_copy(mjnk[:], csl[:, 0:4])
                        s_dev.inc(i, 1)
    return s_acc


def emit_swiglu(P, xn_hbm, wg_hbm, wu_hbm, h_hbm, MF, T=TL, name="sw"):
    nc, blk = P.nc, P.blk
    MG = MF // 128
    nch = T // TCH
    s_w = P.sem(name)
    s_ld = SemPair(P, name)
    s_mm = P.sem(name)
    s_sil = P.sem(name)
    s_h = P.sem(name)
    s_st = P.sem(name)

    wgs = load_w_bf16(P, s_w, wg_hbm, 1024, MF, name + "wg")
    wus = load_w_bf16(P, s_w, wu_hbm, 1024, MF, name + "wu")
    xt = [P.sb([128, G * TCH], BF16, name + "x") for _ in range(2)]
    sil = P.sb([128, TCH], F32, name + "sil")
    jnk = P.sb([128, 4], F32, name + "jnk")
    ht = P.sb([128, MG * TCH], BF16, name + "h")
    ps_g = [P.ps([128, TCH], name + "pg") for _ in range(2)]
    ps_u = [P.ps([128, TCH], name + "pu") for _ in range(2)]

    @blk.sync
    def _(sync):
        for j in range(nch):
            if j >= 2:
                sync.wait_ge(s_mm.h, (j - 1) * MG)
            s_ld.inc(sync.dma_start(out=xt[j % 2][:],
                                    in_=dram_fm(xn_hbm, j * TCH, TCH, G)), j)
            if j >= 1:
                sync.wait_ge(s_h.h, j * MG)
                s_st.inc(sync.dma_start(out=dram_fm(h_hbm, (j - 1) * TCH, TCH, MG),
                                        in_=sb_fm(ht, TCH, MG)), 16)
        sync.wait_ge(s_h.h, nch * MG)
        s_st.inc(sync.dma_start(out=dram_fm(h_hbm, (nch - 1) * TCH, TCH, MG),
                                in_=sb_fm(ht, TCH, MG)), 16)
        sync.wait_ge(s_st.h, s_st.n)

    @blk.tensor
    def _(t):
        t.wait_ge(s_w.h, s_w.n)
        for j in range(nch):
            s_ld.wait(t, j)
            for mg in range(MG):
                imm = j * MG + mg
                if imm >= 2:
                    t.wait_ge(s_h.h, imm - 1)
                for kg in range(G):
                    t.matmul(ps_g[imm % 2][:],
                             wgs[:, (kg * MG + mg) * 128:(kg * MG + mg + 1) * 128],
                             xt[j % 2][:, kg * TCH:(kg + 1) * TCH],
                             start=(kg == 0), stop=(kg == G - 1))
                for kg in range(G):
                    i = t.matmul(ps_u[imm % 2][:],
                                 wus[:, (kg * MG + mg) * 128:(kg * MG + mg + 1) * 128],
                                 xt[j % 2][:, kg * TCH:(kg + 1) * TCH],
                                 start=(kg == 0), stop=(kg == G - 1))
                s_mm.inc(i, 1)

    @blk.scalar
    def _(s):
        for j in range(nch):
            for mg in range(MG):
                imm = j * MG + mg
                s.wait_ge(s_mm.h, imm + 1)
                if imm >= 1:
                    s.wait_ge(s_h.h, imm)
                i = s.activation(sil[:], ps_g[imm % 2][:], AF.Silu)
                s_sil.inc(i, 1)

    @blk.vector
    def _(v):
        for j in range(nch):
            if j >= 1:
                v.wait_ge(s_st.h, j * 16)
            for mg in range(MG):
                imm = j * MG + mg
                v.wait_ge(s_sil.h, imm + 1)
                sl = slice(mg * TCH, (mg + 1) * TCH)
                v.tensor_mul(ht[:, sl], sil[:], ps_u[imm % 2][:])
                i = v.tensor_copy(jnk[:], sil[:, 0:4])
                s_h.inc(i, 1)
    return s_st


def emit_attn(P, q_hbm, k_hbm, v_hbm, o_hbm, name="at"):
    nc, blk = P.nc, P.blk
    HD = 256
    HG = HD // 128
    NQB = L // 128
    sc = float(HD) ** -0.5
    P.const(sc)
    P.const(-1.0)
    s_ci = P.sem(name)
    s_id = P.sem(name)
    s_w = P.sem(name)
    s_vt = P.sem(name)
    s_vte = P.sem(name)
    s_S = P.sem(name)
    s_nm = P.sem(name)
    s_P = P.sem(name)
    s_fac = P.sem(name)
    s_pt = P.sem(name)
    s_pte = P.sem(name)
    s_pv = P.sem(name)
    s_ost = P.sem(name)
    s_fin = P.sem(name)
    s_otr = P.sem(name)
    s_oev = P.sem(name)
    s_go = P.sem(name)

    ident = emit_ident(P, blk, s_id, BF16, name + "id")
    qs = P.sb([128, 4 * L], BF16, name + "qs")
    ks = P.sb([128, HG * L], BF16, name + "ks")
    vs = P.sb([128, HG * L], BF16, name + "vs")
    vt = P.sb([128, 2 * L], BF16, name + "vt")
    masks = P.sb([128, 4 * 512], F32, name + "mask")
    zeros = P.sb([128, 512], F32, name + "z")
    Sm = P.sb([128, 512], F32, name + "smm")
    Pt = P.sb([128, 512], BF16, name + "ptm")
    ptt = P.sb([128, 512], BF16, name + "ptt")
    nmb = P.sb([128, 4], F32, name + "nmb")
    fac = P.sb([128, 1], F32, name + "fac")
    lrow = P.sb([128, 1], F32, name + "lr")
    lblk = P.sb([128, 1], F32, name + "lb")
    osb = P.sb([128, HD], F32, name + "osb")
    ofin = P.sb([128, HD], BF16, name + "ofin")
    ojnk = P.sb([128, 4], F32, name + "ojnk")
    ajnk = P.sb([128, 4], F32, name + "ajnk")
    oev = P.sb([128, HG * 128], BF16, name + "oevt")
    ps_s = [P.ps([128, 512], name + "pss") for _ in range(2)]
    ps_pt = P.psb([128, 1024], name + "pspt")
    ps_o = [P.ps([128, HD], name + "pso") for _ in range(2)]
    ps_ot = [P.psb([128, 1024], name + "psot") for _ in range(2)]

    def nkb(qb):
        return qb // 4 + 1

    tot_ikb = 2 * sum(nkb(qb) for qb in range(NQB))

    @blk.gpsimd
    def _(g):
        s_w.inc(g.dma_start(out=sb_fm(qs, L, 4), in_=dram_fm(q_hbm, 0, L, 4)), 16)
        s_w.inc(g.dma_start(out=sb_fm(ks, L, HG), in_=dram_fm(k_hbm, 0, L, HG)), 16)
        s_w.inc(g.dma_start(out=sb_fm(vs, L, HG), in_=dram_fm(v_hbm, 0, L, HG)), 16)
        g.memset(zeros[:], 0.0)
        for r in range(4):
            i = g.affine_select(masks[:, r * 512:(r + 1) * 512], zeros[:],
                                [[-1, 512]], ALU.is_ge, -1e30,
                                base=r * 128, channel_multiplier=1)
            s_ci.inc(i, 1)
        P.ensure_consts(g, s_ci)

    @blk.tensor
    def _(t):
        t.wait_ge(s_w.h, s_w.n)
        t.wait_ge(s_id.h, 2)
        for kb1 in range(L // 128):
            for vg in range(HG):
                ivt = kb1 * HG + vg
                if ivt >= 1:
                    t.wait_ge(s_vte.h, ivt)
                i = t.transpose(ps_pt[:, 0:128],
                                vs[:, vg * L + kb1 * 128:vg * L + (kb1 + 1) * 128],
                                ident[:])
                s_vt.inc(i, 1)
        ikb = 0
        for hh in range(2):
            for qb in range(NQB):
                iq = hh * NQB + qb
                for kb in range(nkb(qb)):
                    if ikb >= 2:
                        t.wait_ge(s_P.h, ikb - 1)
                    for hg in range(HG):
                        i = t.matmul(
                            ps_s[ikb % 2][:],
                            qs[:, (hh * HG + hg) * L + qb * 128:(hh * HG + hg) * L + (qb + 1) * 128],
                            ks[:, hg * L + kb * 512:hg * L + (kb + 1) * 512],
                            start=(hg == 0), stop=(hg == HG - 1))
                    s_S.inc(i, 1)
                    t.wait_ge(s_P.h, ikb + 1)
                    if ikb >= 1:
                        t.wait_ge(s_pte.h, ikb)
                    for sb_ in range(4):
                        i = t.transpose(ps_pt[:, sb_ * 128:(sb_ + 1) * 128],
                                        Pt[:, sb_ * 128:(sb_ + 1) * 128], ident[:])
                    s_pt.inc(i, 1)
                    t.wait_ge(s_pte.h, ikb + 1)
                    if ikb >= 2:
                        t.wait_ge(s_ost.h, ikb - 1)
                    for sb_ in range(4):
                        kbsub = kb * 4 + sb_
                        i = t.matmul(ps_o[ikb % 2][:],
                                     ptt[:, sb_ * 128:(sb_ + 1) * 128],
                                     vt[:, kbsub * HD:(kbsub + 1) * HD],
                                     start=(sb_ == 0), stop=(sb_ == 3))
                    s_pv.inc(i, 1)
                    ikb += 1
                t.wait_ge(s_fin.h, iq + 1)
                if iq >= 1:
                    t.wait_ge(s_oev.h, iq)
                for g_ in range(HG):
                    i = t.transpose(ps_ot[g_][:, 0:128],
                                    ofin[:, g_ * 128:(g_ + 1) * 128], ident[:])
                    s_otr.inc(i, 1)

    @blk.vector
    def _(v):
        v.wait_ge(s_ci.h, s_ci.n)
        for kb1 in range(L // 128):
            for vg in range(HG):
                ivt = kb1 * HG + vg
                v.wait_ge(s_vt.h, ivt + 1)
                i = v.tensor_copy(vt[:, (kb1 * HG + vg) * 128:(kb1 * HG + vg + 1) * 128],
                                  ps_pt[:, 0:128])
                s_vte.inc(i, 1)
        ikb = 0
        for hh in range(2):
            for qb in range(NQB):
                iq = hh * NQB + qb
                diag_kb = qb // 4
                for kb in range(nkb(qb)):
                    v.wait_ge(s_S.h, ikb + 1)
                    if kb == 0:
                        v.memset(nmb[:, 3:4], 1e30)
                        v.memset(osb[:], 0.0)
                        v.memset(lrow[:], 0.0)
                    else:
                        v.tensor_copy(nmb[:, 3:4], nmb[:, 2:3])
                        v.tensor_copy(ojnk[:], zeros[:, 0:4])
                    if kb == diag_kb:
                        r = qb % 4
                        v.tensor_add(Sm[:], ps_s[ikb % 2][:], masks[:, r * 512:(r + 1) * 512])
                        rsrc = Sm[:]
                        v.tensor_copy(ojnk[:], zeros[:, 0:4])
                    else:
                        rsrc = ps_s[ikb % 2][:]
                    v.reduce_max(nmb[:, 0:1], rsrc, axis=AX.X, negate=True)
                    v.tensor_copy(ojnk[:], zeros[:, 0:4])
                    v.tensor_scalar_mul(nmb[:, 1:2], nmb[:, 0:1], sc)
                    v.tensor_copy(ojnk[:], zeros[:, 0:4])
                    v.tensor_tensor(nmb[:, 2:3], nmb[:, 3:4], nmb[:, 1:2], ALU.min)
                    i = v.tensor_copy(ojnk[:], zeros[:, 0:4])
                    s_nm.inc(i, 1)
                    v.wait_ge(s_pt.h, ikb + 1)
                    if ikb >= 1:
                        v.wait_ge(s_pv.h, ikb)
                    v.tensor_copy(ptt[:], ps_pt[:, 0:512])
                    i = v.tensor_copy(ojnk[:], zeros[:, 0:4])
                    s_pte.inc(i, 1)
                    v.wait_ge(s_fac.h, ikb + 1)
                    v.scalar_tensor_tensor(lrow[:], lrow[:], fac[:, 0:1], lblk[:],
                                           ALU.mult, ALU.add)
                    v.wait_ge(s_pv.h, ikb + 1)
                    i = v.scalar_tensor_tensor(osb[:], osb[:], fac[:, 0:1],
                                               ps_o[ikb % 2][:], ALU.mult, ALU.add)
                    s_ost.inc(i, 1)
                    ikb += 1
                v.tensor_copy(ojnk[:], zeros[:, 0:4])
                v.reciprocal(nmb[:, 0:1], lrow[:])
                v.tensor_copy(ojnk[:], zeros[:, 0:4])
                v.tensor_scalar_mul(ofin[:], osb[:], nmb[:, 0:1])
                i = v.tensor_copy(ojnk[:], zeros[:, 0:4])
                s_fin.inc(i, 1)
                v.wait_ge(s_otr.h, (iq + 1) * 2)
                if iq >= 1:
                    v.wait_ge(s_go.h, iq * 16)
                for g_ in range(HG):
                    v.tensor_copy(oev[:, g_ * 128:(g_ + 1) * 128], ps_ot[g_][:, 0:128])
                i = v.tensor_copy(ojnk[:], zeros[:, 0:4])
                s_oev.inc(i, 1)

    @blk.scalar
    def _(s):
        s.wait_ge(s_ci.h, s_ci.n)
        ikb = 0
        for hh in range(2):
            for qb in range(NQB):
                diag_kb = qb // 4
                for kb in range(nkb(qb)):
                    s.wait_ge(s_nm.h, ikb + 1)
                    if ikb >= 1:
                        s.wait_ge(s_pt.h, ikb)
                    rsrc = Sm[:] if kb == diag_kb else ps_s[ikb % 2][:]
                    s.activation(Pt[:], rsrc, AF.Exp, scale=sc, bias=nmb[:, 2:3],
                                 accum_out=lblk[:])
                    s.activation(fac[:, 0:1], nmb[:, 3:4], AF.Exp, scale=-1.0,
                                 bias=nmb[:, 2:3])
                    i = s.copy(ajnk[:, 0:1], nmb[:, 0:1])
                    s_fac.inc(i, 1)
                    i = s.copy(ajnk[:, 1:2], nmb[:, 0:1])
                    s_P.inc(i, 1)
                    ikb += 1

    @blk.sync
    def _(sync):
        for hh in range(2):
            for qb in range(NQB):
                iq = hh * NQB + qb
                sync.wait_ge(s_oev.h, iq + 1)
                s_go.inc(sync.dma_start(
                    out=dram_fm(o_hbm, qb * 128, 128, HG, g0=hh * HG),
                    in_=sb_fm(oev, 128, HG)), 16)
        sync.wait_ge(s_go.h, s_go.n)
    return s_go


def emit_addstream(P, a_hbm, b_hbm, out_hbm, R, C, name="ad"):
    nc, blk = P.nc, P.blk
    CH = 2048
    nch = C // CH
    s_l = [P.sem(name), P.sem(name)]
    s_v = P.sem(name)
    s_o = P.sem(name)
    at = [P.sb([R, CH], F32, name + "a") for _ in range(2)]
    bt = [P.sb([R, CH], F32, name + "b") for _ in range(2)]
    ot = [P.sb([R, CH], F32, name + "o") for _ in range(2)]

    @blk.sync
    def _(sync):
        for j in range(nch):
            if j >= 2:
                sync.wait_ge(s_v.h, j - 1)
            s_l[j % 2].inc(sync.dma_start(out=at[j % 2][:],
                                          in_=_a(a_hbm)[:, j * CH:(j + 1) * CH]), 16)
            s_l[j % 2].inc(sync.dma_start(out=bt[j % 2][:],
                                          in_=_a(b_hbm)[:, j * CH:(j + 1) * CH]), 16)
            if j >= 1:
                sync.wait_ge(s_v.h, j)
                s_o.inc(sync.dma_start(out=_a(out_hbm)[:, (j - 1) * CH:j * CH],
                                       in_=ot[(j - 1) % 2][:]), 16)
        sync.wait_ge(s_v.h, nch)
        s_o.inc(sync.dma_start(out=_a(out_hbm)[:, (nch - 1) * CH:nch * CH],
                               in_=ot[(nch - 1) % 2][:]), 16)
        sync.wait_ge(s_o.h, s_o.n)

    @blk.vector
    def _(v):
        for j in range(nch):
            v.wait_ge(s_l[j % 2].h, ((j // 2) + 1) * 32)
            if j >= 2:
                v.wait_ge(s_o.h, (j - 1) * 16)
            i = v.tensor_add(ot[j % 2][:], at[j % 2][:], bt[j % 2][:])
            s_v.inc(i, 1)
    return s_o


def build_launch_D():
    P = Prog("D")
    nc = P.nc
    x1 = nc.dram_tensor("x1", [D, TL], F32, kind="ExternalInput").ap()
    nmw = nc.dram_tensor("nmw", [128, G], F32, kind="ExternalInput").ap()
    xn1b = nc.dram_tensor("xn1b", [D, TL], BF16, kind="ExternalOutput").ap()
    xn1f = nc.dram_tensor("xn1f", [D, TL], F32, kind="ExternalOutput").ap()
    xm1 = nc.dram_tensor("xm1", [D, TL], BF16, kind="ExternalOutput").ap()
    rows1 = nc.dram_tensor("rows1", [2, TL], F32, kind="Internal").ap()
    rows2 = nc.dram_tensor("rows2", [2, TL], F32, kind="Internal").ap()
    P.start_block()
    with ExitStack() as pctx:
        P.pctx = pctx
        emit_ln(P, x1, xn1b, rows1, xn_f32_hbm=xn1f, ncols=TL, name="ln1")
    P.barrier()
    with ExitStack() as pctx:
        P.pctx = pctx
        emit_ln(P, xn1f, xm1, rows2, ncols=TL, rms=True, eps=RMS_EPS,
                gamma_hbm=nmw, name="rm1")
    P.gctx.close()
    return nc


def build_launch_D2():
    P = Prog("D2")
    nc = P.nc
    xm1 = nc.dram_tensor("xm1", [D, L], BF16, kind="ExternalInput").ap()
    w1_in_x = nc.dram_tensor("w1_in_x", [D, DIH], F32, kind="ExternalInput").ap()
    w1_in_z = nc.dram_tensor("w1_in_z", [D, DIH], F32, kind="ExternalInput").ap()
    conv1_w = nc.dram_tensor("conv1_w", [128, GI * 4], F32, kind="ExternalInput").ap()
    conv1_b = nc.dram_tensor("conv1_b", [128, GI], F32, kind="ExternalInput").ap()
    w1_x = nc.dram_tensor("w1_x", [DIH, 33], F32, kind="ExternalInput").ap()
    u1 = nc.dram_tensor("u1", [DIH, L], F32, kind="ExternalOutput").ap()
    z1 = nc.dram_tensor("z1", [DIH, L], BF16, kind="ExternalOutput").ap()
    sp1 = nc.dram_tensor("sp1", [33, L], F32, kind="ExternalOutput").ap()
    P.start_block()
    with ExitStack() as pctx:
        P.pctx = pctx
        emit_mamba_front(P, xm1, w1_in_x, w1_in_z, conv1_w, conv1_b, w1_x,
                         u1, z1, sp1, Rsp=33, name="mf1")
    P.gctx.close()
    return nc


def build_launch_SPADD(Rsp):
    P = Prog("SP")
    nc = P.nc
    spa = nc.dram_tensor("spa", [Rsp, L], F32, kind="ExternalInput").ap()
    spb = nc.dram_tensor("spb", [Rsp, L], F32, kind="ExternalInput").ap()
    spo = nc.dram_tensor("spo", [Rsp, L], F32, kind="ExternalOutput").ap()
    P.start_block()
    with ExitStack() as pctx:
        P.pctx = pctx
        emit_addstream(P, spa, spb, spo, Rsp, L, name="sa")
    P.gctx.close()
    return nc


def build_launch_F():
    P = Prog("F")
    nc = P.nc
    g1 = nc.dram_tensor("g1", [DI, TL], BF16, kind="ExternalInput").ap()
    wop = nc.dram_tensor("wop", [DI, D], F32, kind="ExternalInput").ap()
    xn1f = nc.dram_tensor("xn1f", [D, TL], F32, kind="ExternalInput").ap()
    naw = nc.dram_tensor("naw", [128, G], F32, kind="ExternalInput").ap()
    m = nc.dram_tensor("m", [D, TL], F32, kind="ExternalOutput").ap()
    xn2 = nc.dram_tensor("xn2", [D, TL], BF16, kind="ExternalOutput").ap()
    rows = nc.dram_tensor("rows", [2, TL], F32, kind="Internal").ap()
    P.start_block()
    with ExitStack() as pctx:
        P.pctx = pctx
        emit_proj(P, g1, wop, m, DI, D, TL, res_hbm=xn1f, name="op1")
    P.barrier()
    with ExitStack() as pctx:
        P.pctx = pctx
        emit_ln(P, m, xn2, rows, ncols=TL, rms=True, eps=RMS_EPS,
                gamma_hbm=naw, name="rm2")
    P.gctx.close()
    return nc


def build_launch_F2():
    P = Prog("F2")
    nc = P.nc
    xn2 = nc.dram_tensor("xn2", [D, L], BF16, kind="ExternalInput").ap()
    qkvw = nc.dram_tensor("qkvw", [D, 1024], F32, kind="ExternalInput").ap()
    qkv = nc.dram_tensor("qkv", [1024, L], BF16, kind="ExternalOutput").ap()
    P.start_block()
    with ExitStack() as pctx:
        P.pctx = pctx
        emit_proj(P, xn2, qkvw, qkv, D, 1024, L, out_f32=False, name="qkv")
    P.gctx.close()
    return nc


def build_launch_G():
    P = Prog("G")
    nc = P.nc
    qkv = nc.dram_tensor("qkv", [1024, L], BF16, kind="ExternalInput").ap()
    o = nc.dram_tensor("o", [512, L], BF16, kind="ExternalOutput").ap()
    P.start_block()
    with ExitStack() as pctx:
        P.pctx = pctx
        emit_attn(P, qkv[0:512, :], qkv[512:768, :], qkv[768:1024, :], o, name="at")
    P.gctx.close()
    return nc


def build_launch_H(dump=False):
    P = Prog("H")
    nc = P.nc
    of = nc.dram_tensor("of", [D, TL], BF16, kind="ExternalInput").ap()
    wo2 = nc.dram_tensor("wo2", [D, D], F32, kind="ExternalInput").ap()
    m = nc.dram_tensor("m", [D, TL], F32, kind="ExternalInput").ap()
    nfw = nc.dram_tensor("nfw", [128, G], F32, kind="ExternalInput").ap()
    gwp = nc.dram_tensor("gwp", [D, 2816], F32, kind="ExternalInput").ap()
    uwp = nc.dram_tensor("uwp", [D, 2816], F32, kind="ExternalInput").ap()
    dwp = nc.dram_tensor("dwp", [2816, D], F32, kind="ExternalInput").ap()
    x1 = nc.dram_tensor("x1", [D, TL], F32, kind="ExternalInput").ap()
    xn1b = nc.dram_tensor("xn1b", [D, TL], BF16, kind="ExternalInput").ap()
    xn1f = nc.dram_tensor("xn1f", [D, TL], F32, kind="ExternalInput").ap()
    lng = nc.dram_tensor("lng", [128, G], F32, kind="ExternalInput").ap()
    lnb = nc.dram_tensor("lnb", [128, G], F32, kind="ExternalInput").ap()
    wr1 = nc.dram_tensor("wr1", [D, 8], F32, kind="ExternalInput").ap()
    br1 = nc.dram_tensor("br1", [1, 8], F32, kind="ExternalInput").ap()
    we1 = nc.dram_tensor("we1", [8, D, 4096], F32, kind="ExternalInput").ap()
    we2 = nc.dram_tensor("we2", [8, 4096, D], F32, kind="ExternalInput").ap()
    b1 = nc.dram_tensor("b1", [128, 8 * 32], F32, kind="ExternalInput").ap()
    b2 = nc.dram_tensor("b2", [128, 8 * 8], F32, kind="ExternalInput").ap()
    x2 = nc.dram_tensor("x2", [D, TL], F32, kind="ExternalOutput").ap()
    kd = "ExternalOutput" if dump else "Internal"
    aout = nc.dram_tensor("aout", [D, TL], F32, kind=kd).ap()
    xn3 = nc.dram_tensor("xn3", [D, TL], BF16, kind=kd).ap()
    hsw = nc.dram_tensor("hsw", [2816, TL], BF16, kind=kd).ap()
    xnln = nc.dram_tensor("xnln", [D, TL], BF16, kind=kd).ap()
    mg = nc.dram_tensor("mg", [8, TL], F32, kind=kd).ap()
    rows = nc.dram_tensor("rows", [2, TL], F32, kind="Internal").ap()
    rows2 = nc.dram_tensor("rows2", [2, TL], F32, kind="Internal").ap()
    P.start_block()
    with ExitStack() as pctx:
        P.pctx = pctx
        emit_proj(P, of, wo2, aout, D, D, TL, res_hbm=m, name="op2")
    P.barrier()
    with ExitStack() as pctx:
        P.pctx = pctx
        emit_ln(P, aout, xn3, rows, ncols=TL, rms=True, eps=RMS_EPS,
                gamma_hbm=nfw, name="rm3")
    P.barrier()
    with ExitStack() as pctx:
        P.pctx = pctx
        emit_swiglu(P, xn3, gwp, uwp, hsw, 2816, T=TL, name="sw")
    P.barrier()
    with ExitStack() as pctx:
        P.pctx = pctx
        emit_proj(P, hsw, dwp, x2, 2816, D, TL, res_hbm=aout, res2_hbm=x1, name="dn")
    P.barrier()
    with ExitStack() as pctx:
        P.pctx = pctx
        emit_ln(P, xn1f, xnln, rows2, ncols=TL, gamma_hbm=lng, beta_hbm=lnb,
                eps=1e-5, name="iln")
    P.barrier()
    with ExitStack() as pctx:
        P.pctx = pctx
        emit_router(P, xn1b, wr1, None, br1, mg, "l1", T=TL, name="rt1")
    P.barrier()
    with ExitStack() as pctx:
        P.pctx = pctx
        emit_moe(P, xnln, we1, None, we2, mg, x2, two_mats=False,
                 b1_hbm=b1, b2_hbm=b2, T=TL, name="moe1")
    P.gctx.close()
    return nc

def emit_xchg(P, wrl0, wrl1, rdl0, rdl1, name):
    nc, blk = P.nc, P.blk
    assert len(wrl0) == len(wrl1) and len(rdl0) == len(rdl1)
    I32 = mybir.dt.int32
    flg = nc.dram_tensor(name + "_flg", [2, 16], I32, kind="Internal",
                         addr_space="Shared").ap()
    s_d = P.sem(name + "d")
    s_p = P.sem(name + "p", pooled=False)
    if not hasattr(P, "_ep"):
        P._ep = P.gsb([1, 16], I32, "ep")
        P._zz = P.gsb([1, 16], I32, "zz")
        P._fl = P.gsb([1, 16], I32, "fl")
        P._ep_init = False
    ep, zz, fl = P._ep, P._zz, P._fl
    base = s_d.base
    nw, nr = len(wrl0), len(rdl0)
    first = not P._ep_init
    P._ep_init = True

    @blk.gpsimd
    def _(g):
        raw = s_d.raw
        if first:
            g.memset(ep[:], 7)
            g.memset(zz[:], 0)
        core = g.to_reg(nc.partition_id())
        with g.register(name + "par") as par:
            g.reg_alu(par, core, 1, ALU.bitwise_and)
            with g.If_eq(par, 0):
                for (o, i_) in wrl0:
                    g.dma_start(out=o, in_=i_).then_inc(raw, 16)
            with g.Else():
                for (o, i_) in wrl1:
                    g.dma_start(out=o, in_=i_).then_inc(raw, 16)
            g.wait_ge(raw, base + nw * 16)
            with g.If_eq(par, 0):
                g.dma_start(out=flg[0:1, :], in_=ep[:]).then_inc(raw, 16)
            with g.Else():
                g.dma_start(out=flg[1:2, :], in_=ep[:]).then_inc(raw, 16)
            g.wait_ge(raw, base + (nw + 1) * 16)
            with g.register(name + "c") as cond, g.register(name + "t") as tgt:
                g.reg_mov(cond, 1)
                g.reg_mov(tgt, 0)
                with g.While(cond):
                    g.reg_add(tgt, tgt, 16)
                    with g.If_eq(par, 0):
                        g.dma_start(out=fl[:], in_=flg[1:2, :]).then_inc(s_p.raw, 16)
                    with g.Else():
                        g.dma_start(out=fl[:], in_=flg[0:1, :]).then_inc(s_p.raw, 16)
                    g.wait_ge(s_p.raw, tgt)
                    v = g.value_load(fl[0:1, 0:1])
                    g.reg_alu(cond, v, 7, ALU.not_equal)
                    with g.register(name + "b") as bound:
                        g.reg_alu(bound, tgt, 16 * 200000, ALU.is_lt)
                        g.reg_alu(cond, cond, bound, ALU.bitwise_and)
            with g.If_eq(par, 0):
                for (o, i_) in rdl0:
                    g.dma_start(out=o, in_=i_).then_inc(raw, 16)
            with g.Else():
                for (o, i_) in rdl1:
                    g.dma_start(out=o, in_=i_).then_inc(raw, 16)
            g.wait_ge(raw, base + (nw + 1 + nr) * 16)
            with g.If_eq(par, 0):
                g.dma_start(out=flg[1:2, :], in_=zz[:]).then_inc(raw, 16)
            with g.Else():
                g.dma_start(out=flg[0:1, :], in_=zz[:]).then_inc(raw, 16)
            g.wait_ge(raw, base + (nw + 2 + nr) * 16)
    s_d.n = (nw + 2 + nr) * 16


def build_launch_ALL():
    P = Prog("ALL")
    nc = P.nc
    EI = "ExternalInput"
    D_ = lambda n, sh, dt=F32, kind="Internal", **kw: nc.dram_tensor(
        n, sh, dt, kind=kind, **kw).ap()
    x0 = D_("x0", [D, L], F32, EI)
    w0_in_x = D_("w0_in_x", [D, DIH], F32, EI)
    w0_in_z = D_("w0_in_z", [D, DIH], F32, EI)
    conv0_w = D_("conv0_w", [128, GI * 4], F32, EI)
    conv0_b = D_("conv0_b", [128, GI], F32, EI)
    w0_x = D_("w0_x", [DIH, 96], F32, EI)
    wdt0 = D_("wdt0", [64, DIH], F32, EI)
    bdt0 = D_("bdt0", [128, GI], F32, EI)
    alog0 = D_("alog0", [128, GI * NST], F32, EI)
    dvec0 = D_("dvec0", [128, GI], F32, EI)
    wo0 = D_("wo0", [DI, D], F32, EI)
    wrt0 = D_("wr0", [D, 8], F32, EI)
    wg0 = D_("wg0", [D, 8], F32, EI)
    weg0 = D_("weg0", [8, D, 4096], F32, EI)
    weu0 = D_("weu0", [8, D, 4096], F32, EI)
    wed0 = D_("wed0", [8, 4096, D], F32, EI)
    res0 = D_("res0", [D, TL], F32, EI)
    nmw = D_("nmw", [128, G], F32, EI)
    w1_in_x = D_("w1_in_x", [D, DIH], F32, EI)
    w1_in_z = D_("w1_in_z", [D, DIH], F32, EI)
    conv1_w = D_("conv1_w", [128, GI * 4], F32, EI)
    conv1_b = D_("conv1_b", [128, GI], F32, EI)
    w1_x = D_("w1_x", [DIH, 33], F32, EI)
    dtw1 = D_("dtw1", [128, GI], F32, EI)
    bdt1 = D_("bdt1", [128, GI], F32, EI)
    alog1 = D_("alog1", [128, GI * NST], F32, EI)
    dvec1 = D_("dvec1", [128, GI], F32, EI)
    wop = D_("wop", [DI, D], F32, EI)
    naw = D_("naw", [128, G], F32, EI)
    qkvw = D_("qkvw", [D, 1024], F32, EI)
    wo2 = D_("wo2", [D, D], F32, EI)
    nfw = D_("nfw", [128, G], F32, EI)
    gwp = D_("gwp", [D, 2816], F32, EI)
    uwp = D_("uwp", [D, 2816], F32, EI)
    dwp = D_("dwp", [2816, D], F32, EI)
    lng = D_("lng", [128, G], F32, EI)
    lnb = D_("lnb", [128, G], F32, EI)
    wr1_w = D_("wr1", [D, 8], F32, EI)
    br1 = D_("br1", [1, 8], F32, EI)
    we1 = D_("we1", [8, D, 4096], F32, EI)
    we2 = D_("we2", [8, 4096, D], F32, EI)
    b1 = D_("b1", [128, 8 * 32], F32, EI)
    b2 = D_("b2", [128, 8 * 8], F32, EI)
    x2 = D_("x2", [D, TL], F32, "ExternalOutput")
    xn0 = D_("xn0", [D, L], BF16)
    rows0 = D_("rows0", [2, L])
    u0 = D_("u0", [DIH, L])
    z0 = D_("z0", [DIH, L], BF16)
    sp0 = D_("sp0", [96, L])
    spb0 = D_("spb0", [96, L])
    spo0 = D_("spo0", [96, L])
    gout0 = D_("gout0", [DIH, L], BF16)
    gf0 = D_("gf0", [DI, TL], BF16)
    xnh = D_("xnh", [D, TL], BF16)
    cw0 = D_("cw0", [8, TL])
    x1 = D_("x1", [D, TL])
    xn1b = D_("xn1b", [D, TL], BF16)
    xn1f = D_("xn1f", [D, TL])
    xm1 = D_("xm1", [D, TL], BF16)
    xmf = D_("xmf", [D, L], BF16)
    rows1 = D_("rows1", [2, TL])
    rows2 = D_("rows2", [2, TL])
    rows3 = D_("rows3", [2, TL])
    rows4 = D_("rows4", [2, TL])
    rows5 = D_("rows5", [2, TL])
    u1 = D_("u1", [DIH, L])
    z1 = D_("z1", [DIH, L], BF16)
    sp1 = D_("sp1", [33, L])
    spb1 = D_("spb1", [33, L])
    spo1 = D_("spo1", [33, L])
    gout1 = D_("gout1", [DIH, L], BF16)
    g1f = D_("g1f", [DI, TL], BF16)
    m_s = D_("m", [D, TL])
    xn2 = D_("xn2", [D, TL], BF16)
    xn2f = D_("xn2f", [D, L], BF16)
    qkv = D_("qkv", [1024, L], BF16)
    o_s = D_("o", [512, L], BF16)
    of_s = D_("of", [D, TL], BF16)
    aout = D_("aout", [D, TL])
    xn3 = D_("xn3", [D, TL], BF16)
    hsw = D_("hsw", [2816, TL], BF16)
    xnln = D_("xnln", [D, TL], BF16)
    mg = D_("mg", [8, TL])
    SH = dict(kind="Internal", addr_space="Shared")
    shsp0 = D_("shsp0", [192, L], F32, **SH)
    shg0 = D_("shg0", [DI, TL], BF16, **SH)
    shxm = D_("shxm", [DI, TL], BF16, **SH)
    shsp1 = D_("shsp1", [66, L], F32, **SH)
    shg1 = D_("shg1", [DI, TL], BF16, **SH)
    shx2 = D_("shx2", [DI, TL], BF16, **SH)
    sho = D_("sho", [1024, TL], BF16, **SH)

    P.start_block()

    def phase(fn):
        with ExitStack() as pctx:
            P.pctx = pctx
            fn()
        P.end_phase()

    H = DIH
    phase(lambda: emit_ln(P, x0, xn0, rows0, ncols=L, name="ln0"))
    phase(lambda: emit_mamba_front(P, xn0, w0_in_x, w0_in_z, conv0_w, conv0_b,
                                   w0_x, u0, z0, sp0, Rsp=96, name="mf0"))
    phase(lambda: emit_xchg(
        P, [(shsp0[0:96, :], sp0)], [(shsp0[96:192, :], sp0)],
        [(spb0, shsp0[96:192, :])], [(spb0, shsp0[0:96, :])], "xsp0"))
    phase(lambda: emit_addstream(P, sp0, spb0, spo0, 96, L, name="sa0"))
    phase(lambda: emit_scan(P, u0, z0, spo0, gout0, wdt0, alog0, bdt0, dvec0,
                            None, 0, name="sc0"))
    phase(lambda: emit_xchg(
        P,
        [(shg0[0:H, :], gout0[:, TL:2 * TL]), (gf0[0:H, :], gout0[:, 0:TL]),
         (xnh, xn0[:, 0:TL])],
        [(shg0[H:2 * H, :], gout0[:, 0:TL]), (gf0[H:2 * H, :], gout0[:, TL:2 * TL]),
         (xnh, xn0[:, TL:2 * TL])],
        [(gf0[H:2 * H, :], shg0[H:2 * H, :])],
        [(gf0[0:H, :], shg0[0:H, :])], "xg0"))
    phase(lambda: emit_router(P, xnh, wrt0, wg0, None, cw0, "l0", T=TL, name="rt0"))
    phase(lambda: emit_proj(P, gf0, wo0, x1, DI, D, TL, res_hbm=res0, name="wo0"))
    phase(lambda: emit_moe(P, xnh, weg0, weu0, wed0, cw0, x1, two_mats=True,
                           T=TL, name="moe0"))
    phase(lambda: emit_ln(P, x1, xn1b, rows1, xn_f32_hbm=xn1f, ncols=TL, name="ln1"))
    phase(lambda: emit_ln(P, xn1f, xm1, rows2, ncols=TL, rms=True, eps=RMS_EPS,
                          gamma_hbm=nmw, name="rm1"))
    phase(lambda: emit_xchg(
        P, [(shxm[0:H, :], xm1), (xmf[:, 0:TL], xm1)],
        [(shxm[H:2 * H, :], xm1), (xmf[:, TL:2 * TL], xm1)],
        [(xmf[:, TL:2 * TL], shxm[H:2 * H, :])],
        [(xmf[:, 0:TL], shxm[0:H, :])], "xxm"))
    phase(lambda: emit_mamba_front(P, xmf, w1_in_x, w1_in_z, conv1_w, conv1_b,
                                   w1_x, u1, z1, sp1, Rsp=33, name="mf1"))
    phase(lambda: emit_xchg(
        P, [(shsp1[0:33, :], sp1)], [(shsp1[33:66, :], sp1)],
        [(spb1, shsp1[33:66, :])], [(spb1, shsp1[0:33, :])], "xsp1"))
    phase(lambda: emit_addstream(P, sp1, spb1, spo1, 33, L, name="sa1"))
    phase(lambda: emit_scan(P, u1, z1, spo1, gout1, None, alog1, bdt1, dvec1,
                            dtw1, 1, name="sc1"))
    phase(lambda: emit_xchg(
        P,
        [(shg1[0:H, :], gout1[:, TL:2 * TL]), (g1f[0:H, :], gout1[:, 0:TL])],
        [(shg1[H:2 * H, :], gout1[:, 0:TL]), (g1f[H:2 * H, :], gout1[:, TL:2 * TL])],
        [(g1f[H:2 * H, :], shg1[H:2 * H, :])],
        [(g1f[0:H, :], shg1[0:H, :])], "xg1"))
    phase(lambda: emit_proj(P, g1f, wop, m_s, DI, D, TL, res_hbm=xn1f, name="op1"))
    phase(lambda: emit_ln(P, m_s, xn2, rows3, ncols=TL, rms=True, eps=RMS_EPS,
                          gamma_hbm=naw, name="rm2"))
    phase(lambda: emit_xchg(
        P, [(shx2[0:H, :], xn2), (xn2f[:, 0:TL], xn2)],
        [(shx2[H:2 * H, :], xn2), (xn2f[:, TL:2 * TL], xn2)],
        [(xn2f[:, TL:2 * TL], shx2[H:2 * H, :])],
        [(xn2f[:, 0:TL], shx2[0:H, :])], "xx2"))
    phase(lambda: emit_proj(P, xn2f, qkvw, qkv, D, 1024, L, out_f32=False,
                            name="qkv"))
    phase(lambda: emit_attn(P, qkv[0:512, :], qkv[512:768, :], qkv[768:1024, :],
                            o_s, name="at"))
    phase(lambda: emit_xchg(
        P,
        [(sho[0:512, :], o_s[:, TL:2 * TL]), (of_s[0:512, :], o_s[:, 0:TL])],
        [(sho[512:1024, :], o_s[:, 0:TL]), (of_s[512:1024, :], o_s[:, TL:2 * TL])],
        [(of_s[512:1024, :], sho[512:1024, :])],
        [(of_s[0:512, :], sho[0:512, :])], "xo"))
    phase(lambda: emit_proj(P, of_s, wo2, aout, D, D, TL, res_hbm=m_s, name="op2"))
    phase(lambda: emit_ln(P, aout, xn3, rows4, ncols=TL, rms=True, eps=RMS_EPS,
                          gamma_hbm=nfw, name="rm3"))
    phase(lambda: emit_swiglu(P, xn3, gwp, uwp, hsw, 2816, T=TL, name="sw"))
    phase(lambda: emit_proj(P, hsw, dwp, x2, 2816, D, TL, res_hbm=aout,
                            res2_hbm=x1, name="dn"))
    phase(lambda: emit_ln(P, xn1f, xnln, rows5, ncols=TL, gamma_hbm=lng,
                          beta_hbm=lnb, eps=1e-5, name="iln"))
    phase(lambda: emit_router(P, xn1b, wr1_w, None, br1, mg, "l1", T=TL,
                              name="rt1"))
    phase(lambda: emit_moe(P, xnln, we1, None, we2, mg, x2, two_mats=False,
                           b1_hbm=b1, b2_hbm=b2, T=TL, name="moe1"))
    P.gctx.close()
    return nc


def build_launch_C1(layer=0):
    P = Prog("C1")
    nc = P.nc
    gf = nc.dram_tensor("gf", [DI, TL], BF16, kind="ExternalInput").ap()
    wo = nc.dram_tensor("wo", [DI, D], F32, kind="ExternalInput").ap()
    res = nc.dram_tensor("res", [D, TL], F32, kind="ExternalInput").ap()
    x1 = nc.dram_tensor("x1", [D, TL], F32, kind="ExternalOutput").ap()
    P.start_block()
    with ExitStack() as pctx:
        P.pctx = pctx
        emit_proj(P, gf, wo, x1, DI, D, TL, res_hbm=res, name="wo")
    P.gctx.close()
    return nc

def pcol(a):
    a = np.asarray(a)
    if a.ndim == 1:
        a = a[:, None]
    n = a.shape[0] // 128
    return np.ascontiguousarray(a.reshape(n, 128, -1).transpose(1, 0, 2).reshape(128, -1))


def prep_inputs(inp):
    per = []
    A = lambda k: np.asarray(inp[k])
    xT = np.ascontiguousarray(A("L0_attn_hidden_states").transpose(0, 2, 1))
    w_in = A("L0_attn_W_in")
    conv0 = A("L0_attn_conv_w")
    w_x0 = A("L0_attn_W_x")
    in1T = np.ascontiguousarray(A("L1_attn_in_proj_w").T)
    conv1 = A("L1_attn_conv_w")[:, 0, :]
    xp1T = np.ascontiguousarray(A("L1_attn_x_proj_w").T)
    op1T = np.ascontiguousarray(A("L1_attn_out_proj_w").T)
    qT = np.ascontiguousarray(A("L1_attn_q_w").T)
    kT = np.ascontiguousarray(A("L1_attn_k_w").T)
    vT = np.ascontiguousarray(A("L1_attn_v_w").T)
    oT = np.ascontiguousarray(A("L1_attn_o_w").T)
    gwT = np.ascontiguousarray(A("L1_attn_gate_w").T)
    uwT = np.ascontiguousarray(A("L1_attn_up_w").T)
    dwT = np.ascontiguousarray(A("L1_attn_down_w").T)
    gwp = np.pad(gwT, ((0, 0), (0, 2816 - 2731)))
    uwp = np.pad(uwT, ((0, 0), (0, 2816 - 2731)))
    dwp = np.pad(dwT, ((0, 0), (0, 0))) if dwT.shape[0] == 2816 else \
        np.pad(dwT, ((0, 2816 - 2731), (0, 0)))
    b1all = np.concatenate([pcol(A("L1_ffn_b1")[e]) for e in range(8)], axis=1)
    b2all = np.concatenate([pcol(A("L1_ffn_b2")[e]) for e in range(8)], axis=1)
    for c in range(NCORES):
        b, q = divmod(c, 2)
        sl = slice(q * DIH, (q + 1) * DIH)
        d = {
            "x0": np.ascontiguousarray(xT[b]),
            "res0": np.ascontiguousarray(xT[b][:, q * TL:(q + 1) * TL]),
            "w0_in_x": np.ascontiguousarray(w_in[:, q * DIH:(q + 1) * DIH]),
            "w0_in_z": np.ascontiguousarray(w_in[:, DI + q * DIH:DI + (q + 1) * DIH]),
            "conv0_w": pcol(conv0[:, sl].T),
            "conv0_b": pcol(A("L0_attn_conv_b")[sl]),
            "w0_x": np.ascontiguousarray(w_x0[sl, :]),
            "wdt0": np.ascontiguousarray(A("L0_attn_W_dt")[:, sl]),
            "bdt0": pcol(A("L0_attn_b_dt")[sl]),
            "alog0": pcol(A("L0_attn_A_log")[sl]),
            "dvec0": pcol(A("L0_attn_D")[sl]),
            "wo0": A("L0_attn_W_out"),
            "wr0": A("L0_ffn_w_router"),
            "wg0": A("L0_ffn_w_gating"),
            "weg0": A("L0_ffn_w_gate"),
            "weu0": A("L0_ffn_w_up"),
            "wed0": A("L0_ffn_w_down"),
            "nmw": pcol(A("L1_attn_norm_m_w")),
            "w1_in_x": np.ascontiguousarray(in1T[:, q * DIH:(q + 1) * DIH]),
            "w1_in_z": np.ascontiguousarray(in1T[:, DI + q * DIH:DI + (q + 1) * DIH]),
            "conv1_w": pcol(conv1[sl, :]),
            "conv1_b": pcol(A("L1_attn_conv_b")[sl]),
            "w1_x": np.ascontiguousarray(xp1T[sl, :]),
            "dtw1": pcol(A("L1_attn_dt_w")[sl, 0]),
            "bdt1": pcol(A("L1_attn_dt_b")[sl]),
            "alog1": pcol(A("L1_attn_A_log")[sl]),
            "dvec1": pcol(A("L1_attn_D_skip")[sl]),
            "wop": op1T,
            "naw": pcol(A("L1_attn_norm_attn_w")),
            "qkvw": np.ascontiguousarray(np.concatenate(
                [qT[:, q * 512:(q + 1) * 512],
                 kT[:, q * 256:(q + 1) * 256],
                 vT[:, q * 256:(q + 1) * 256]], axis=1)),
            "wo2": oT,
            "nfw": pcol(A("L1_attn_norm_ffn_w")),
            "gwp": gwp, "uwp": uwp, "dwp": dwp,
            "lng": pcol(A("L1_ffn_ln_g")),
            "lnb": pcol(A("L1_ffn_ln_b")),
            "wr1": A("L1_ffn_Wr"),
            "br1": A("L1_ffn_br").reshape(1, 8),
            "we1": A("L1_ffn_W1"),
            "we2": A("L1_ffn_W2"),
            "b1": b1all, "b2": b2all,
        }
        per.append(d)
    return per


def build_launch_A(only_ln=False):
    P = Prog("A")
    nc = P.nc
    x0 = nc.dram_tensor("x0", [D, L], F32, kind="ExternalInput").ap()
    w0_in_x = nc.dram_tensor("w0_in_x", [D, DIH], F32, kind="ExternalInput").ap()
    w0_in_z = nc.dram_tensor("w0_in_z", [D, DIH], F32, kind="ExternalInput").ap()
    conv0_w = nc.dram_tensor("conv0_w", [128, GI * 4], F32, kind="ExternalInput").ap()
    conv0_b = nc.dram_tensor("conv0_b", [128, GI], F32, kind="ExternalInput").ap()
    w0_x = nc.dram_tensor("w0_x", [DIH, 96], F32, kind="ExternalInput").ap()
    xn0 = nc.dram_tensor("xn0", [D, L], BF16, kind="ExternalOutput").ap()
    u0 = nc.dram_tensor("u0", [DIH, L], F32, kind="ExternalOutput").ap()
    z0 = nc.dram_tensor("z0", [DIH, L], BF16, kind="ExternalOutput").ap()
    sp0 = nc.dram_tensor("sp0", [96, L], F32, kind="ExternalOutput").ap()

    rows0 = nc.dram_tensor("rows0", [2, L], F32, kind="Internal").ap()
    P.start_block()
    with ExitStack() as pctx:
        P.pctx = pctx
        emit_ln(P, x0, xn0, rows0, ncols=L, name="ln0")
    if not only_ln:
        P.barrier()
        with ExitStack() as pctx:
            P.pctx = pctx
            emit_mamba_front(P, xn0, w0_in_x, w0_in_z, conv0_w, conv0_b, w0_x,
                             u0, z0, sp0, Rsp=96, name="mf0")
    P.gctx.close()
    return nc


def build_launch_B(layer):
    P = Prog("B")
    nc = P.nc
    u = nc.dram_tensor("u", [DIH, L], F32, kind="ExternalInput").ap()
    z = nc.dram_tensor("z", [DIH, L], BF16, kind="ExternalInput").ap()
    Rsp = 96 if layer == 0 else 33
    sp = nc.dram_tensor("sp", [Rsp, L], F32, kind="ExternalInput").ap()
    alog = nc.dram_tensor("alog", [128, GI * NST], F32, kind="ExternalInput").ap()
    bdt = nc.dram_tensor("bdt", [128, GI], F32, kind="ExternalInput").ap()
    dvec = nc.dram_tensor("dvec", [128, GI], F32, kind="ExternalInput").ap()
    if layer == 0:
        wdt = nc.dram_tensor("wdt", [64, DIH], F32, kind="ExternalInput").ap()
        dtw = None
    else:
        dtw = nc.dram_tensor("dtw", [128, GI], F32, kind="ExternalInput").ap()
        wdt = None
    gout = nc.dram_tensor("gout", [DIH, L], BF16, kind="ExternalOutput").ap()

    P.start_block()
    with ExitStack() as pctx:
        P.pctx = pctx
        emit_scan(P, u, z, sp, gout, wdt, alog, bdt, dvec, dtw, layer, name="sc")
    P.gctx.close()
    return nc


def build_launch_C(layer=0, parts="rpm", cw_input=False):
    P = Prog("C")
    nc = P.nc
    xn = nc.dram_tensor("xn", [D, TL], BF16, kind="ExternalInput").ap()
    gf = nc.dram_tensor("gf", [DI, TL], BF16, kind="ExternalInput").ap()
    wo = nc.dram_tensor("wo", [DI, D], F32, kind="ExternalInput").ap()
    res = nc.dram_tensor("res", [D, TL], F32, kind="ExternalInput").ap()
    wr = nc.dram_tensor("wr", [D, 8], F32, kind="ExternalInput").ap()
    wg2 = nc.dram_tensor("wg2", [D, 8], F32, kind="ExternalInput").ap()
    weg = nc.dram_tensor("weg", [8, D, 4096], F32, kind="ExternalInput").ap()
    weu = nc.dram_tensor("weu", [8, D, 4096], F32, kind="ExternalInput").ap()
    wed = nc.dram_tensor("wed", [8, 4096, D], F32, kind="ExternalInput").ap()
    x1 = nc.dram_tensor("x1", [D, TL], F32, kind="ExternalOutput").ap()
    cw = nc.dram_tensor("cw", [8, TL], F32,
                        kind="ExternalInput" if cw_input else "Internal").ap()

    P.start_block()
    if "r" in parts:
        with ExitStack() as pctx:
            P.pctx = pctx
            emit_router(P, xn, wr, wg2, None, cw, "l0", T=TL, name="rt")
        P.barrier()
    if "p" in parts:
        with ExitStack() as pctx:
            P.pctx = pctx
            emit_proj(P, gf, wo, x1, DI, D, TL, res_hbm=res, name="wo")
        P.barrier()
    if "m" in parts:
        with ExitStack() as pctx:
            P.pctx = pctx
            emit_moe(P, xn, weg, weu, wed, cw, x1, two_mats=True, T=TL, name="moe")
    P.gctx.close()
    return nc


def _np(x):
    return np.asarray(x)


def kernel(**inputs):
    per = prep_inputs(inputs)
    C8 = list(range(NCORES))
    nc = build_launch_ALL()
    res = run_bass_kernel_spmd(nc, per, core_ids=C8).results
    out = np.zeros((BATCH, L, D), np.float32)
    for c in C8:
        b, q = divmod(c, 2)
        out[b, q * TL:(q + 1) * TL, :] = _np(res[c]["x2"]).T
    return out


def kernel_unfused(**inputs):
    per = prep_inputs(inputs)
    C8 = list(range(NCORES))

    def run(nc, names, extra):
        im = [{**{k: per[c][k] for k in names}, **extra[c]} for c in range(NCORES)]
        return run_bass_kernel_spmd(nc, im, core_ids=C8).results

    rA = run(build_launch_A(),
             ["x0", "w0_in_x", "w0_in_z", "conv0_w", "conv0_b", "w0_x"],
             [{} for _ in C8])
    ncsp = build_launch_SPADD(96)
    rSP = run_bass_kernel_spmd(ncsp, [
        {"spa": _np(rA[c]["sp0"]), "spb": _np(rA[c ^ 1]["sp0"])} for c in C8],
        core_ids=C8).results
    rB = run(build_launch_B(0), [],
             [{"u": _np(rA[c]["u0"]), "z": _np(rA[c]["z0"]),
               "sp": _np(rSP[c]["spo"]),
               "wdt": per[c]["wdt0"], "bdt": per[c]["bdt0"],
               "alog": per[c]["alog0"], "dvec": per[c]["dvec0"]} for c in C8])
    extraC = []
    for c in C8:
        b, q = divmod(c, 2)
        gfull = np.concatenate([_np(rB[2 * b]["gout"]), _np(rB[2 * b + 1]["gout"])],
                               axis=0)[:, q * TL:(q + 1) * TL]
        extraC.append({
            "xn": np.ascontiguousarray(_np(rA[c]["xn0"])[:, q * TL:(q + 1) * TL]),
            "gf": np.ascontiguousarray(gfull),
            "wo": per[c]["wo0"],
            "res": np.ascontiguousarray(per[c]["x0"][:, q * TL:(q + 1) * TL]),
            "wr": per[c]["wr0"], "wg2": per[c]["wg0"],
            "weg": per[c]["weg0"], "weu": per[c]["weu0"], "wed": per[c]["wed0"],
        })
    rC = run(build_launch_C(0), [], extraC)
    rD = run(build_launch_D(), ["nmw"],
             [{"x1": _np(rC[c]["x1"])} for c in C8])
    extraD2 = []
    for c in C8:
        b, q = divmod(c, 2)
        xm_full = np.concatenate([_np(rD[2 * b]["xm1"]), _np(rD[2 * b + 1]["xm1"])],
                                 axis=1)
        extraD2.append({"xm1": np.ascontiguousarray(xm_full)})
    rD2 = run(build_launch_D2(),
              ["w1_in_x", "w1_in_z", "conv1_w", "conv1_b", "w1_x"], extraD2)
    ncsp1 = build_launch_SPADD(33)
    rSP1 = run_bass_kernel_spmd(ncsp1, [
        {"spa": _np(rD2[c]["sp1"]), "spb": _np(rD2[c ^ 1]["sp1"])} for c in C8],
        core_ids=C8).results
    rB1 = run(build_launch_B(1), [],
              [{"u": _np(rD2[c]["u1"]), "z": _np(rD2[c]["z1"]),
                "sp": _np(rSP1[c]["spo"]),
                "dtw": per[c]["dtw1"], "bdt": per[c]["bdt1"],
                "alog": per[c]["alog1"], "dvec": per[c]["dvec1"]} for c in C8])
    extraF = []
    for c in C8:
        b, q = divmod(c, 2)
        g1full = np.concatenate([_np(rB1[2 * b]["gout"]), _np(rB1[2 * b + 1]["gout"])],
                                axis=0)[:, q * TL:(q + 1) * TL]
        extraF.append({"g1": np.ascontiguousarray(g1full),
                       "xn1f": _np(rD[c]["xn1f"])})
    rF = run(build_launch_F(), ["wop", "naw"], extraF)
    extraF2 = []
    for c in C8:
        b, q = divmod(c, 2)
        xn2full = np.concatenate([_np(rF[2 * b]["xn2"]), _np(rF[2 * b + 1]["xn2"])],
                                 axis=1)
        extraF2.append({"xn2": np.ascontiguousarray(xn2full)})
    rF2 = run(build_launch_F2(), ["qkvw"], extraF2)
    rG = run(build_launch_G(), [], [{"qkv": _np(rF2[c]["qkv"])} for c in C8])
    extraH = []
    for c in C8:
        b, q = divmod(c, 2)
        ofull = np.concatenate([_np(rG[2 * b]["o"]), _np(rG[2 * b + 1]["o"])],
                               axis=0)[:, q * TL:(q + 1) * TL]
        extraH.append({"of": np.ascontiguousarray(ofull),
                       "m": _np(rF[c]["m"]),
                       "x1": _np(rC[c]["x1"]),
                       "xn1b": _np(rD[c]["xn1b"]),
                       "xn1f": _np(rD[c]["xn1f"])})
    rH = run(build_launch_H(),
             ["wo2", "nfw", "gwp", "uwp", "dwp", "lng", "lnb", "wr1", "br1",
              "we1", "we2", "b1", "b2"], extraH)
    out = np.zeros((BATCH, L, D), np.float32)
    for c in C8:
        b, q = divmod(c, 2)
        out[b, q * TL:(q + 1) * TL, :] = _np(rH[c]["x2"]).T
    return out
